# version 05
# baseline (speedup 1.0000x reference)
import numpy as np
import ml_dtypes
from contextlib import ExitStack
import concourse.bass as bass
import concourse.mybir as mybir
from concourse.bass_utils import run_bass_kernel_spmd

F32 = mybir.dt.float32
BF16 = mybir.dt.bfloat16
I32 = mybir.dt.int32
AX = mybir.AxisListType
ALU = mybir.AluOpType
ACT = mybir.ActivationFunctionType

B, T, D = 2, 16384, 1024
H = 4 * D
TSEG = T // 4
C = 128
G = 4
GT = G * C
NG = TSEG // GT
ND = D // 128
NH = H // 128
NT = TSEG // C
REPLICA_GROUPS = [[0, 1, 2, 3], [4, 5, 6, 7]]

ENG_ATTR = {"pe": "tensor", "dve": "vector", "act": "scalar", "sp": "sync", "gp": "gpsimd"}


class _SerialEng:

    _PASSTHRU = {"wait_ge"}

    def __init__(self, eng, sem, state):
        self._eng = eng
        self._sem = sem
        self._state = state

    def __getattr__(self, name):
        fn = getattr(self._eng, name)
        if name in self._PASSTHRU or not callable(fn):
            return fn

        def wrapped(*a, **kw):
            st = self._state
            if st["issued"] > st["waited"]:
                self._eng.wait_ge(self._sem, st["issued"])
                st["waited"] = st["issued"]
            inst = fn(*a, **kw)
            inst.then_inc(self._sem, 1)
            st["issued"] += 1
            return inst
        return wrapped


class Sched:
    ENGS = ("pe", "dve", "act", "sp", "gp")
    SERIAL = {"dve", "act"}

    def __init__(self, nc, block, es):
        self.nc = nc
        self.block = block
        self.sem = {e: es.enter_context(nc.semaphore("sem_" + e)) for e in self.ENGS}
        self.sem["cc"] = es.enter_context(nc.semaphore("sem_cc"))
        self.cnt = {e: 0 for e in self.ENGS}
        self.cnt["cc"] = 0
        self.waited = {e: {e2: 0 for e2 in (*self.ENGS, "cc")} for e in self.ENGS}
        self.serial_state = {e: {"issued": 0, "waited": 0} for e in self.SERIAL}

    def step(self, eng, deps, body, n_dma=0):
        sem = self.sem
        waited = self.waited[eng]
        serial = eng in self.SERIAL
        use = []
        for d in deps:
            if d is None:
                continue
            de, dv = d
            if de == eng:
                if serial:
                    continue
            if dv > waited[de]:
                if de == eng and not n_dma and not serial:
                    continue
                use.append((de, dv))
                waited[de] = dv
        n_body_inst = [0]
        own_wait = 0
        if n_dma:
            own_wait = self.cnt[eng]
            if own_wait <= waited[eng]:
                own_wait = 0
            else:
                waited[eng] = own_wait
            self.cnt[eng] += 16 * n_dma
        issued = [0]

        def emit(e):
            if own_wait:
                e.wait_ge(sem[eng], own_wait)
            for de, dv in use:
                e.wait_ge(sem[de], dv)
            if n_dma:
                def dma(out, in_, **kw):
                    issued[0] += 1
                    return e.dma_start(out=out, in_=in_, **kw).then_inc(sem[eng], 16)
                body(e, dma)
                assert issued[0] == n_dma, f"declared {n_dma} dmas, issued {issued[0]}"
            elif serial:
                body(_SerialEng(e, sem[eng], self.serial_state[eng]))
            else:
                last = body(e)
                assert last is not None, "compute step must return its last instruction"
                last.then_inc(sem[eng], 1)

        if serial and not n_dma:
            getattr(self.block, ENG_ATTR[eng])(emit)
            self.cnt[eng] = self.serial_state[eng]["issued"]
            return (eng, self.cnt[eng])
        if not n_dma:
            self.cnt[eng] += 1
        getattr(self.block, ENG_ATTR[eng])(emit)
        return (eng, self.cnt[eng])

    def all_done(self):
        return [(e, self.cnt[e]) for e in self.ENGS]


def _finish(S, block):
    sem = S.sem
    cnts = {e: S.cnt[e] for e in S.ENGS}

    def gp_final(e):
        for e2 in S.ENGS:
            if cnts[e2]:
                e.wait_ge(sem[e2], cnts[e2])
    block.gpsimd(gp_final)


def mm_acc(pe, out, pairs):
    n = len(pairs)
    last = None
    for i, (l, r) in enumerate(pairs):
        last = pe.matmul(out, l, r, start=(i == 0), stop=(i == n - 1))
    return last


def hl3(lh, ll, rh, rl):
    return [(lh, rh), (lh, rl), (ll, rh)]


class _StopEmit(Exception):
    pass


def build_program(dbg=False, upto='p3b'):
    p2sub = upto.split('@', 1)[1] if upto.startswith('p2@') else None
    if p2sub:
        upto = 'p2'
    nc = bass.Bass("TRN2", target_bir_lowering=False)

    def dram_in(name, shape, dt=F32):
        return nc.dram_tensor(name, shape, dt, kind="ExternalInput").ap()

    def dram_scratch(name, shape, dt=F32):
        if dbg:
            return nc.dram_tensor(name, shape, dt, kind="ExternalOutput").ap()
        return nc.dram_tensor(name, shape, dt).ap()

    x_d = dram_in("x_seg", [TSEG, D])
    xT_d = dram_in("xT_seg", [D, TSEG])
    WT_d = {w: dram_in(w, [D, D]) for w in ("WqT", "WkT", "WvT", "WoT")}
    bq_d = dram_in("bq_t", [128, ND]); bk_d = dram_in("bk_t", [128, ND])
    bvb_d = dram_in("bv_b", [128, D]); bob_d = dram_in("bo_b", [128, D])
    bkb_d = dram_in("bk_b", [128, D])
    w1T_d = dram_in("w1T", [D, H]); w2T_d = dram_in("w2T", [H, D])
    triu_d = dram_in("triu", [128, 128])
    ident_d = dram_in("ident", [128, 128])
    pmask_d = dram_in("pmask", [128, 4])
    out_d = nc.dram_tensor("out", [TSEG, D], F32, kind="ExternalOutput").ap()

    spill_k = dram_scratch("spill_k", [NG, 2, 128, G * D], BF16)
    spill_v = dram_scratch("spill_v", [NG, 2, 128, G * D], BF16)
    spill_q = dram_scratch("spill_q", [NG, 2, 128, ND * GT], BF16)
    prefix_s = dram_scratch("prefix_s", [NG, 128, ND * D])
    prefix_z = dram_scratch("prefix_z", [NG, 128, ND])
    y1_dram = dram_scratch("y1_spill", [TSEG, D])
    hq_dram = dram_scratch("hq_sp", [NT, 128, H], BF16)
    w1q_dram = dram_scratch("w1q", [128, ND * H], BF16)
    w2q_dram = dram_scratch("w2q", [128, NH * D], BF16)
    sstart_d = dram_scratch("sstart", [128, ND * D])
    cc_in = nc.dram_tensor("cc_in", [D + 1, D], F32).ap()
    cc_out = nc.dram_tensor("cc_out", [4 * (D + 1), D], F32).ap()

    es = ExitStack()
    with es:
        block = es.enter_context(nc.Block())
        S = Sched(nc, block, es)
        sem = S.sem

        def SB(scope, name, shape, dt=F32):
            return scope.enter_context(nc.sbuf_tensor(name, shape, dt))

        def PS(scope, name, shape, dt=F32):
            return scope.enter_context(nc.psum_tensor(name, shape, dt))

        ones_bf = SB(es, "ones_bf", [128, 128], BF16)
        ident_f = SB(es, "ident_f", [128, 128], F32)
        ident_b = SB(es, "ident_b", [128, 128], BF16)
        triu_f = SB(es, "triu_f", [128, 128], F32)
        s1bc = SB(es, "s1bc", [128, 1]); s2bc = SB(es, "s2bc", [128, 1])
        thr1 = SB(es, "thr1", [128, 1]); nthr1 = SB(es, "nthr1", [128, 1])
        thr2 = SB(es, "thr2", [128, 1]); nthr2 = SB(es, "nthr2", [128, 1])
        bq_s = SB(es, "bq_s", [128, ND]); bk_s = SB(es, "bk_s", [128, ND])
        bvb_s = SB(es, "bvb_s", [128, D]); bob_s = SB(es, "bob_s", [128, D])
        bkb_s = SB(es, "bkb_s", [128, D])
        pmask_s = SB(es, "pmask_s", [128, 4])
        z_acc = SB(es, "z_acc", [128, ND]); z_start = SB(es, "z_start", [128, ND])
        zg = SB(es, "zg", [128, ND])
        zg_h = SB(es, "zg_h", [128, ND], BF16); zg_l = SB(es, "zg_l", [128, ND], BF16)
        f2all = SB(es, "f2all", [128, NT])
        eps_c = SB(es, "eps_c", [128, 1])
        s_scope = ExitStack()
        Smaster = SB(s_scope, "Smaster", [128, ND * D])

        def ld_const(e, dma):
            dma(out=ident_f[:], in_=ident_d[:])
            dma(out=triu_f[:], in_=triu_d[:])
            dma(out=bq_s[:], in_=bq_d[:])
            dma(out=bk_s[:], in_=bk_d[:])
            dma(out=bvb_s[:], in_=bvb_d[:])
            dma(out=bob_s[:], in_=bob_d[:])
            dma(out=bkb_s[:], in_=bkb_d[:])
            dma(out=pmask_s[:], in_=pmask_d[:])
        h_const = S.step("sp", [], ld_const, n_dma=8)

        def init_body(v):
            v.memset(ones_bf[:], 1.0)
            v.memset(eps_c[:], 1e-5)
            v.memset(z_acc[:], 0.0)
            v.memset(Smaster[:], 0.0)
            return v.tensor_copy(ident_b[:], ident_f[:])
        h_init = S.step("dve", [h_const], init_body)

        p0 = ExitStack()
        with p0:
            wpiece = SB(p0, "wpiece", [128, H])
            wpiece2 = SB(p0, "wpiece2", [128, H])
            tq2_ = SB(p0, "tq2_", [128, H], BF16)
            wsum = SB(p0, "wsum", [128, 1]); wtmp1 = SB(p0, "wtmp1", [128, 1])
            sh_ = SB(p0, "sh_", [128, 1], BF16); sl_ = SB(p0, "sl_", [128, 1], BF16)
            sc_row = SB(p0, "sc_row", [1, 2])
            sc_h = SB(p0, "sc_h", [1, 2], BF16); sc_l = SB(p0, "sc_l", [1, 2], BF16)
            cge = SB(p0, "cge", [128, H])
            tq_ = SB(p0, "tq_", [128, H], BF16)
            ps_s = PS(p0, "ps_s", [1, 2])
            ps_b = PS(p0, "ps_b", [128, 2])

            prev_wi_spill = None
            for wi, (wT_dd, npieces, pw, wq_dram, sbc, thr, nthr) in enumerate([
                    (w1T_d, ND, H, w1q_dram, s1bc, thr1, nthr1),
                    (w2T_d, NH, D, w2q_dram, s2bc, thr2, nthr2)]):
                hz = S.step("dve", [h_init], lambda v: v.memset(wsum[:], 0.0))
                hrs = [hz, hz]
                for pi in range(npieces):
                    buf = wpiece if pi % 2 == 0 else wpiece2

                    def ldp(e, dma, pi=pi, wT_dd=wT_dd, pw=pw, buf=buf):
                        dma(out=buf[:, :pw], in_=wT_dd[pi * 128:(pi + 1) * 128, :])
                    hl = S.step("sp", [hrs[pi % 2]], ldp, n_dma=1)

                    def red(v, pw=pw, buf=buf):
                        v.reduce_sum(wtmp1[:], buf[:, :pw], axis=AX.X,
                                     apply_absolute_value=True)
                        return v.tensor_add(wsum[:], wsum[:], wtmp1[:])
                    hrs[pi % 2] = S.step("dve", [hl], red)
                hr = hrs[(npieces - 1) % 2]

                def spl(v):
                    v.tensor_copy(sh_[:], wsum[:])
                    return v.tensor_sub(sl_[:], wsum[:], sh_[:])
                hsp = S.step("dve", [hr], spl)
                hmm = S.step("pe", [hsp], lambda p, wi=wi: mm_acc(
                    p, ps_s[0:1, wi:wi + 1],
                    [(sh_[:], ones_bf[:, 0:1]), (sl_[:], ones_bf[:, 0:1])]))
                nelem = float(npieces * 128 * pw)

                def mean(v, wi=wi, nelem=nelem):
                    v.tensor_scalar(sc_row[0:1, wi:wi + 1], ps_s[0:1, wi:wi + 1],
                                    1.0 / nelem, 1e-5, ALU.mult, ALU.max)
                    v.tensor_copy(sc_h[0:1, wi:wi + 1], sc_row[0:1, wi:wi + 1])
                    return v.tensor_sub(sc_l[0:1, wi:wi + 1], sc_row[0:1, wi:wi + 1],
                                        sc_h[0:1, wi:wi + 1])
                hmean = S.step("dve", [hmm], mean)
                hbc = S.step("pe", [hmean], lambda p, wi=wi: mm_acc(
                    p, ps_b[:, wi:wi + 1],
                    [(ones_bf[0:1, :], sc_h[0:1, wi:wi + 1]),
                     (ones_bf[0:1, :], sc_l[0:1, wi:wi + 1])]))

                def mkthr(v, sbc=sbc, thr=thr, nthr=nthr, wi=wi):
                    v.tensor_copy(sbc[:], ps_b[:, wi:wi + 1])
                    v.tensor_scalar(thr[:], sbc[:], 0.5, None, ALU.mult)
                    return v.tensor_scalar(nthr[:], sbc[:], -0.5, None, ALU.mult)
                hthr = S.step("dve", [hbc], mkthr)

                hcs = [hthr, hthr]
                hqs = [prev_wi_spill, prev_wi_spill]
                hq = hthr
                for pi in range(npieces):
                    buf = wpiece if pi % 2 == 0 else wpiece2
                    tqb = tq_ if pi % 2 == 0 else tq2_

                    def ldp2(e, dma, pi=pi, wT_dd=wT_dd, pw=pw, buf=buf):
                        dma(out=buf[:, :pw], in_=wT_dd[pi * 128:(pi + 1) * 128, :])
                    hl = S.step("sp", [hcs[pi % 2], prev_wi_spill if pi < 2 else None],
                                ldp2, n_dma=1)

                    def qz(v, pw=pw, thr=thr, nthr=nthr, buf=buf, tqb=tqb):
                        v.tensor_scalar(cge[:, :pw], buf[:, :pw], nthr[:, 0:1], None,
                                        ALU.is_le)
                        return v.scalar_tensor_tensor(
                            tqb[:, :pw], buf[:, :pw], thr[:, 0:1], cge[:, :pw],
                            ALU.is_ge, ALU.subtract)
                    hcmp = S.step("dve", [hl, hqs[pi % 2]], qz)
                    hcs[pi % 2] = hcmp

                    def stq(e, dma, pi=pi, wq_dram=wq_dram, pw=pw, tqb=tqb):
                        dma(out=wq_dram[:, pi * pw:(pi + 1) * pw], in_=tqb[:, :pw])
                    hq = S.step("gp", [hcmp], stq, n_dma=1)
                    hqs[pi % 2] = hq
                prev_wi_spill = hq
            h_p0_done = S.all_done()
        if upto == 'p0':
            _finish(S, block); s_scope.close(); return nc

        p1 = ExitStack()
        with p1:
            wk_h = SB(p1, "wk_h", [128, ND * D], BF16); wk_l = SB(p1, "wk_l", [128, ND * D], BF16)
            wv_h = SB(p1, "wv_h", [128, ND * D], BF16); wv_l = SB(p1, "wv_l", [128, ND * D], BF16)
            ws_scope = ExitStack()
            wstage = SB(ws_scope, "wstage", [128, ND * D])

            hws = h_p0_done
            for (wd, wh, wl) in [(WT_d["WkT"], wk_h, wk_l), (WT_d["WvT"], wv_h, wv_l)]:
                def ldw(e, dma, wd=wd):
                    dma(out=wstage[:].rearrange("p (dt o) -> p dt o", dt=ND),
                        in_=wd.rearrange("(dt p) o -> p dt o", p=128))
                hlw = S.step("sp", hws if isinstance(hws, list) else list(hws), ldw, n_dma=1)

                def splw_h(a, wh=wh):
                    return a.copy(wh[:], wstage[:])
                hwa = S.step("act", [hlw], splw_h)

                def splw_l(v, wh=wh, wl=wl):
                    return v.tensor_sub(wl[:], wstage[:], wh[:])
                hws = (hwa, S.step("dve", [hwa], splw_l))
            ws_scope.close()
            h_wkv = hws[1]

            xT_st = SB(p1, "xT_st", [128, ND * GT])
            xh = SB(p1, "xh", [128, ND * GT], BF16); xl = SB(p1, "xl", [128, ND * GT], BF16)
            kphi_e = SB(p1, "kphi_e", [128, D]); kphi_r = SB(p1, "kphi_r", [128, D])
            kpT_f = SB(p1, "kpT_f", [128, D])
            kT_h = SB(p1, "kT_h", [128, G * D], BF16); kT_l = SB(p1, "kT_l", [128, G * D], BF16)
            vT_h = SB(p1, "vT_h", [128, G * D], BF16); vT_l = SB(p1, "vT_l", [128, G * D], BF16)
            bkr_h = SB(p1, "bkr_h", [1, D], BF16); bkr_l = SB(p1, "bkr_l", [1, D], BF16)
            bvr_h = SB(p1, "bvr_h", [1, D], BF16); bvr_l = SB(p1, "bvr_l", [1, D], BF16)
            pk = PS(p1, "pk", [128, 2 * D])
            pv = PS(p1, "pv", [128, D])
            pkv = PS(p1, "pkv", [128, 2 * 512])

            def mk_rows_h(a):
                a.copy(bkr_h[:], bkb_s[0:1, :])
                return a.copy(bvr_h[:], bvb_s[0:1, :])
            hra = S.step("act", [h_const], mk_rows_h)

            def mk_rows_l(v):
                v.tensor_sub(bkr_l[:], bkb_s[0:1, :], bkr_h[:])
                return v.tensor_sub(bvr_l[:], bvb_s[0:1, :], bvr_h[:])
            hrl = S.step("dve", [hra], mk_rows_l)

            prev_split = h_wkv
            prev_x_pe = None
            prev_kv_done = h_init
            prev_spk = None; prev_spv = None
            prev_kv_pe = None
            for g in range(NG):
                def ldx(e, dma, g=g):
                    dma(out=xT_st[:].rearrange("p (dt t) -> p dt t", dt=ND),
                        in_=xT_d[:, g * GT:(g + 1) * GT]
                        .rearrange("(dt p) t -> p dt t", p=128))
                hx = S.step("sp", [prev_split], ldx, n_dma=1)

                hxa = S.step("act", [hx, prev_x_pe], lambda a: a.copy(xh[:], xT_st[:]))
                hxs = S.step("dve", [hxa], lambda v: v.tensor_sub(xl[:], xT_st[:], xh[:]))

                def spz(e, dma, g=g):
                    dma(out=prefix_z[g], in_=z_acc[:])
                    dma(out=prefix_s[g], in_=Smaster[:])
                hspill = S.step("gp", [prev_kv_done], spz, n_dma=2)

                hk_stt = None
                hk_hi = None
                hk_lo = None
                hz_mm = None
                for tt in range(G):
                    def mmk(p, tt=tt):
                        slot = tt % 2
                        last = None
                        for eh in range(2):
                            pairs = []
                            for dk in range(ND):
                                lh = xh[:, dk * GT + tt * 128: dk * GT + tt * 128 + 128]
                                ll = xl[:, dk * GT + tt * 128: dk * GT + tt * 128 + 128]
                                rh = wk_h[:, dk * D + eh * 512: dk * D + eh * 512 + 512]
                                rl = wk_l[:, dk * D + eh * 512: dk * D + eh * 512 + 512]
                                pairs += hl3(lh, ll, rh, rl)
                            pairs.append((ones_bf[0:1, :], bkr_h[0:1, eh * 512:(eh + 1) * 512]))
                            pairs.append((ones_bf[0:1, :], bkr_l[0:1, eh * 512:(eh + 1) * 512]))
                            last = mm_acc(p, pk[:, slot * D + eh * 512: slot * D + eh * 512 + 512],
                                          pairs)
                        return last
                    dep = [hxs, h_wkv, hrl]
                    if tt >= 2:
                        dep.append(hk_evac2[tt - 2])
                    hmk = S.step("pe", dep, mmk)

                    def ek_e(a, tt=tt):
                        return a.activation(kphi_e[:], pk[:, (tt % 2) * D:(tt % 2 + 1) * D],
                                            ACT.Exp)
                    dep = [hmk] + ([hk_stt] if hk_stt else [])
                    hea = S.step("act", dep, ek_e)

                    def ek_r(a, tt=tt):
                        return a.activation(kphi_r[:], pk[:, (tt % 2) * D:(tt % 2 + 1) * D],
                                            ACT.Relu)
                    her = S.step("act", [hea], ek_r)

                    def ek_stt(v, tt=tt):
                        return v.scalar_tensor_tensor(kpT_f[:], kphi_e[:], 1.0, kphi_r[:],
                                                      ALU.min, ALU.add)
                    hk_stt = S.step("dve", [her] + ([hk_hi] if hk_hi else []), ek_stt)

                    def ek_hi(a, tt=tt):
                        return a.copy(kT_h[:, tt * D:(tt + 1) * D], kpT_f[:])
                    hk_hi = S.step("act", [hk_stt, prev_spk, prev_kv_pe], ek_hi)

                    def ek_lo(v, tt=tt):
                        return v.tensor_sub(kT_l[:, tt * D:(tt + 1) * D], kpT_f[:],
                                            kT_h[:, tt * D:(tt + 1) * D])
                    hk_lo = S.step("dve", [hk_hi], ek_lo)
                    if tt == 0:
                        hk_evac2 = {}
                    hk_evac2[tt] = hk_lo

                    def zmm(p, tt=tt):
                        last = None
                        for dt in range(ND):
                            for j, pl in enumerate((kT_h, kT_l)):
                                st = (tt == 0 and dt == 0 and j == 0)
                                sp_ = (tt == G - 1 and dt == ND - 1 and j == 1)
                                last = p.matmul(pkv[:, dt:dt + 1],
                                                pl[:, tt * D + dt * 128: tt * D + dt * 128 + 128],
                                                ones_bf[:, 0:1], start=st, stop=sp_)
                        return last
                    dep = [hk_lo, hk_hi]
                    if tt == 0:
                        dep.append(prev_kv_done)
                    hz_mm = S.step("pe", dep, zmm)

                def evz(v):
                    return v.tensor_add(z_acc[:], z_acc[:], pkv[:, 0:ND])
                hz_ev = S.step("dve", [hz_mm, hspill], evz)

                def spk(e, dma, g=g):
                    dma(out=spill_k[g, 0], in_=kT_h[:])
                    dma(out=spill_k[g, 1], in_=kT_l[:])
                hspk = S.step("gp", [hk_lo, hk_hi], spk, n_dma=2)

                hv_hi = None
                hv_lo = None
                for tt in range(G):
                    def mmv(p, tt=tt):
                        last = None
                        for eh in range(2):
                            pairs = []
                            for dk in range(ND):
                                lh = xh[:, dk * GT + tt * 128: dk * GT + tt * 128 + 128]
                                ll = xl[:, dk * GT + tt * 128: dk * GT + tt * 128 + 128]
                                rh = wv_h[:, dk * D + eh * 512: dk * D + eh * 512 + 512]
                                rl = wv_l[:, dk * D + eh * 512: dk * D + eh * 512 + 512]
                                pairs += hl3(lh, ll, rh, rl)
                            pairs.append((ones_bf[0:1, :], bvr_h[0:1, eh * 512:(eh + 1) * 512]))
                            pairs.append((ones_bf[0:1, :], bvr_l[0:1, eh * 512:(eh + 1) * 512]))
                            last = mm_acc(p, pv[:, eh * 512:(eh + 1) * 512], pairs)
                        return last
                    dep = [hxs, h_wkv, hrl]
                    if tt >= 1:
                        dep.append(hv_lo)
                    hmv = S.step("pe", dep, mmv)

                    def ev_hi(a, tt=tt):
                        return a.copy(vT_h[:, tt * D:(tt + 1) * D], pv[:])
                    hv_hi = S.step("act", [hmv, prev_spv, prev_kv_pe], ev_hi)

                    def ev_lo(v, tt=tt):
                        return v.tensor_sub(vT_l[:, tt * D:(tt + 1) * D], pv[:],
                                            vT_h[:, tt * D:(tt + 1) * D])
                    hv_lo = S.step("dve", [hmv, hv_hi], ev_lo)

                def spv(e, dma, g=g):
                    dma(out=spill_v[g, 0], in_=vT_h[:])
                    dma(out=spill_v[g, 1], in_=vT_l[:])
                hspv = S.step("gp", [hv_lo, hv_hi], spv, n_dma=2)

                hkv_ev = []
                hm_last = None
                for dt in range(ND):
                    for eh in range(2):
                        idx = dt * 2 + eh

                        def mmkv(p, dt=dt, eh=eh, idx=idx):
                            pairs = []
                            for tc in range(G):
                                lh = kT_h[:, tc * D + dt * 128: tc * D + dt * 128 + 128]
                                ll = kT_l[:, tc * D + dt * 128: tc * D + dt * 128 + 128]
                                rh = vT_h[:, tc * D + eh * 512: tc * D + eh * 512 + 512]
                                rl = vT_l[:, tc * D + eh * 512: tc * D + eh * 512 + 512]
                                pairs += hl3(lh, ll, rh, rl)
                            slot = idx % 2
                            return mm_acc(p, pkv[:, slot * 512:(slot + 1) * 512], pairs)
                        dep = [hk_lo, hk_hi, hv_lo, hv_hi, hz_ev]
                        if idx >= 2:
                            dep.append(hkv_ev[idx - 2])
                        hm_last = S.step("pe", dep, mmkv)

                        def evkv(v, dt=dt, eh=eh, idx=idx):
                            slot = idx % 2
                            dst = Smaster[:, dt * D + eh * 512: dt * D + eh * 512 + 512]
                            return v.tensor_add(dst, dst, pkv[:, slot * 512:(slot + 1) * 512])
                        hkv_ev.append(S.step("dve", [hm_last, hspill], evkv))

                prev_kv_done = hkv_ev[-1]
                prev_kv_pe = hm_last
                prev_x_pe = hm_last
                prev_split = hxs
                prev_spk = hspk; prev_spv = hspv

            h_p1a_done = S.all_done()

        if upto == 'p1a':
            _finish(S, block); s_scope.close(); return nc

        def wr_cc(e, dma):
            for dt in range(ND):
                dma(out=cc_in[dt * 128:(dt + 1) * 128, :],
                    in_=Smaster[:, dt * D:(dt + 1) * D])
            for dt in range(ND):
                dma(out=cc_in[D:D + 1, dt * 128:(dt + 1) * 128],
                    in_=z_acc[:, dt:dt + 1])
        h_wrcc = S.step("gp", h_p1a_done, wr_cc, n_dma=16)

        _wr_val = h_wrcc[1]

        def _cc_emit(e):
            e.wait_ge(S.sem["gp"], _wr_val)
            e.collective_compute(
                "AllGather", ALU.bypass, replica_groups=REPLICA_GROUPS,
                ins=[cc_in[:]], outs=[cc_out[:]]).then_inc(S.sem["cc"], 1)
        block.gpsimd(_cc_emit)
        S.cnt["cc"] = 1
        h_cc = ("cc", 1)

        p1b = ExitStack()
        with p1b:
            wq_h = SB(p1b, "wq_h", [128, ND * D], BF16); wq_l = SB(p1b, "wq_l", [128, ND * D], BF16)
            ws2_scope = ExitStack()
            wstage2 = SB(ws2_scope, "wstage2", [128, ND * D])

            def ldwq(e, dma):
                dma(out=wstage2[:].rearrange("p (dt o) -> p dt o", dt=ND),
                    in_=WT_d["WqT"].rearrange("(dt p) o -> p dt o", p=128))
            hlq = S.step("sp", h_p1a_done, ldwq, n_dma=1)

            def splwq(v):
                v.tensor_copy(wq_h[:], wstage2[:])
                return v.tensor_sub(wq_l[:], wstage2[:], wq_h[:])
            hwq = S.step("dve", [hlq], splwq)
            ws2_scope.close()

            xT_st2 = SB(p1b, "xT_st2", [128, ND * GT])
            xh2 = SB(p1b, "xh2", [128, ND * GT], BF16); xl2 = SB(p1b, "xl2", [128, ND * GT], BF16)
            qD_f = SB(p1b, "qD_f", [128, ND * GT])
            q_h = SB(p1b, "q_h", [128, ND * GT], BF16); q_l = SB(p1b, "q_l", [128, ND * GT], BF16)
            ephi2 = SB(p1b, "ephi2", [128, GT]); rphi2 = SB(p1b, "rphi2", [128, GT])
            pq = PS(p1b, "pq", [128, 2 * GT])

            prev_split = hwq
            prev_x_pe = None
            prev_spq = None
            for g in range(NG):
                def ldx2(e, dma, g=g):
                    dma(out=xT_st2[:].rearrange("p (dt t) -> p dt t", dt=ND),
                        in_=xT_d[:, g * GT:(g + 1) * GT]
                        .rearrange("(dt p) t -> p dt t", p=128))
                hx = S.step("sp", [prev_split], ldx2, n_dma=1)

                def splx2(v):
                    v.tensor_copy(xh2[:], xT_st2[:])
                    return v.tensor_sub(xl2[:], xT_st2[:], xh2[:])
                hxs = S.step("dve", [hx, hwq, prev_x_pe], splx2)

                hq_evac = []
                hmq = None
                for ot in range(ND):
                    def mmq(p, ot=ot):
                        pairs = []
                        for dk in range(ND):
                            lh = wq_h[:, dk * D + ot * 128: dk * D + ot * 128 + 128]
                            ll = wq_l[:, dk * D + ot * 128: dk * D + ot * 128 + 128]
                            rh = xh2[:, dk * GT:(dk + 1) * GT]
                            rl = xl2[:, dk * GT:(dk + 1) * GT]
                            pairs += hl3(lh, ll, rh, rl)
                        return mm_acc(p, pq[:, (ot % 2) * GT:(ot % 2 + 1) * GT], pairs)
                    dep = [hxs] + ([hq_evac[ot - 2][1]] if ot >= 2 else [])
                    hmq = S.step("pe", dep, mmq)

                    def eq_act(a, ot=ot):
                        a.activation(ephi2[:], pq[:, (ot % 2) * GT:(ot % 2 + 1) * GT],
                                     ACT.Exp, bias=bq_s[:, ot:ot + 1])
                        return a.activation(rphi2[:], pq[:, (ot % 2) * GT:(ot % 2 + 1) * GT],
                                            ACT.Relu, bias=bq_s[:, ot:ot + 1])
                    heq = S.step("act", [hmq] + ([hq_evac[ot - 1][1]] if ot >= 1 else []),
                                 eq_act)

                    def eq_stt(v, ot=ot):
                        return v.scalar_tensor_tensor(qD_f[:, ot * GT:(ot + 1) * GT],
                                                      ephi2[:], 1.0, rphi2[:],
                                                      ALU.min, ALU.add)
                    hst = S.step("dve", [heq, prev_spq, prev_x_pe], eq_stt)

                    def eq_hi(a, ot=ot):
                        return a.copy(q_h[:, ot * GT:(ot + 1) * GT],
                                      qD_f[:, ot * GT:(ot + 1) * GT])
                    hhi = S.step("act", [hst], eq_hi)

                    def eq_lo(v, ot=ot):
                        return v.tensor_sub(q_l[:, ot * GT:(ot + 1) * GT],
                                            qD_f[:, ot * GT:(ot + 1) * GT],
                                            q_h[:, ot * GT:(ot + 1) * GT])
                    hev = S.step("dve", [hhi], eq_lo)
                    hq_evac.append((hmq, hev))

                def spq(e, dma, g=g):
                    dma(out=spill_q[g, 0], in_=q_h[:])
                    dma(out=spill_q[g, 1], in_=q_l[:])
                prev_spq = S.step("gp", [hq_evac[-1][1]], spq, n_dma=2)
                prev_split = hxs
                prev_x_pe = hmq

            h_p1b_done = S.all_done()

        if upto == 'p1b':
            _finish(S, block); s_scope.close(); return nc

        px = ExitStack()
        with px:
            Ttmp = SB(px, "Ttmp", [128, ND * D])
            ztmp8 = SB(px, "ztmp8", [128, ND])
            hacc = None
            for r in range(4):
                def ldT(e, dma, r=r):
                    base = r * (D + 1)
                    dma(out=Ttmp[:].rearrange("p (dt e) -> p dt e", dt=ND),
                        in_=cc_out[base:base + D, :].rearrange("(dt p) e -> p dt e", p=128))
                    dma(out=ztmp8[:],
                        in_=cc_out[base + D:base + D + 1, :]
                        .rearrange("r (dt p) -> p (r dt)", p=128, r=1),
                        allow_slow_non_contiguous=True)
                dep = [h_cc] + ([hacc] if hacc else []) + (h_p1b_done if r == 0 else [])
                hl = S.step("sp", dep, ldT, n_dma=2)

                def accT(v, r=r):
                    if r == 0:
                        v.tensor_scalar(Smaster[:], Ttmp[:], pmask_s[:, 0:1], None, ALU.mult)
                        return v.tensor_scalar(z_start[:], ztmp8[:], pmask_s[:, 0:1],
                                               None, ALU.mult)
                    v.scalar_tensor_tensor(Smaster[:], Ttmp[:], pmask_s[:, r:r + 1],
                                           Smaster[:], ALU.mult, ALU.add)
                    return v.scalar_tensor_tensor(z_start[:], ztmp8[:], pmask_s[:, r:r + 1],
                                                  z_start[:], ALU.mult, ALU.add)
                hacc = S.step("dve", [hl] + h_p1b_done, accT)

            def spss(e, dma):
                dma(out=sstart_d[:], in_=Smaster[:])
            h_spss = S.step("gp", [hacc], spss, n_dma=1)
            h_x2_done = S.all_done()
        s_scope.close()
        if upto == 'x2':
            _finish(S, block); return nc

        p2 = ExitStack()
        with p2:
            wo_h = SB(p2, "wo_h", [128, ND * D], BF16); wo_l = SB(p2, "wo_l", [128, ND * D], BF16)
            ws3_scope = ExitStack()
            wstage3 = SB(ws3_scope, "wstage3", [128, ND * D])

            def ldwo(e, dma):
                dma(out=wstage3[:].rearrange("p (dt o) -> p dt o", dt=ND),
                    in_=WT_d["WoT"].rearrange("(dt p) o -> p dt o", p=128))
            hlo = S.step("sp", h_x2_done, ldwo, n_dma=1)

            def splwo(v):
                v.tensor_copy(wo_h[:], wstage3[:])
                return v.tensor_sub(wo_l[:], wstage3[:], wo_h[:])
            hwo = S.step("dve", [hlo], splwo)
            ws3_scope.close()
            if p2sub == 'w':
                _finish(S, block)
                return nc

            Sgrp = SB(p2, "Sgrp", [128, ND * D])
            s_h = SB(p2, "s_h", [128, ND * D], BF16); s_l = SB(p2, "s_l", [128, ND * D], BF16)
            q2_h = SB(p2, "q2_h", [128, ND * GT], BF16); q2_l = SB(p2, "q2_l", [128, ND * GT], BF16)
            k2_h = SB(p2, "k2_h", [128, G * D], BF16); k2_l = SB(p2, "k2_l", [128, G * D], BF16)
            v2_h = SB(p2, "v2_h", [128, G * D], BF16); v2_l = SB(p2, "v2_l", [128, G * D], BF16)
            kd_h = SB(p2, "kd_h", [128, ND * 128], BF16); kd_l = SB(p2, "kd_l", [128, ND * 128], BF16)
            aT_h = SB(p2, "aT_h", [128, G * GT], BF16); aT_l = SB(p2, "aT_l", [128, G * GT], BF16)
            n_h = SB(p2, "n_h", [128, ND * GT], BF16); n_l = SB(p2, "n_l", [128, ND * GT], BF16)
            atmp = SB(p2, "atmp", [128, 128])
            dcol = SB(p2, "dcol", [128, G])
            recip_den = SB(p2, "recip_den", [128, G])
            x_res = SB(p2, "x_res", [128, D])
            y_t = SB(p2, "y_t", [128, D])
            y1_t = SB(p2, "y1_t", [128, D])
            zgtmp = SB(p2, "zgtmp", [128, ND])
            bnb = SB(p2, "bnb", [128, 12]); bna = SB(p2, "bna", [128, 2])
            sd_c = SB(p2, "sd_c", [128, 1]); rstd_c = SB(p2, "rstd_c", [128, 1])
            negm_c = SB(p2, "negm_c", [128, 1])
            pA = PS(p2, "pA", [128, 512])
            pden = PS(p2, "pden", [128, G])
            pnum2 = PS(p2, "pnum2", [128, 2 * 512])
            pwo = PS(p2, "pwo", [128, D])
            ptrbA = PS(p2, "ptrbA", [128, 512], BF16)
            ptrbB = PS(p2, "ptrbB", [128, 512], BF16)

            h_prev_group_end = None
            prev_hey = None
            prev_hsd = None
            prev_sty1 = None
            prev_hxr_reader = None
            prev_trev = None
            prev_mn_pe = None
            h_ldsz = None
            h_accS = None
            for g in range(NG if not p2sub else 1):
              try:
                  if h_ldsz is None:
                      def ldsz0(e, dma, g=g):
                          dma(out=Sgrp[:], in_=prefix_s[g])
                          dma(out=zgtmp[:], in_=prefix_z[g])
                      h_ldsz = S.step("sp", h_x2_done + [hwo], ldsz0, n_dma=2)

                      def accS0(e, dma):
                          for i in range(4):
                              dma(out=Sgrp[:, i * 2048:(i + 1) * 2048],
                                  in_=sstart_d[:, i * 2048:(i + 1) * 2048],
                                  accum_op=ALU.add)
                      h_accS = S.step("gp", [h_ldsz], accS0, n_dma=4)

                  def ldkqv(e, dma, g=g):
                      dma(out=q2_h[:], in_=spill_q[g, 0]); dma(out=q2_l[:], in_=spill_q[g, 1])
                      dma(out=k2_h[:], in_=spill_k[g, 0]); dma(out=k2_l[:], in_=spill_k[g, 1])
                      dma(out=v2_h[:], in_=spill_v[g, 0]); dma(out=v2_l[:], in_=spill_v[g, 1])
                  hld = S.step("sp", [prev_mn_pe], ldkqv, n_dma=6)
                  if p2sub == 'ld0':
                      raise _StopEmit

                  hz0 = S.step("dve", [h_ldsz, prev_mn_pe], lambda v: v.tensor_add(
                      zg[:], zgtmp[:], z_start[:]))

                  def mks_hi(a):
                      a.copy(s_h[:], Sgrp[:])
                      return a.copy(zg_h[:], zg[:])
                  hSa = S.step("act", [h_ldsz, h_accS, hz0, prev_mn_pe], mks_hi)

                  def mks_lo(v):
                      v.tensor_sub(s_l[:], Sgrp[:], s_h[:])
                      return v.tensor_sub(zg_l[:], zg[:], zg_h[:])
                  hS = S.step("dve", [hSa], mks_lo)
                  if p2sub == 'ld':
                      raise _StopEmit

                  if g + 1 < NG:
                      def ldsz(e, dma, g=g):
                          dma(out=Sgrp[:], in_=prefix_s[g + 1])
                          dma(out=zgtmp[:], in_=prefix_z[g + 1])
                      h_ldsz = S.step("sp", [hS], ldsz, n_dma=2)

                      def accSn(e, dma):
                          for i in range(4):
                              dma(out=Sgrp[:, i * 2048:(i + 1) * 2048],
                                  in_=sstart_d[:, i * 2048:(i + 1) * 2048],
                                  accum_op=ALU.add)
                      h_accS = S.step("gp", [h_ldsz], accSn, n_dma=4)

                  hA_ev = []
                  for st_ in range(G):
                      htr_ev = None
                      for (ksrc, kdst) in ((k2_h, kd_h), (k2_l, kd_l)):
                          for half in range(2):
                              ptrb = ptrbA if half == 0 else ptrbB

                              def trk2(p, st_=st_, half=half, ksrc=ksrc, ptrb=ptrb):
                                  last = None
                                  for j in range(4):
                                      dk = half * 4 + j
                                      last = p.transpose(
                                          ptrb[:, j * 128:(j + 1) * 128],
                                          ksrc[:, st_ * D + dk * 128: st_ * D + dk * 128 + 128],
                                          ident_b[:])
                                  return last
                              dep = [hld, prev_trev]
                              if hA_ev:
                                  dep.append(hA_ev[-1])
                              ht = S.step("pe", dep, trk2)

                              def evt(a, half=half, kdst=kdst, ptrb=ptrb):
                                  return a.copy(kdst[:, half * 512:(half + 1) * 512],
                                                ptrb[:])
                              prev_trev = S.step("act", [ht], evt)
                          htr_ev = prev_trev

                      ncols = GT - st_ * 128

                      def mma(p, st_=st_, ncols=ncols):
                          pairs = []
                          for dk in range(ND):
                              lh = kd_h[:, dk * 128:(dk + 1) * 128]
                              ll = kd_l[:, dk * 128:(dk + 1) * 128]
                              rh = q2_h[:, dk * GT + st_ * 128: dk * GT + GT]
                              rl = q2_l[:, dk * GT + st_ * 128: dk * GT + GT]
                              pairs += hl3(lh, ll, rh, rl)
                          return mm_acc(p, pA[:, 0:ncols], pairs)
                      dep = [htr_ev, hS]
                      if hA_ev:
                          dep.append(hA_ev[-1])
                      hma = S.step("pe", dep, mma)

                      def eva(v, st_=st_, ncols=ncols):
                          base = st_ * GT + st_ * 128
                          v.tensor_mul(atmp[:], pA[:, 0:128], triu_f[:])
                          v.tensor_copy(aT_h[:, base:base + 128], atmp[:])
                          v.scalar_tensor_tensor(aT_l[:, base:base + 128], atmp[:],
                                                 1.0, aT_h[:, base:base + 128],
                                                 ALU.mult, ALU.subtract)
                          if ncols > 128:
                              v.tensor_copy(aT_h[:, base + 128: st_ * GT + GT],
                                            pA[:, 128:ncols])
                              return v.scalar_tensor_tensor(
                                  aT_l[:, base + 128: st_ * GT + GT], pA[:, 128:ncols],
                                  1.0, aT_h[:, base + 128: st_ * GT + GT],
                                  ALU.mult, ALU.subtract)
                          return v.engine_nop()
                      hA_ev.append(S.step("dve", [hma], eva))
                  if p2sub == 'A':
                      raise _StopEmit

                  def mden(p):
                      last = None
                      first = True
                      for cq in range(G):
                          out = pden[:, cq:cq + 1]
                          for st_ in range(cq + 1):
                              base = st_ * GT + cq * 128
                              for pl in (aT_h, aT_l):
                                  last = p.matmul(out, pl[:, base:base + 128],
                                                  ones_bf[:, 0:1],
                                                  start=first, stop=False)
                                  first = False
                          for dk in range(ND):
                              qh_sl = q2_h[:, dk * GT + cq * 128: dk * GT + cq * 128 + 128]
                              ql_sl = q2_l[:, dk * GT + cq * 128: dk * GT + cq * 128 + 128]
                              for (qp, zp) in ((qh_sl, zg_h), (ql_sl, zg_h), (qh_sl, zg_l)):
                                  isl = (cq == G - 1 and dk == ND - 1 and
                                         qp is qh_sl and zp is zg_l)
                                  last = p.matmul(out, qp, zp[:, dk:dk + 1],
                                                  start=False, stop=isl)
                      return last
                  hmden = S.step("pe", hA_ev + [hS], mden)

                  def evden(v):
                      v.tensor_scalar(dcol[:], pden[:], 1e-6, None, ALU.add)
                      return v.reciprocal(recip_den[:], dcol[:])
                  hevden = S.step("dve", [hmden], evden)
                  if p2sub == 'den':
                      raise _StopEmit

                  h_ev_lo = {}
                  h_ev_hi = {}
                  hmn = None
                  for et in range(ND):
                      def mnum(p, et=et):
                          slot = et % 2
                          pairs = []
                          for dk in range(ND):
                              sh_sl = s_h[:, dk * D + et * 128: dk * D + et * 128 + 128]
                              sl_sl = s_l[:, dk * D + et * 128: dk * D + et * 128 + 128]
                              qh_sl = q2_h[:, dk * GT:(dk + 1) * GT]
                              ql_sl = q2_l[:, dk * GT:(dk + 1) * GT]
                              pairs += hl3(sh_sl, sl_sl, qh_sl, ql_sl)
                          n = len(pairs)
                          last = None
                          for i, (l, r) in enumerate(pairs):
                              last = p.matmul(pnum2[:, slot * 512:(slot + 1) * 512],
                                              l, r, start=(i == 0), stop=False)
                          for st_ in range(G):
                              vh_sl = v2_h[:, st_ * D + et * 128: st_ * D + et * 128 + 128]
                              vl_sl = v2_l[:, st_ * D + et * 128: st_ * D + et * 128 + 128]
                              ah_sl = aT_h[:, st_ * GT + st_ * 128: st_ * GT + GT]
                              al_sl = aT_l[:, st_ * GT + st_ * 128: st_ * GT + GT]
                              ipairs = [(vh_sl, ah_sl), (vh_sl, al_sl), (vl_sl, ah_sl)]
                              for j, (l, r) in enumerate(ipairs):
                                  isl = (st_ == G - 1 and j == 2)
                                  last = p.matmul(
                                      pnum2[:, slot * 512 + st_ * 128:(slot + 1) * 512],
                                      l, r, start=False, stop=isl)
                          return last
                      dep = [hS, hA_ev[-1]]
                      if et >= 2:
                          dep += [h_ev_lo[et - 2], h_ev_hi[et - 2]]
                      hmn = S.step("pe", dep, mnum)

                      def ev_hi(a, et=et):
                          return a.copy(n_h[:, et * GT:(et + 1) * GT],
                                        pnum2[:, (et % 2) * 512:(et % 2 + 1) * 512])
                      h_ev_hi[et] = S.step("act", [hmn], ev_hi)

                      def ev_lo(v, et=et):
                          return v.scalar_tensor_tensor(
                              n_l[:, et * GT:(et + 1) * GT],
                              pnum2[:, (et % 2) * 512:(et % 2 + 1) * 512],
                              1.0, n_h[:, et * GT:(et + 1) * GT],
                              ALU.mult, ALU.subtract)
                      h_ev_lo[et] = S.step("dve", [hmn, h_ev_hi[et]], ev_lo)
                  prev_mn_pe = hmn
                  if p2sub == 'num':
                      raise _StopEmit

                  for cq in range(G):
                      def mwo(p, cq=cq):
                          last = None
                          for eh in range(2):
                              pairs = []
                              for ek in range(ND):
                                  lh = n_h[:, ek * GT + cq * 128: ek * GT + cq * 128 + 128]
                                  ll = n_l[:, ek * GT + cq * 128: ek * GT + cq * 128 + 128]
                                  rh = wo_h[:, ek * D + eh * 512: ek * D + eh * 512 + 512]
                                  rl = wo_l[:, ek * D + eh * 512: ek * D + eh * 512 + 512]
                                  pairs += hl3(lh, ll, rh, rl)
                              last = mm_acc(p, pwo[:, eh * 512:(eh + 1) * 512], pairs)
                          return last
                      hmwo = S.step("pe", [h_ev_lo[ND - 1], h_ev_hi[ND - 1], prev_hey], mwo)
                      if p2sub == 'wo':
                          raise _StopEmit

                      def ldxr(e, dma, g=g, cq=cq):
                          t0 = (g * G + cq) * 128
                          dma(out=x_res[:], in_=x_d[t0:t0 + 128, :])
                      hxr = S.step("sp", [prev_hxr_reader], ldxr, n_dma=1)

                      def evy(v, cq=cq):
                          v.scalar_tensor_tensor(y_t[:], pwo[:], recip_den[:, cq:cq + 1],
                                                 bob_s[:], ALU.mult, ALU.add)
                          v.tensor_add(y_t[:], y_t[:], x_res[:])
                          v.bn_stats(bnb[:, 0:6], y_t[:, 0:512])
                          v.bn_stats(bnb[:, 6:12], y_t[:, 512:1024])
                          v.bn_aggr(bna[:, 0:2], bnb[:, 0:12])
                          return v.tensor_scalar(negm_c[:], bna[:, 0:1], -1.0, None, ALU.mult)
                      hey = S.step("dve", [hmwo, hxr, hevden, prev_hsd], evy)
                      prev_hey = hey
                      prev_hxr_reader = hey

                      hsd = S.step("act", [hey], lambda a: a.activation(
                          sd_c[:], bna[:, 1:2], ACT.Sqrt, bias=eps_c[:, 0:1]))
                      prev_hsd = hsd

                      def evy2(v):
                          v.reciprocal(rstd_c[:], sd_c[:])
                          return v.tensor_scalar(y1_t[:], y_t[:], negm_c[:, 0:1],
                                                 rstd_c[:, 0:1], ALU.add, ALU.mult)
                      hy1 = S.step("dve", [hsd, prev_sty1], evy2)

                      def sty1(e, dma, g=g, cq=cq):
                          t0 = (g * G + cq) * 128
                          dma(out=y1_dram[t0:t0 + 128, :], in_=y1_t[:])
                      prev_sty1 = S.step("gp", [hy1], sty1, n_dma=1)

                  h_prev_group_end = hy1
              except _StopEmit:
                pass

            h_p2_done = S.all_done()
            if p2sub:
                pass

        if upto == 'p2':
            _finish(S, block); return nc

        p3 = ExitStack()
        with p3:
            w1q_s = SB(p3, "w1q_s", [128, ND * H], BF16)
            w2q_s = SB(p3, "w2q_s", [128, NH * D], BF16)
            y1_t0 = SB(p3, "y1_t0", [128, D]); y1_t1 = SB(p3, "y1_t1", [128, D])
            m1c = SB(p3, "m1c", [128, 1]); r1c = SB(p3, "r1c", [128, 1])
            f1c = SB(p3, "f1c", [128, 1])
            m2c = SB(p3, "m2c", [128, 1]); r2c = SB(p3, "r2c", [128, 1])
            f2c = SB(p3, "f2c", [128, 1])
            mtmp = SB(p3, "mtmp", [128, 1])
            xi_ = SB(p3, "xi_", [128, D], I32)
            xb_ = SB(p3, "xb_", [128, D], BF16)
            yqD = SB(p3, "yqD", [128, ND * 128], BF16)
            h_f = SB(p3, "h_f", [128, H])
            hi_pc = SB(p3, "hi_pc", [128, 512], I32)
            hb_pc = SB(p3, "hb_pc", [128, 512], BF16)
            hqD = SB(p3, "hqD", [128, NH * 128], BF16)
            y2_t = SB(p3, "y2_t", [128, D])
            out_t = SB(p3, "out_t", [128, D])
            bnb2 = SB(p3, "bnb2", [128, 12]); bna2 = SB(p3, "bna2", [128, 2])
            sd2 = SB(p3, "sd2", [128, 1]); rstd2 = SB(p3, "rstd2", [128, 1])
            negm2 = SB(p3, "negm2", [128, 1])
            ph_ = PS(p3, "ph_", [128, 2 * 512])
            po_ = PS(p3, "po_", [128, D])
            ptrA3 = PS(p3, "ptrA3", [128, 512], BF16)
            ptrB3 = PS(p3, "ptrB3", [128, 512], BF16)

            def ldw12(e, dma):
                dma(out=w1q_s[:], in_=w1q_dram[:])
                dma(out=w2q_s[:], in_=w2q_dram[:])
            hw12 = S.step("sp", h_p2_done, ldw12, n_dma=2)

            prev_q1 = None
            prev_trq_pe = None
            prev_xb_act = None
            prev_gel = None
            prev_hq2 = None
            prev_mm1 = None
            prev_mm2 = None
            prev_hs2 = None
            prev_stout = None
            prev_trev3 = None
            prev_he2 = {0: None, 1: None}
            h_last_out = None
            hly = {}
            for ti in range(NT):
                y1b = y1_t0 if ti % 2 == 0 else y1_t1
                if ti == 0:
                    def ldy0(e, dma):
                        dma(out=y1_t0[:], in_=y1_dram[0:128, :])
                    hly[0] = S.step("sp", [hw12], ldy0, n_dma=1)

                if ti + 1 < NT:
                    def ldyn(e, dma, ti=ti):
                        nb = y1_t1 if (ti + 1) % 2 == 1 else y1_t0
                        dma(out=nb[:], in_=y1_dram[(ti + 1) * 128:(ti + 2) * 128, :])
                    hly[ti + 1] = S.step("sp", [hly[ti], prev_he2[(ti + 1) % 2]],
                                         ldyn, n_dma=1)

                def q1(v, ti=ti, y1b=y1b):
                    v.reduce_max(m1c[:], y1b[:], axis=AX.X, apply_absolute_value=True)
                    v.tensor_scalar(m1c[:], m1c[:], 1e-5, None, ALU.max)
                    v.reciprocal(r1c[:], m1c[:])
                    v.tensor_scalar(f1c[:], m1c[:], s1bc[:, 0:1], 1.0 / 127.0,
                                    ALU.mult, ALU.mult)
                    return v.tensor_scalar(xi_[:], y1b[:], r1c[:, 0:1], 127.0,
                                           ALU.mult, ALU.mult)
                hq1 = S.step("dve", [hly[ti], prev_trq_pe, prev_gel], q1)
                prev_q1 = hq1

                hxb = S.step("act", [hq1, prev_trq_pe], lambda a: a.copy(xb_[:], xi_[:]))

                htr_ev = None
                htr_pe = None
                for half in range(2):
                    ptrb = ptrA3 if half == 0 else ptrB3

                    def trq(p, half=half, ptrb=ptrb):
                        last = None
                        for j in range(4):
                            dk = half * 4 + j
                            last = p.transpose(ptrb[:, j * 128:(j + 1) * 128],
                                               xb_[:, dk * 128:(dk + 1) * 128], ident_b[:])
                        return last
                    dep = [hxb, prev_trev3] + ([prev_mm1] if half == 0 else [])
                    htr_pe = S.step("pe", dep, trq)

                    def evq(a, half=half, ptrb=ptrb):
                        return a.copy(yqD[:, half * 512:(half + 1) * 512], ptrb[:])
                    htr_ev = S.step("act", [htr_pe], evq)
                    prev_trev3 = htr_ev
                prev_trq_pe = htr_pe

                hml = {}
                hgl = {}
                hmx = None
                for hh in range(8):
                    def mm1(p, hh=hh):
                        pairs = [(yqD[:, dk * 128:(dk + 1) * 128],
                                  w1q_s[:, dk * H + hh * 512: dk * H + hh * 512 + 512])
                                 for dk in range(ND)]
                        return mm_acc(p, ph_[:, (hh % 2) * 512:(hh % 2 + 1) * 512], pairs)
                    dep = [htr_ev, hw12] + ([hgl[hh - 2]] if hh >= 2 else [])
                    hm = S.step("pe", dep, mm1)
                    hml[hh] = hm

                    def gel(a, hh=hh):
                        return a.activation(h_f[:, hh * 512:(hh + 1) * 512],
                                            ph_[:, (hh % 2) * 512:(hh % 2 + 1) * 512],
                                            ACT.Gelu, scale=f1c[:, 0:1])
                    gdep = [hm] + ([prev_hq2] if (hh == 0 and prev_hq2) else [])
                    hg = S.step("act", gdep, gel)
                    hgl[hh] = hg

                    def mx(v, hh=hh):
                        if hh == 0:
                            return v.reduce_max(m2c[:], h_f[:, 0:512], axis=AX.X,
                                                apply_absolute_value=True)
                        v.reduce_max(mtmp[:], h_f[:, hh * 512:(hh + 1) * 512], axis=AX.X,
                                     apply_absolute_value=True)
                        return v.tensor_max(m2c[:], m2c[:], mtmp[:])
                    hmx = S.step("dve", [hg], mx)
                prev_mm1 = hml[7]
                prev_gel = hgl[7]

                def q2s(v, ti=ti):
                    v.tensor_scalar(m2c[:], m2c[:], 1e-5, None, ALU.max)
                    v.reciprocal(r2c[:], m2c[:])
                    return v.tensor_scalar(f2c[:], m2c[:], s2bc[:, 0:1], 1.0 / 127.0,
                                           ALU.mult, ALU.mult)
                hq2a = S.step("dve", [hmx], q2s)

                hq2 = hq2a
                hpc_act = None
                for pc in range(8):
                    def qpc(v, pc=pc):
                        return v.tensor_scalar(hi_pc[:], h_f[:, pc * 512:(pc + 1) * 512],
                                               r2c[:, 0:1], 127.0, ALU.mult, ALU.mult)
                    hqp = S.step("dve", [hq2a] + ([hpc_act] if hpc_act else []), qpc)

                    def cpc(a, pc=pc):
                        return a.copy(hb_pc[:], hi_pc[:])
                    hpc_act = S.step("act", [hqp, prev_trq_pe if pc == 0 else None,
                                             prev_trev3], cpc)

                    ptrb = ptrA3 if pc % 2 == 0 else ptrB3

                    def trh(p, pc=pc, ptrb=ptrb):
                        last = None
                        for j in range(4):
                            last = p.transpose(ptrb[:, j * 128:(j + 1) * 128],
                                               hb_pc[:, j * 128:(j + 1) * 128], ident_b[:])
                        return last
                    dep = [hpc_act, prev_trev3] + ([prev_mm2] if pc == 0 else [])
                    htp = S.step("pe", dep, trh)

                    def evh(a, pc=pc, ptrb=ptrb):
                        return a.copy(hqD[:, pc * 512:(pc + 1) * 512], ptrb[:])
                    prev_trev3 = S.step("act", [htp], evh)
                    hq2 = prev_trev3
                prev_hq2 = hq2

                def mm2(p):
                    last = None
                    for eh in range(2):
                        pairs = [(hqD[:, hk * 128:(hk + 1) * 128],
                                  w2q_s[:, hk * D + eh * 512: hk * D + eh * 512 + 512])
                                 for hk in range(NH)]
                        last = mm_acc(p, po_[:, eh * 512:(eh + 1) * 512], pairs)
                    return last
                hm2 = S.step("pe", [hq2, hw12, prev_hs2], mm2)
                prev_mm2 = hm2

                def evy2b(v, ti=ti, y1b=y1b):
                    v.scalar_tensor_tensor(y2_t[:], po_[:], f2c[:, 0:1],
                                           y1b[:], ALU.mult, ALU.add)
                    v.bn_stats(bnb2[:, 0:6], y2_t[:, 0:512])
                    v.bn_stats(bnb2[:, 6:12], y2_t[:, 512:1024])
                    v.bn_aggr(bna2[:, 0:2], bnb2[:, 0:12])
                    return v.tensor_scalar(negm2[:], bna2[:, 0:1], -1.0, None, ALU.mult)
                he2 = S.step("dve", [hm2, prev_hs2], evy2b)
                prev_he2[ti % 2] = he2

                hs2 = S.step("act", [he2], lambda a: a.activation(
                    sd2[:], bna2[:, 1:2], ACT.Sqrt, bias=eps_c[:, 0:1]))
                prev_hs2 = hs2

                def fin(v):
                    v.reciprocal(rstd2[:], sd2[:])
                    return v.tensor_scalar(out_t[:], y2_t[:], negm2[:, 0:1],
                                           rstd2[:, 0:1], ALU.add, ALU.mult)
                hf = S.step("dve", [hs2, prev_stout], fin)

                def stout(e, dma, ti=ti):
                    dma(out=out_d[ti * 128:(ti + 1) * 128, :], in_=out_t[:])
                h_last_out = S.step("gp", [hf], stout, n_dma=1)
                prev_stout = h_last_out

            tgt = S.cnt["gp"]

            def gp_final(e):
                e.wait_ge(sem["gp"], tgt)
            block.gpsimd(gp_final)

    return nc


_CACHE = {}


def _get_nc(dbg=False):
    key = bool(dbg)
    if key not in _CACHE:
        _CACHE[key] = build_program(dbg=dbg)
    return _CACHE[key]


def _prep_inputs(inputs):
    x = np.asarray(inputs["L0_attn_x"], np.float32)
    Wq = np.asarray(inputs["L0_attn_Wq"], np.float32)
    bq = np.asarray(inputs["L0_attn_bq"], np.float32)
    Wk = np.asarray(inputs["L0_attn_Wk"], np.float32)
    bk = np.asarray(inputs["L0_attn_bk"], np.float32)
    Wv = np.asarray(inputs["L0_attn_Wv"], np.float32)
    bv = np.asarray(inputs["L0_attn_bv"], np.float32)
    Wo = np.asarray(inputs["L0_attn_Wo"], np.float32)
    bo = np.asarray(inputs["L0_attn_bo"], np.float32)
    w1 = np.asarray(inputs["L0_ffn_w1"], np.float32)
    w2 = np.asarray(inputs["L0_ffn_w2"], np.float32)

    shared = {
        "WqT": np.ascontiguousarray(Wq.T), "WkT": np.ascontiguousarray(Wk.T),
        "WvT": np.ascontiguousarray(Wv.T), "WoT": np.ascontiguousarray(Wo.T),
        "bq_t": np.ascontiguousarray(bq.reshape(ND, 128).T),
        "bk_t": np.ascontiguousarray(bk.reshape(ND, 128).T),
        "bv_b": np.ascontiguousarray(np.broadcast_to(bv, (128, D))),
        "bk_b": np.ascontiguousarray(np.broadcast_to(bk, (128, D))),
        "bo_b": np.ascontiguousarray(np.broadcast_to(bo, (128, D))),
        "w1T": np.ascontiguousarray(w1.T), "w2T": np.ascontiguousarray(w2.T),
        "triu": np.triu(np.ones((128, 128), np.float32)),
        "ident": np.eye(128, dtype=np.float32),
    }
    in_maps = []
    for core in range(8):
        b, q = divmod(core, 4)
        seg = x[b, q * TSEG:(q + 1) * TSEG, :]
        m = dict(shared)
        m["x_seg"] = np.ascontiguousarray(seg)
        m["xT_seg"] = np.ascontiguousarray(seg.T)
        pm = (np.arange(4) < q).astype(np.float32)
        m["pmask"] = np.ascontiguousarray(np.broadcast_to(pm, (128, 4)))
        in_maps.append(m)
    return in_maps


def kernel(**inputs):
    in_maps = _prep_inputs(inputs)
    nc = _get_nc(dbg=False)
    res = run_bass_kernel_spmd(nc, in_maps, core_ids=list(range(8)))
    out = np.empty((B, T, D), np.float32)
    for core in range(8):
        b, q = divmod(core, 4)
        out[b, q * TSEG:(q + 1) * TSEG, :] = np.asarray(res.results[core]["out"])
    return out


# version 06
# speedup vs baseline: 1.0506x; 1.0506x over previous
import numpy as np
import ml_dtypes
from contextlib import ExitStack
import concourse.bass as bass
import concourse.mybir as mybir
from concourse.bass_utils import run_bass_kernel_spmd

F32 = mybir.dt.float32
BF16 = mybir.dt.bfloat16
I32 = mybir.dt.int32
AX = mybir.AxisListType
ALU = mybir.AluOpType
ACT = mybir.ActivationFunctionType

B, T, D = 2, 16384, 1024
H = 4 * D
TSEG = T // 4
C = 128
G = 4
GT = G * C
NG = TSEG // GT
ND = D // 128
NH = H // 128
NT = TSEG // C
REPLICA_GROUPS = [[0, 1, 2, 3], [4, 5, 6, 7]]

ENG_ATTR = {"pe": "tensor", "dve": "vector", "act": "scalar", "sp": "sync",
            "gp": "gpsimd", "acd": "scalar"}


class _SerialEng:

    _PASSTHRU = {"wait_ge"}

    def __init__(self, eng, sem, state):
        self._eng = eng
        self._sem = sem
        self._state = state

    def __getattr__(self, name):
        fn = getattr(self._eng, name)
        if name in self._PASSTHRU or not callable(fn):
            return fn

        def wrapped(*a, **kw):
            st = self._state
            if st["issued"] > st["waited"]:
                self._eng.wait_ge(self._sem, st["issued"])
                st["waited"] = st["issued"]
            inst = fn(*a, **kw)
            inst.then_inc(self._sem, 1)
            st["issued"] += 1
            return inst
        return wrapped


class Sched:
    ENGS = ("pe", "dve", "act", "sp", "gp", "acd")
    SERIAL = {"dve", "act"}

    def __init__(self, nc, block, es):
        self.nc = nc
        self.block = block
        self.sem = {e: es.enter_context(nc.semaphore("sem_" + e)) for e in self.ENGS}
        self.sem["cc"] = es.enter_context(nc.semaphore("sem_cc"))
        self.cnt = {e: 0 for e in self.ENGS}
        self.cnt["cc"] = 0
        self.waited = {e: {e2: 0 for e2 in (*self.ENGS, "cc")} for e in self.ENGS}
        self.serial_state = {e: {"issued": 0, "waited": 0} for e in self.SERIAL}

    def step(self, eng, deps, body, n_dma=0):
        sem = self.sem
        waited = self.waited[eng]
        serial = eng in self.SERIAL
        use = []
        for d in deps:
            if d is None:
                continue
            de, dv = d
            if de == eng:
                if serial:
                    continue
            if dv > waited[de]:
                if de == eng and not n_dma and not serial:
                    continue
                use.append((de, dv))
                waited[de] = dv
        n_body_inst = [0]
        own_wait = 0
        if n_dma:
            own_wait = self.cnt[eng]
            if own_wait <= waited[eng]:
                own_wait = 0
            else:
                waited[eng] = own_wait
            self.cnt[eng] += 16 * n_dma
        issued = [0]

        def emit(e):
            if own_wait:
                e.wait_ge(sem[eng], own_wait)
            for de, dv in use:
                e.wait_ge(sem[de], dv)
            if n_dma:
                def dma(out, in_, **kw):
                    issued[0] += 1
                    return e.dma_start(out=out, in_=in_, **kw).then_inc(sem[eng], 16)
                body(e, dma)
                assert issued[0] == n_dma, f"declared {n_dma} dmas, issued {issued[0]}"
            elif serial:
                body(_SerialEng(e, sem[eng], self.serial_state[eng]))
            else:
                last = body(e)
                assert last is not None, "compute step must return its last instruction"
                last.then_inc(sem[eng], 1)

        if serial and not n_dma:
            getattr(self.block, ENG_ATTR[eng])(emit)
            self.cnt[eng] = self.serial_state[eng]["issued"]
            return (eng, self.cnt[eng])
        if not n_dma:
            self.cnt[eng] += 1
        getattr(self.block, ENG_ATTR[eng])(emit)
        return (eng, self.cnt[eng])

    def all_done(self):
        return [(e, self.cnt[e]) for e in self.ENGS]


def _finish(S, block):
    sem = S.sem
    cnts = {e: S.cnt[e] for e in S.ENGS}

    def gp_final(e):
        for e2 in S.ENGS:
            if cnts[e2]:
                e.wait_ge(sem[e2], cnts[e2])
    block.gpsimd(gp_final)


def mm_acc(pe, out, pairs):
    n = len(pairs)
    last = None
    for i, (l, r) in enumerate(pairs):
        last = pe.matmul(out, l, r, start=(i == 0), stop=(i == n - 1))
    return last


def hl3(lh, ll, rh, rl):
    return [(lh, rh), (lh, rl), (ll, rh)]


class _StopEmit(Exception):
    pass


def build_program(dbg=False, upto='p3b'):
    p2sub = upto.split('@', 1)[1] if upto.startswith('p2@') else None
    if p2sub:
        upto = 'p2'
    nc = bass.Bass("TRN2", target_bir_lowering=False)

    def dram_in(name, shape, dt=F32):
        return nc.dram_tensor(name, shape, dt, kind="ExternalInput").ap()

    def dram_scratch(name, shape, dt=F32):
        if dbg:
            return nc.dram_tensor(name, shape, dt, kind="ExternalOutput").ap()
        return nc.dram_tensor(name, shape, dt).ap()

    x_d = dram_in("x_seg", [TSEG, D])
    xT_d = dram_in("xT_seg", [D, TSEG])
    WT_d = {w: dram_in(w, [D, D]) for w in ("WqT", "WkT", "WvT", "WoT")}
    bq_d = dram_in("bq_t", [128, ND]); bk_d = dram_in("bk_t", [128, ND])
    bvb_d = dram_in("bv_b", [128, D]); bob_d = dram_in("bo_b", [128, D])
    bkb_d = dram_in("bk_b", [128, D])
    w1T_d = dram_in("w1T", [D, H]); w2T_d = dram_in("w2T", [H, D])
    triu_d = dram_in("triu", [128, 128])
    ident_d = dram_in("ident", [128, 128])
    pmask_d = dram_in("pmask", [128, 4])
    out_d = nc.dram_tensor("out", [TSEG, D], F32, kind="ExternalOutput").ap()

    spill_k = dram_scratch("spill_k", [NG, 2, 128, G * D], BF16)
    spill_v = dram_scratch("spill_v", [NG, 2, 128, G * D], BF16)
    spill_q = dram_scratch("spill_q", [NG, 2, 128, ND * GT], BF16)
    prefix_s = dram_scratch("prefix_s", [NG, 128, ND * D])
    prefix_z = dram_scratch("prefix_z", [NG, 128, ND])
    y1_dram = dram_scratch("y1_spill", [TSEG, D])
    hq_dram = dram_scratch("hq_sp", [NT, 128, H], BF16)
    w1q_dram = dram_scratch("w1q", [128, ND * H], BF16)
    w2q_dram = dram_scratch("w2q", [128, NH * D], BF16)
    sstart_d = dram_scratch("sstart", [128, ND * D])
    cc_in = nc.dram_tensor("cc_in", [D + 1, D], F32).ap()
    cc_out = nc.dram_tensor("cc_out", [4 * (D + 1), D], F32).ap()

    es = ExitStack()
    with es:
        block = es.enter_context(nc.Block())
        S = Sched(nc, block, es)
        sem = S.sem

        def SB(scope, name, shape, dt=F32):
            return scope.enter_context(nc.sbuf_tensor(name, shape, dt))

        def PS(scope, name, shape, dt=F32):
            return scope.enter_context(nc.psum_tensor(name, shape, dt))

        ones_bf = SB(es, "ones_bf", [128, 128], BF16)
        ident_f = SB(es, "ident_f", [128, 128], F32)
        ident_b = SB(es, "ident_b", [128, 128], BF16)
        triu_f = SB(es, "triu_f", [128, 128], F32)
        s1bc = SB(es, "s1bc", [128, 1]); s2bc = SB(es, "s2bc", [128, 1])
        thr1 = SB(es, "thr1", [128, 1]); nthr1 = SB(es, "nthr1", [128, 1])
        thr2 = SB(es, "thr2", [128, 1]); nthr2 = SB(es, "nthr2", [128, 1])
        bq_s = SB(es, "bq_s", [128, ND]); bk_s = SB(es, "bk_s", [128, ND])
        bvb_s = SB(es, "bvb_s", [128, D]); bob_s = SB(es, "bob_s", [128, D])
        bkb_s = SB(es, "bkb_s", [128, D])
        pmask_s = SB(es, "pmask_s", [128, 4])
        z_acc = SB(es, "z_acc", [128, ND]); z_start = SB(es, "z_start", [128, ND])
        zg = SB(es, "zg", [128, ND])
        zg_h = SB(es, "zg_h", [128, ND], BF16); zg_l = SB(es, "zg_l", [128, ND], BF16)
        f2all = SB(es, "f2all", [128, NT])
        eps_c = SB(es, "eps_c", [128, 1])
        s_scope = ExitStack()
        Smaster = SB(s_scope, "Smaster", [128, ND * D])

        def ld_const(e, dma):
            dma(out=ident_f[:], in_=ident_d[:])
            dma(out=triu_f[:], in_=triu_d[:])
            dma(out=bq_s[:], in_=bq_d[:])
            dma(out=bk_s[:], in_=bk_d[:])
            dma(out=bvb_s[:], in_=bvb_d[:])
            dma(out=bob_s[:], in_=bob_d[:])
            dma(out=bkb_s[:], in_=bkb_d[:])
            dma(out=pmask_s[:], in_=pmask_d[:])
        h_const = S.step("sp", [], ld_const, n_dma=8)

        def init_body(v):
            v.memset(ones_bf[:], 1.0)
            v.memset(eps_c[:], 1e-5)
            v.memset(z_acc[:], 0.0)
            v.memset(Smaster[:], 0.0)
            return v.tensor_copy(ident_b[:], ident_f[:])
        h_init = S.step("dve", [h_const], init_body)

        wkv_scope = ExitStack()
        wk_h = SB(wkv_scope, "wk_h", [128, ND * D], BF16)
        wk_l = SB(wkv_scope, "wk_l", [128, ND * D], BF16)
        wv_h = SB(wkv_scope, "wv_h", [128, ND * D], BF16)
        wv_l = SB(wkv_scope, "wv_l", [128, ND * D], BF16)
        wsh_scope = ExitStack()
        wstage_half = SB(wsh_scope, "wstage_half", [128, ND * D // 2])
        hws = [h_init]
        for (wd, wh, wl) in [(WT_d["WkT"], wk_h, wk_l), (WT_d["WvT"], wv_h, wv_l)]:
            for hf in range(2):
                def ldw(e, dma, wd=wd, hf=hf):
                    dma(out=wstage_half[:].rearrange("p (dt o) -> p dt o", dt=ND // 2),
                        in_=wd[hf * 512:(hf + 1) * 512, :]
                        .rearrange("(dt p) o -> p dt o", p=128))
                hlw = S.step("acd", hws, ldw, n_dma=1)

                def splw_h(a, wh=wh, hf=hf):
                    return a.copy(wh[:, hf * (ND // 2) * D:(hf + 1) * (ND // 2) * D],
                                  wstage_half[:])
                hwa = S.step("act", [hlw], splw_h)

                def splw_l(v, wh=wh, wl=wl, hf=hf):
                    return v.tensor_sub(wl[:, hf * (ND // 2) * D:(hf + 1) * (ND // 2) * D],
                                        wstage_half[:],
                                        wh[:, hf * (ND // 2) * D:(hf + 1) * (ND // 2) * D])
                hwl = S.step("dve", [hwa], splw_l)
                hws = [hwa, hwl]
        h_wkv = hws[1]
        wsh_scope.close()

        p0 = ExitStack()
        with p0:
            wpiece = SB(p0, "wpiece", [128, H])
            wpiece2 = SB(p0, "wpiece2", [128, H])
            tq2_ = SB(p0, "tq2_", [128, H], BF16)
            wsum = SB(p0, "wsum", [128, 1]); wtmp1 = SB(p0, "wtmp1", [128, 1])
            sh_ = SB(p0, "sh_", [128, 1], BF16); sl_ = SB(p0, "sl_", [128, 1], BF16)
            sc_row = SB(p0, "sc_row", [1, 2])
            sc_h = SB(p0, "sc_h", [1, 2], BF16); sc_l = SB(p0, "sc_l", [1, 2], BF16)
            cge = SB(p0, "cge", [128, H], BF16)
            tq_ = SB(p0, "tq_", [128, H], BF16)
            ps_s = PS(p0, "ps_s", [1, 2])
            ps_b = PS(p0, "ps_b", [128, 2])

            prev_wi_spill = None
            for wi, (wT_dd, npieces, pw, wq_dram, sbc, thr, nthr) in enumerate([
                    (w1T_d, ND, H, w1q_dram, s1bc, thr1, nthr1),
                    (w2T_d, NH, D, w2q_dram, s2bc, thr2, nthr2)]):
                hz = S.step("dve", [h_init], lambda v: v.memset(wsum[:], 0.0))
                hrs = [hz, hz]
                for pi in range(npieces):
                    buf = wpiece if pi % 2 == 0 else wpiece2

                    def ldp(e, dma, pi=pi, wT_dd=wT_dd, pw=pw, buf=buf):
                        dma(out=buf[:, :pw], in_=wT_dd[pi * 128:(pi + 1) * 128, :])
                    hl = S.step("sp", [hrs[pi % 2], h_wkv], ldp, n_dma=1)

                    def red(v, pw=pw, buf=buf):
                        v.reduce_sum(wtmp1[:], buf[:, :pw], axis=AX.X,
                                     apply_absolute_value=True)
                        return v.tensor_add(wsum[:], wsum[:], wtmp1[:])
                    hrs[pi % 2] = S.step("dve", [hl], red)
                hr = hrs[(npieces - 1) % 2]

                def spl(v):
                    v.tensor_copy(sh_[:], wsum[:])
                    return v.tensor_sub(sl_[:], wsum[:], sh_[:])
                hsp = S.step("dve", [hr], spl)
                hmm = S.step("pe", [hsp], lambda p, wi=wi: mm_acc(
                    p, ps_s[0:1, wi:wi + 1],
                    [(sh_[:], ones_bf[:, 0:1]), (sl_[:], ones_bf[:, 0:1])]))
                nelem = float(npieces * 128 * pw)

                def mean(v, wi=wi, nelem=nelem):
                    v.tensor_scalar(sc_row[0:1, wi:wi + 1], ps_s[0:1, wi:wi + 1],
                                    1.0 / nelem, 1e-5, ALU.mult, ALU.max)
                    v.tensor_copy(sc_h[0:1, wi:wi + 1], sc_row[0:1, wi:wi + 1])
                    return v.tensor_sub(sc_l[0:1, wi:wi + 1], sc_row[0:1, wi:wi + 1],
                                        sc_h[0:1, wi:wi + 1])
                hmean = S.step("dve", [hmm], mean)
                hbc = S.step("pe", [hmean], lambda p, wi=wi: mm_acc(
                    p, ps_b[:, wi:wi + 1],
                    [(ones_bf[0:1, :], sc_h[0:1, wi:wi + 1]),
                     (ones_bf[0:1, :], sc_l[0:1, wi:wi + 1])]))

                def mkthr(v, sbc=sbc, thr=thr, nthr=nthr, wi=wi):
                    v.tensor_copy(sbc[:], ps_b[:, wi:wi + 1])
                    v.tensor_scalar(thr[:], sbc[:], 0.5, None, ALU.mult)
                    return v.tensor_scalar(nthr[:], sbc[:], -0.5, None, ALU.mult)
                hthr = S.step("dve", [hbc], mkthr)

                hcs = [hthr, hthr]
                hqs = [prev_wi_spill, prev_wi_spill]
                hq = hthr
                for pi in range(npieces):
                    buf = wpiece if pi % 2 == 0 else wpiece2
                    tqb = tq_ if pi % 2 == 0 else tq2_

                    def ldp2(e, dma, pi=pi, wT_dd=wT_dd, pw=pw, buf=buf):
                        dma(out=buf[:, :pw], in_=wT_dd[pi * 128:(pi + 1) * 128, :])
                    hl = S.step("sp", [hcs[pi % 2], h_wkv,
                                 prev_wi_spill if pi < 2 else None],
                                ldp2, n_dma=1)

                    def qz(v, pw=pw, thr=thr, nthr=nthr, buf=buf, tqb=tqb):
                        v.tensor_scalar(cge[:, :pw], buf[:, :pw], nthr[:, 0:1], None,
                                        ALU.is_le)
                        return v.scalar_tensor_tensor(
                            tqb[:, :pw], buf[:, :pw], thr[:, 0:1], cge[:, :pw],
                            ALU.is_ge, ALU.subtract)
                    hcmp = S.step("dve", [hl, hqs[pi % 2]], qz)
                    hcs[pi % 2] = hcmp

                    def stq(e, dma, pi=pi, wq_dram=wq_dram, pw=pw, tqb=tqb):
                        dma(out=wq_dram[:, pi * pw:(pi + 1) * pw], in_=tqb[:, :pw])
                    hq = S.step("gp", [hcmp], stq, n_dma=1)
                    hqs[pi % 2] = hq
                prev_wi_spill = hq
            h_p0_done = S.all_done()
        if upto == 'p0':
            _finish(S, block); s_scope.close(); return nc

        p1 = ExitStack()
        with p1:
            xT_st = SB(p1, "xT_st", [128, ND * GT])
            xh = SB(p1, "xh", [128, ND * GT], BF16); xl = SB(p1, "xl", [128, ND * GT], BF16)
            kphi_e = SB(p1, "kphi_e", [128, D]); kphi_r = SB(p1, "kphi_r", [128, D])
            kpT_f = SB(p1, "kpT_f", [128, D])
            kT_h = SB(p1, "kT_h", [128, G * D], BF16); kT_l = SB(p1, "kT_l", [128, G * D], BF16)
            vT_h = SB(p1, "vT_h", [128, G * D], BF16); vT_l = SB(p1, "vT_l", [128, G * D], BF16)
            bkr_h = SB(p1, "bkr_h", [1, D], BF16); bkr_l = SB(p1, "bkr_l", [1, D], BF16)
            bvr_h = SB(p1, "bvr_h", [1, D], BF16); bvr_l = SB(p1, "bvr_l", [1, D], BF16)
            pk = PS(p1, "pk", [128, 2 * D])
            pv = PS(p1, "pv", [128, D])
            pkv = PS(p1, "pkv", [128, 2 * 512])

            def mk_rows_h(a):
                a.copy(bkr_h[:], bkb_s[0:1, :])
                return a.copy(bvr_h[:], bvb_s[0:1, :])
            hra = S.step("act", [h_const], mk_rows_h)

            def mk_rows_l(v):
                v.tensor_sub(bkr_l[:], bkb_s[0:1, :], bkr_h[:])
                return v.tensor_sub(bvr_l[:], bvb_s[0:1, :], bvr_h[:])
            hrl = S.step("dve", [hra], mk_rows_l)

            prev_split = h_wkv
            prev_x_pe = None
            prev_kv_done = h_init
            prev_spk = None; prev_spv = None
            prev_kv_pe = None
            for g in range(NG):
                def ldx(e, dma, g=g):
                    dma(out=xT_st[:].rearrange("p (dt t) -> p dt t", dt=ND),
                        in_=xT_d[:, g * GT:(g + 1) * GT]
                        .rearrange("(dt p) t -> p dt t", p=128))
                hx = S.step("sp", [prev_split] + (h_p0_done if g == 0 else []),
                            ldx, n_dma=1)

                hxa = S.step("act", [hx, prev_x_pe], lambda a: a.copy(xh[:], xT_st[:]))
                hxs = S.step("dve", [hxa], lambda v: v.tensor_sub(xl[:], xT_st[:], xh[:]))

                def spz(e, dma, g=g):
                    dma(out=prefix_z[g], in_=z_acc[:])
                    dma(out=prefix_s[g], in_=Smaster[:])
                hspill = S.step("gp", [prev_kv_done], spz, n_dma=2)

                hk_stt = None
                hk_hi = None
                hk_lo = None
                hz_mm = None
                zdeps = []
                for tt in range(G):
                    def mmk(p, tt=tt):
                        slot = tt % 2
                        last = None
                        for eh in range(2):
                            pairs = []
                            for dk in range(ND):
                                lh = xh[:, dk * GT + tt * 128: dk * GT + tt * 128 + 128]
                                ll = xl[:, dk * GT + tt * 128: dk * GT + tt * 128 + 128]
                                rh = wk_h[:, dk * D + eh * 512: dk * D + eh * 512 + 512]
                                rl = wk_l[:, dk * D + eh * 512: dk * D + eh * 512 + 512]
                                pairs += hl3(lh, ll, rh, rl)
                            pairs.append((ones_bf[0:1, :], bkr_h[0:1, eh * 512:(eh + 1) * 512]))
                            pairs.append((ones_bf[0:1, :], bkr_l[0:1, eh * 512:(eh + 1) * 512]))
                            last = mm_acc(p, pk[:, slot * D + eh * 512: slot * D + eh * 512 + 512],
                                          pairs)
                        return last
                    dep = [hxs, h_wkv, hrl]
                    if tt >= 2:
                        dep.append(hk_evac2[tt - 2])
                    hmk = S.step("pe", dep, mmk)

                    def ek_e(a, tt=tt):
                        return a.activation(kphi_e[:], pk[:, (tt % 2) * D:(tt % 2 + 1) * D],
                                            ACT.Exp)
                    dep = [hmk] + ([hk_stt] if hk_stt else [])
                    hea = S.step("act", dep, ek_e)

                    def ek_r(a, tt=tt):
                        return a.activation(kphi_r[:], pk[:, (tt % 2) * D:(tt % 2 + 1) * D],
                                            ACT.Relu)
                    her = S.step("act", [hea], ek_r)

                    def ek_stt(v, tt=tt):
                        return v.scalar_tensor_tensor(kpT_f[:], kphi_e[:], 1.0, kphi_r[:],
                                                      ALU.min, ALU.add)
                    hk_stt = S.step("dve", [her] + ([hk_hi] if hk_hi else []), ek_stt)

                    def ek_hi(a, tt=tt):
                        return a.copy(kT_h[:, tt * D:(tt + 1) * D], kpT_f[:])
                    hk_hi = S.step("act", [hk_stt, prev_spk, prev_kv_pe], ek_hi)

                    def ek_lo(v, tt=tt):
                        return v.tensor_sub(kT_l[:, tt * D:(tt + 1) * D], kpT_f[:],
                                            kT_h[:, tt * D:(tt + 1) * D])
                    hk_lo = S.step("dve", [hk_hi], ek_lo)
                    if tt == 0:
                        hk_evac2 = {}
                    hk_evac2[tt] = hk_lo

                    zdeps.append((hk_lo, hk_hi))

                def spk(e, dma, g=g):
                    dma(out=spill_k[g, 0], in_=kT_h[:])
                    dma(out=spill_k[g, 1], in_=kT_l[:])
                hspk = S.step("gp", [hk_lo, hk_hi], spk, n_dma=2)

                hv_hi = None
                hv_lo = None
                for tt in range(G):
                    def mmv(p, tt=tt):
                        last = None
                        for eh in range(2):
                            pairs = []
                            for dk in range(ND):
                                lh = xh[:, dk * GT + tt * 128: dk * GT + tt * 128 + 128]
                                ll = xl[:, dk * GT + tt * 128: dk * GT + tt * 128 + 128]
                                rh = wv_h[:, dk * D + eh * 512: dk * D + eh * 512 + 512]
                                rl = wv_l[:, dk * D + eh * 512: dk * D + eh * 512 + 512]
                                pairs += hl3(lh, ll, rh, rl)
                            pairs.append((ones_bf[0:1, :], bvr_h[0:1, eh * 512:(eh + 1) * 512]))
                            pairs.append((ones_bf[0:1, :], bvr_l[0:1, eh * 512:(eh + 1) * 512]))
                            last = mm_acc(p, pv[:, eh * 512:(eh + 1) * 512], pairs)
                        return last
                    dep = [hxs, h_wkv, hrl]
                    if tt >= 1:
                        dep.append(hv_lo)
                    hmv = S.step("pe", dep, mmv)

                    def zmm(p, tt=tt):
                        last = None
                        for dt in range(ND):
                            for j, pl in enumerate((kT_h, kT_l)):
                                st = (tt == 0 and dt == 0 and j == 0)
                                sp_ = (tt == G - 1 and dt == ND - 1 and j == 1)
                                last = p.matmul(pkv[:, dt:dt + 1],
                                                pl[:, tt * D + dt * 128: tt * D + dt * 128 + 128],
                                                ones_bf[:, 0:1], start=st, stop=sp_)
                        return last
                    zdep = list(zdeps[tt])
                    if tt == 0:
                        zdep.append(prev_kv_done)
                    hz_mm = S.step("pe", zdep, zmm)

                    def ev_hi(a, tt=tt):
                        return a.copy(vT_h[:, tt * D:(tt + 1) * D], pv[:])
                    hv_hi = S.step("act", [hmv, prev_spv, prev_kv_pe], ev_hi)

                    def ev_lo(v, tt=tt):
                        return v.tensor_sub(vT_l[:, tt * D:(tt + 1) * D], pv[:],
                                            vT_h[:, tt * D:(tt + 1) * D])
                    hv_lo = S.step("dve", [hmv, hv_hi], ev_lo)

                def spv(e, dma, g=g):
                    dma(out=spill_v[g, 0], in_=vT_h[:])
                    dma(out=spill_v[g, 1], in_=vT_l[:])
                hspv = S.step("gp", [hv_lo, hv_hi], spv, n_dma=2)

                def evz(v):
                    return v.tensor_add(z_acc[:], z_acc[:], pkv[:, 0:ND])
                hz_ev = S.step("dve", [hz_mm, hspill], evz)

                hkv_ev = []
                hm_last = None
                for dt in range(ND):
                    for eh in range(2):
                        idx = dt * 2 + eh

                        def mmkv(p, dt=dt, eh=eh, idx=idx):
                            pairs = []
                            for tc in range(G):
                                lh = kT_h[:, tc * D + dt * 128: tc * D + dt * 128 + 128]
                                ll = kT_l[:, tc * D + dt * 128: tc * D + dt * 128 + 128]
                                rh = vT_h[:, tc * D + eh * 512: tc * D + eh * 512 + 512]
                                rl = vT_l[:, tc * D + eh * 512: tc * D + eh * 512 + 512]
                                pairs += hl3(lh, ll, rh, rl)
                            slot = idx % 2
                            return mm_acc(p, pkv[:, slot * 512:(slot + 1) * 512], pairs)
                        dep = [hk_lo, hk_hi, hv_lo, hv_hi, hz_ev]
                        if idx >= 2:
                            dep.append(hkv_ev[idx - 2])
                        hm_last = S.step("pe", dep, mmkv)

                        def evkv(v, dt=dt, eh=eh, idx=idx):
                            slot = idx % 2
                            dst = Smaster[:, dt * D + eh * 512: dt * D + eh * 512 + 512]
                            return v.tensor_add(dst, dst, pkv[:, slot * 512:(slot + 1) * 512])
                        hkv_ev.append(S.step("dve", [hm_last, hspill], evkv))

                prev_kv_done = hkv_ev[-1]
                prev_kv_pe = hm_last
                prev_x_pe = hm_last
                prev_split = hxs
                prev_spk = hspk; prev_spv = hspv

            h_p1a_done = S.all_done()

        wkv_scope.close()
        if upto == 'p1a':
            _finish(S, block); s_scope.close(); return nc

        def wr_cc(e, dma):
            for dt in range(ND):
                dma(out=cc_in[dt * 128:(dt + 1) * 128, :],
                    in_=Smaster[:, dt * D:(dt + 1) * D])
            for dt in range(ND):
                dma(out=cc_in[D:D + 1, dt * 128:(dt + 1) * 128],
                    in_=z_acc[:, dt:dt + 1])
        h_wrcc = S.step("gp", h_p1a_done, wr_cc, n_dma=16)

        _wr_val = h_wrcc[1]

        def _cc_emit(e):
            e.wait_ge(S.sem["gp"], _wr_val)
            e.collective_compute(
                "AllGather", ALU.bypass, replica_groups=REPLICA_GROUPS,
                ins=[cc_in[:]], outs=[cc_out[:]]).then_inc(S.sem["cc"], 1)
        block.gpsimd(_cc_emit)
        S.cnt["cc"] = 1
        h_cc = ("cc", 1)

        p1b = ExitStack()
        with p1b:
            wq_h = SB(p1b, "wq_h", [128, ND * D], BF16); wq_l = SB(p1b, "wq_l", [128, ND * D], BF16)
            ws2_scope = ExitStack()
            wstage2 = SB(ws2_scope, "wstage2", [128, ND * D])

            def ldwq(e, dma):
                dma(out=wstage2[:].rearrange("p (dt o) -> p dt o", dt=ND),
                    in_=WT_d["WqT"].rearrange("(dt p) o -> p dt o", p=128))
            hlq = S.step("sp", h_p1a_done, ldwq, n_dma=1)

            def splwq(v):
                v.tensor_copy(wq_h[:], wstage2[:])
                return v.tensor_sub(wq_l[:], wstage2[:], wq_h[:])
            hwq = S.step("dve", [hlq], splwq)
            ws2_scope.close()

            xT_st2 = SB(p1b, "xT_st2", [128, ND * GT])
            xh2 = SB(p1b, "xh2", [128, ND * GT], BF16); xl2 = SB(p1b, "xl2", [128, ND * GT], BF16)
            qD_f = SB(p1b, "qD_f", [128, ND * GT])
            q_h = SB(p1b, "q_h", [128, ND * GT], BF16); q_l = SB(p1b, "q_l", [128, ND * GT], BF16)
            ephi2 = SB(p1b, "ephi2", [128, GT]); rphi2 = SB(p1b, "rphi2", [128, GT])
            pq = PS(p1b, "pq", [128, 2 * GT])

            prev_split = hwq
            prev_x_pe = None
            prev_spq = None
            for g in range(NG):
                def ldx2(e, dma, g=g):
                    dma(out=xT_st2[:].rearrange("p (dt t) -> p dt t", dt=ND),
                        in_=xT_d[:, g * GT:(g + 1) * GT]
                        .rearrange("(dt p) t -> p dt t", p=128))
                hx = S.step("sp", [prev_split], ldx2, n_dma=1)

                def splx2(v):
                    v.tensor_copy(xh2[:], xT_st2[:])
                    return v.tensor_sub(xl2[:], xT_st2[:], xh2[:])
                hxs = S.step("dve", [hx, hwq, prev_x_pe], splx2)

                hq_evac = []
                hmq = None
                for ot in range(ND):
                    def mmq(p, ot=ot):
                        pairs = []
                        for dk in range(ND):
                            lh = wq_h[:, dk * D + ot * 128: dk * D + ot * 128 + 128]
                            ll = wq_l[:, dk * D + ot * 128: dk * D + ot * 128 + 128]
                            rh = xh2[:, dk * GT:(dk + 1) * GT]
                            rl = xl2[:, dk * GT:(dk + 1) * GT]
                            pairs += hl3(lh, ll, rh, rl)
                        return mm_acc(p, pq[:, (ot % 2) * GT:(ot % 2 + 1) * GT], pairs)
                    dep = [hxs] + ([hq_evac[ot - 2][1]] if ot >= 2 else [])
                    hmq = S.step("pe", dep, mmq)

                    def eq_act(a, ot=ot):
                        a.activation(ephi2[:], pq[:, (ot % 2) * GT:(ot % 2 + 1) * GT],
                                     ACT.Exp, bias=bq_s[:, ot:ot + 1])
                        return a.activation(rphi2[:], pq[:, (ot % 2) * GT:(ot % 2 + 1) * GT],
                                            ACT.Relu, bias=bq_s[:, ot:ot + 1])
                    heq = S.step("act", [hmq] + ([hq_evac[ot - 1][1]] if ot >= 1 else []),
                                 eq_act)

                    def eq_stt(v, ot=ot):
                        return v.scalar_tensor_tensor(qD_f[:, ot * GT:(ot + 1) * GT],
                                                      ephi2[:], 1.0, rphi2[:],
                                                      ALU.min, ALU.add)
                    hst = S.step("dve", [heq, prev_spq, prev_x_pe], eq_stt)

                    def eq_hi(a, ot=ot):
                        return a.copy(q_h[:, ot * GT:(ot + 1) * GT],
                                      qD_f[:, ot * GT:(ot + 1) * GT])
                    hhi = S.step("act", [hst], eq_hi)

                    def eq_lo(v, ot=ot):
                        return v.tensor_sub(q_l[:, ot * GT:(ot + 1) * GT],
                                            qD_f[:, ot * GT:(ot + 1) * GT],
                                            q_h[:, ot * GT:(ot + 1) * GT])
                    hev = S.step("dve", [hhi], eq_lo)
                    hq_evac.append((hmq, hev))

                def spq(e, dma, g=g):
                    dma(out=spill_q[g, 0], in_=q_h[:])
                    dma(out=spill_q[g, 1], in_=q_l[:])
                prev_spq = S.step("gp", [hq_evac[-1][1]], spq, n_dma=2)
                prev_split = hxs
                prev_x_pe = hmq

            h_p1b_done = S.all_done()

        if upto == 'p1b':
            _finish(S, block); s_scope.close(); return nc

        px = ExitStack()
        with px:
            Ttmp = SB(px, "Ttmp", [128, ND * D])
            ztmp8 = SB(px, "ztmp8", [128, ND])
            hacc = None
            for r in range(4):
                def ldT(e, dma, r=r):
                    base = r * (D + 1)
                    dma(out=Ttmp[:].rearrange("p (dt e) -> p dt e", dt=ND),
                        in_=cc_out[base:base + D, :].rearrange("(dt p) e -> p dt e", p=128))
                    dma(out=ztmp8[:],
                        in_=cc_out[base + D:base + D + 1, :]
                        .rearrange("r (dt p) -> p (r dt)", p=128, r=1),
                        allow_slow_non_contiguous=True)
                dep = [h_cc] + ([hacc] if hacc else []) + (h_p1b_done if r == 0 else [])
                hl = S.step("sp", dep, ldT, n_dma=2)

                def accT(v, r=r):
                    if r == 0:
                        v.tensor_scalar(Smaster[:], Ttmp[:], pmask_s[:, 0:1], None, ALU.mult)
                        return v.tensor_scalar(z_start[:], ztmp8[:], pmask_s[:, 0:1],
                                               None, ALU.mult)
                    v.scalar_tensor_tensor(Smaster[:], Ttmp[:], pmask_s[:, r:r + 1],
                                           Smaster[:], ALU.mult, ALU.add)
                    return v.scalar_tensor_tensor(z_start[:], ztmp8[:], pmask_s[:, r:r + 1],
                                                  z_start[:], ALU.mult, ALU.add)
                hacc = S.step("dve", [hl] + h_p1b_done, accT)

            def spss(e, dma):
                dma(out=sstart_d[:], in_=Smaster[:])
            h_spss = S.step("gp", [hacc], spss, n_dma=1)
            h_x2_done = S.all_done()
        s_scope.close()
        if upto == 'x2':
            _finish(S, block); return nc

        p2 = ExitStack()
        with p2:
            wo_h = SB(p2, "wo_h", [128, ND * D], BF16); wo_l = SB(p2, "wo_l", [128, ND * D], BF16)
            ws3_scope = ExitStack()
            wstage3 = SB(ws3_scope, "wstage3", [128, ND * D])

            def ldwo(e, dma):
                dma(out=wstage3[:].rearrange("p (dt o) -> p dt o", dt=ND),
                    in_=WT_d["WoT"].rearrange("(dt p) o -> p dt o", p=128))
            hlo = S.step("sp", h_x2_done, ldwo, n_dma=1)

            def splwo(v):
                v.tensor_copy(wo_h[:], wstage3[:])
                return v.tensor_sub(wo_l[:], wstage3[:], wo_h[:])
            hwo = S.step("dve", [hlo], splwo)
            ws3_scope.close()
            if p2sub == 'w':
                _finish(S, block)
                return nc

            Sgrp = SB(p2, "Sgrp", [128, ND * D])
            s_h = SB(p2, "s_h", [128, ND * D], BF16); s_l = SB(p2, "s_l", [128, ND * D], BF16)
            q2_h = SB(p2, "q2_h", [128, ND * GT], BF16); q2_l = SB(p2, "q2_l", [128, ND * GT], BF16)
            k2_h = SB(p2, "k2_h", [128, G * D], BF16); k2_l = SB(p2, "k2_l", [128, G * D], BF16)
            v2_h = SB(p2, "v2_h", [128, G * D], BF16); v2_l = SB(p2, "v2_l", [128, G * D], BF16)
            kd_h = SB(p2, "kd_h", [128, ND * 128], BF16); kd_l = SB(p2, "kd_l", [128, ND * 128], BF16)
            aT_h = SB(p2, "aT_h", [128, G * GT], BF16); aT_l = SB(p2, "aT_l", [128, G * GT], BF16)
            n_h = SB(p2, "n_h", [128, ND * GT], BF16); n_l = SB(p2, "n_l", [128, ND * GT], BF16)
            atmp = SB(p2, "atmp", [128, 128])
            dcol = SB(p2, "dcol", [128, G])
            recip_den = SB(p2, "recip_den", [128, G])
            x_res = SB(p2, "x_res", [128, D])
            y_t = SB(p2, "y_t", [128, D])
            y1_t = SB(p2, "y1_t", [128, D])
            zgtmp = SB(p2, "zgtmp", [128, ND])
            bnb = SB(p2, "bnb", [128, 12]); bna = SB(p2, "bna", [128, 2])
            sd_c = SB(p2, "sd_c", [128, 1]); rstd_c = SB(p2, "rstd_c", [128, 1])
            negm_c = SB(p2, "negm_c", [128, 1])
            pA = PS(p2, "pA", [128, 512])
            pden = PS(p2, "pden", [128, G])
            pnum2 = PS(p2, "pnum2", [128, 2 * 512])
            pwo = PS(p2, "pwo", [128, D])
            ptrbA = PS(p2, "ptrbA", [128, 512], BF16)
            ptrbB = PS(p2, "ptrbB", [128, 512], BF16)

            h_prev_group_end = None
            prev_hey = None
            prev_hsd = None
            prev_sty1 = None
            prev_hxr_reader = None
            prev_trev = None
            prev_mn_pe = None
            h_ldsz = None
            h_accS = None
            for g in range(NG if not p2sub else 1):
              try:
                  if h_ldsz is None:
                      def ldsz0(e, dma, g=g):
                          dma(out=Sgrp[:], in_=prefix_s[g])
                          dma(out=zgtmp[:], in_=prefix_z[g])
                      h_ldsz = S.step("sp", h_x2_done + [hwo], ldsz0, n_dma=2)

                      def accS0(e, dma):
                          for i in range(4):
                              dma(out=Sgrp[:, i * 2048:(i + 1) * 2048],
                                  in_=sstart_d[:, i * 2048:(i + 1) * 2048],
                                  accum_op=ALU.add)
                      h_accS = S.step("gp", [h_ldsz], accS0, n_dma=4)

                  def ldkqv(e, dma, g=g):
                      dma(out=q2_h[:], in_=spill_q[g, 0]); dma(out=q2_l[:], in_=spill_q[g, 1])
                      dma(out=k2_h[:], in_=spill_k[g, 0]); dma(out=k2_l[:], in_=spill_k[g, 1])
                      dma(out=v2_h[:], in_=spill_v[g, 0]); dma(out=v2_l[:], in_=spill_v[g, 1])
                  hld = S.step("sp", [prev_mn_pe], ldkqv, n_dma=6)
                  if p2sub == 'ld0':
                      raise _StopEmit

                  hz0 = S.step("dve", [h_ldsz, prev_mn_pe], lambda v: v.tensor_add(
                      zg[:], zgtmp[:], z_start[:]))

                  def mks_hi(a):
                      a.copy(s_h[:], Sgrp[:])
                      return a.copy(zg_h[:], zg[:])
                  hSa = S.step("act", [h_ldsz, h_accS, hz0, prev_mn_pe], mks_hi)

                  def mks_lo(v):
                      v.tensor_sub(s_l[:], Sgrp[:], s_h[:])
                      return v.tensor_sub(zg_l[:], zg[:], zg_h[:])
                  hS = S.step("dve", [hSa], mks_lo)
                  if p2sub == 'ld':
                      raise _StopEmit

                  if g + 1 < NG:
                      def ldsz(e, dma, g=g):
                          dma(out=Sgrp[:], in_=prefix_s[g + 1])
                          dma(out=zgtmp[:], in_=prefix_z[g + 1])
                      h_ldsz = S.step("sp", [hS], ldsz, n_dma=2)

                      def accSn(e, dma):
                          for i in range(4):
                              dma(out=Sgrp[:, i * 2048:(i + 1) * 2048],
                                  in_=sstart_d[:, i * 2048:(i + 1) * 2048],
                                  accum_op=ALU.add)
                      h_accS = S.step("gp", [h_ldsz], accSn, n_dma=4)

                  hA_ev = []
                  for st_ in range(G):
                      htr_ev = None
                      for (ksrc, kdst) in ((k2_h, kd_h), (k2_l, kd_l)):
                          for half in range(2):
                              ptrb = ptrbA if half == 0 else ptrbB

                              def trk2(p, st_=st_, half=half, ksrc=ksrc, ptrb=ptrb):
                                  last = None
                                  for j in range(4):
                                      dk = half * 4 + j
                                      last = p.transpose(
                                          ptrb[:, j * 128:(j + 1) * 128],
                                          ksrc[:, st_ * D + dk * 128: st_ * D + dk * 128 + 128],
                                          ident_b[:])
                                  return last
                              ht = S.step("pe", [hld, prev_trev], trk2)

                              def evt(a, half=half, kdst=kdst, ptrb=ptrb):
                                  return a.copy(kdst[:, half * 512:(half + 1) * 512],
                                                ptrb[:])
                              prev_trev = S.step("act", [ht], evt)
                          htr_ev = prev_trev

                      ncols = GT - st_ * 128

                      def mma(p, st_=st_, ncols=ncols):
                          pairs = []
                          for dk in range(ND):
                              lh = kd_h[:, dk * 128:(dk + 1) * 128]
                              ll = kd_l[:, dk * 128:(dk + 1) * 128]
                              rh = q2_h[:, dk * GT + st_ * 128: dk * GT + GT]
                              rl = q2_l[:, dk * GT + st_ * 128: dk * GT + GT]
                              pairs += hl3(lh, ll, rh, rl)
                          return mm_acc(p, pA[:, 0:ncols], pairs)
                      dep = [htr_ev, hS]
                      if hA_ev:
                          dep.append(hA_ev[-1])
                      hma = S.step("pe", dep, mma)

                      def eva(v, st_=st_, ncols=ncols):
                          base = st_ * GT + st_ * 128
                          v.tensor_mul(atmp[:], pA[:, 0:128], triu_f[:])
                          v.tensor_copy(aT_h[:, base:base + 128], atmp[:])
                          v.scalar_tensor_tensor(aT_l[:, base:base + 128], atmp[:],
                                                 1.0, aT_h[:, base:base + 128],
                                                 ALU.mult, ALU.subtract)
                          if ncols > 128:
                              v.tensor_copy(aT_h[:, base + 128: st_ * GT + GT],
                                            pA[:, 128:ncols])
                              return v.scalar_tensor_tensor(
                                  aT_l[:, base + 128: st_ * GT + GT], pA[:, 128:ncols],
                                  1.0, aT_h[:, base + 128: st_ * GT + GT],
                                  ALU.mult, ALU.subtract)
                          return v.engine_nop()
                      hA_ev.append(S.step("dve", [hma], eva))
                  if p2sub == 'A':
                      raise _StopEmit


                  hevden = None
                  def emit_den():
                      nonlocal hevden
                      def mden(p):
                          last = None
                          first = True
                          for cq in range(G):
                              out = pden[:, cq:cq + 1]
                              for st_ in range(cq + 1):
                                  base = st_ * GT + cq * 128
                                  for pl in (aT_h, aT_l):
                                      last = p.matmul(out, pl[:, base:base + 128],
                                                      ones_bf[:, 0:1],
                                                      start=first, stop=False)
                                      first = False
                              for dk in range(ND):
                                  qh_sl = q2_h[:, dk * GT + cq * 128: dk * GT + cq * 128 + 128]
                                  ql_sl = q2_l[:, dk * GT + cq * 128: dk * GT + cq * 128 + 128]
                                  for (qp, zp) in ((qh_sl, zg_h), (ql_sl, zg_h), (qh_sl, zg_l)):
                                      isl = (cq == G - 1 and dk == ND - 1 and
                                             qp is qh_sl and zp is zg_l)
                                      last = p.matmul(out, qp, zp[:, dk:dk + 1],
                                                      start=False, stop=isl)
                          return last
                      hmden = S.step("pe", hA_ev + [hS], mden)

                      def evden(v):
                          v.tensor_scalar(dcol[:], pden[:], 1e-6, None, ALU.add)
                          return v.reciprocal(recip_den[:], dcol[:])
                      hevden = S.step("dve", [hmden], evden)

                  h_ev_lo = {}
                  h_ev_hi = {}
                  hmn = None
                  for et in range(ND):
                      def mnum(p, et=et):
                          slot = et % 2
                          pairs = []
                          for dk in range(ND):
                              sh_sl = s_h[:, dk * D + et * 128: dk * D + et * 128 + 128]
                              sl_sl = s_l[:, dk * D + et * 128: dk * D + et * 128 + 128]
                              qh_sl = q2_h[:, dk * GT:(dk + 1) * GT]
                              ql_sl = q2_l[:, dk * GT:(dk + 1) * GT]
                              pairs += hl3(sh_sl, sl_sl, qh_sl, ql_sl)
                          n = len(pairs)
                          last = None
                          for i, (l, r) in enumerate(pairs):
                              last = p.matmul(pnum2[:, slot * 512:(slot + 1) * 512],
                                              l, r, start=(i == 0), stop=False)
                          for st_ in range(G):
                              vh_sl = v2_h[:, st_ * D + et * 128: st_ * D + et * 128 + 128]
                              vl_sl = v2_l[:, st_ * D + et * 128: st_ * D + et * 128 + 128]
                              ah_sl = aT_h[:, st_ * GT + st_ * 128: st_ * GT + GT]
                              al_sl = aT_l[:, st_ * GT + st_ * 128: st_ * GT + GT]
                              ipairs = [(vh_sl, ah_sl), (vh_sl, al_sl), (vl_sl, ah_sl)]
                              for j, (l, r) in enumerate(ipairs):
                                  isl = (st_ == G - 1 and j == 2)
                                  last = p.matmul(
                                      pnum2[:, slot * 512 + st_ * 128:(slot + 1) * 512],
                                      l, r, start=False, stop=isl)
                          return last
                      dep = [hS, hA_ev[-1]]
                      if et >= 2:
                          dep += [h_ev_lo[et - 2], h_ev_hi[et - 2]]
                      hmn = S.step("pe", dep, mnum)

                      def ev_hi(a, et=et):
                          return a.copy(n_h[:, et * GT:(et + 1) * GT],
                                        pnum2[:, (et % 2) * 512:(et % 2 + 1) * 512])
                      h_ev_hi[et] = S.step("act", [hmn], ev_hi)

                      def ev_lo(v, et=et):
                          return v.scalar_tensor_tensor(
                              n_l[:, et * GT:(et + 1) * GT],
                              pnum2[:, (et % 2) * 512:(et % 2 + 1) * 512],
                              1.0, n_h[:, et * GT:(et + 1) * GT],
                              ALU.mult, ALU.subtract)
                      h_ev_lo[et] = S.step("dve", [hmn, h_ev_hi[et]], ev_lo)
                      if et == 0:
                          emit_den()
                  prev_mn_pe = hmn
                  if p2sub == 'num':
                      raise _StopEmit

                  for cq in range(G):
                      def mwo(p, cq=cq):
                          last = None
                          for eh in range(2):
                              pairs = []
                              for ek in range(ND):
                                  lh = n_h[:, ek * GT + cq * 128: ek * GT + cq * 128 + 128]
                                  ll = n_l[:, ek * GT + cq * 128: ek * GT + cq * 128 + 128]
                                  rh = wo_h[:, ek * D + eh * 512: ek * D + eh * 512 + 512]
                                  rl = wo_l[:, ek * D + eh * 512: ek * D + eh * 512 + 512]
                                  pairs += hl3(lh, ll, rh, rl)
                              last = mm_acc(p, pwo[:, eh * 512:(eh + 1) * 512], pairs)
                          return last
                      hmwo = S.step("pe", [h_ev_lo[ND - 1], h_ev_hi[ND - 1], prev_hey], mwo)
                      if p2sub == 'wo':
                          raise _StopEmit

                      def ldxr(e, dma, g=g, cq=cq):
                          t0 = (g * G + cq) * 128
                          dma(out=x_res[:], in_=x_d[t0:t0 + 128, :])
                      hxr = S.step("sp", [prev_hxr_reader], ldxr, n_dma=1)

                      def evy(v, cq=cq):
                          v.scalar_tensor_tensor(y_t[:], pwo[:], recip_den[:, cq:cq + 1],
                                                 bob_s[:], ALU.mult, ALU.add)
                          v.tensor_add(y_t[:], y_t[:], x_res[:])
                          v.bn_stats(bnb[:, 0:6], y_t[:, 0:512])
                          v.bn_stats(bnb[:, 6:12], y_t[:, 512:1024])
                          v.bn_aggr(bna[:, 0:2], bnb[:, 0:12])
                          return v.tensor_scalar(negm_c[:], bna[:, 0:1], -1.0, None, ALU.mult)
                      hey = S.step("dve", [hmwo, hxr, hevden, prev_hsd], evy)
                      prev_hey = hey
                      prev_hxr_reader = hey

                      hsd = S.step("act", [hey], lambda a: a.activation(
                          sd_c[:], bna[:, 1:2], ACT.Sqrt, bias=eps_c[:, 0:1]))
                      prev_hsd = hsd

                      def evy2(v):
                          v.reciprocal(rstd_c[:], sd_c[:])
                          return v.tensor_scalar(y1_t[:], y_t[:], negm_c[:, 0:1],
                                                 rstd_c[:, 0:1], ALU.add, ALU.mult)
                      hy1 = S.step("dve", [hsd, prev_sty1], evy2)

                      def sty1(e, dma, g=g, cq=cq):
                          t0 = (g * G + cq) * 128
                          dma(out=y1_dram[t0:t0 + 128, :], in_=y1_t[:])
                      prev_sty1 = S.step("gp", [hy1], sty1, n_dma=1)

                  h_prev_group_end = hy1
              except _StopEmit:
                pass

            h_p2_done = S.all_done()
            if p2sub:
                pass

        if upto == 'p2':
            _finish(S, block); return nc

        p3 = ExitStack()
        with p3:
            w1q_s = SB(p3, "w1q_s", [128, ND * H], BF16)
            w2q_s = SB(p3, "w2q_s", [128, NH * D], BF16)
            y1_t0 = SB(p3, "y1_t0", [128, D]); y1_t1 = SB(p3, "y1_t1", [128, D])
            m1c = SB(p3, "m1c", [128, 1]); r1c = SB(p3, "r1c", [128, 1])
            f1c = SB(p3, "f1c", [128, 1])
            m2c = SB(p3, "m2c", [128, 1]); r2c = SB(p3, "r2c", [128, 1])
            f2c = SB(p3, "f2c", [128, 1])
            mtmp = SB(p3, "mtmp", [128, 1])
            xi_ = SB(p3, "xi_", [128, D], I32)
            xb_ = SB(p3, "xb_", [128, D], BF16)
            yqD = SB(p3, "yqD", [128, ND * 128], BF16)
            h_f = SB(p3, "h_f", [128, H])
            hi_pc = SB(p3, "hi_pc", [128, 512], I32)
            hb_pc = SB(p3, "hb_pc", [128, 512], BF16)
            hqD = SB(p3, "hqD", [128, NH * 128], BF16)
            y2_t = SB(p3, "y2_t", [128, D])
            out_t = SB(p3, "out_t", [128, D])
            bnb2 = SB(p3, "bnb2", [128, 12]); bna2 = SB(p3, "bna2", [128, 2])
            sd2 = SB(p3, "sd2", [128, 1]); rstd2 = SB(p3, "rstd2", [128, 1])
            negm2 = SB(p3, "negm2", [128, 1])
            ph_ = PS(p3, "ph_", [128, 2 * 512])
            po_ = PS(p3, "po_", [128, D])
            ptrA3 = PS(p3, "ptrA3", [128, 512], BF16)
            ptrB3 = PS(p3, "ptrB3", [128, 512], BF16)

            def ldw12(e, dma):
                dma(out=w1q_s[:], in_=w1q_dram[:])
                dma(out=w2q_s[:], in_=w2q_dram[:])
            hw12 = S.step("sp", h_p2_done, ldw12, n_dma=2)

            prev_q1 = None
            prev_trq_pe = None
            prev_xb_act = None
            prev_gel = None
            prev_hq2 = None
            prev_mm1 = None
            prev_mm2 = None
            prev_hs2 = None
            prev_stout = None
            prev_trev3 = None
            prev_he2 = {0: None, 1: None}
            h_last_out = None
            hly = {}
            for ti in range(NT):
                y1b = y1_t0 if ti % 2 == 0 else y1_t1
                if ti == 0:
                    def ldy0(e, dma):
                        dma(out=y1_t0[:], in_=y1_dram[0:128, :])
                    hly[0] = S.step("sp", [hw12], ldy0, n_dma=1)

                if ti + 1 < NT:
                    def ldyn(e, dma, ti=ti):
                        nb = y1_t1 if (ti + 1) % 2 == 1 else y1_t0
                        dma(out=nb[:], in_=y1_dram[(ti + 1) * 128:(ti + 2) * 128, :])
                    hly[ti + 1] = S.step("sp", [hly[ti], prev_he2[(ti + 1) % 2]],
                                         ldyn, n_dma=1)

                def q1(v, ti=ti, y1b=y1b):
                    v.reduce_max(m1c[:], y1b[:], axis=AX.X, apply_absolute_value=True)
                    v.tensor_scalar(m1c[:], m1c[:], 1e-5, None, ALU.max)
                    v.reciprocal(r1c[:], m1c[:])
                    v.tensor_scalar(f1c[:], m1c[:], s1bc[:, 0:1], 1.0 / 127.0,
                                    ALU.mult, ALU.mult)
                    return v.tensor_scalar(xi_[:], y1b[:], r1c[:, 0:1], 127.0,
                                           ALU.mult, ALU.mult)
                hq1 = S.step("dve", [hly[ti], prev_trq_pe, prev_gel], q1)
                prev_q1 = hq1

                hxb = S.step("act", [hq1, prev_trq_pe], lambda a: a.copy(xb_[:], xi_[:]))

                htr_ev = None
                htr_pe = None
                for half in range(2):
                    ptrb = ptrA3 if half == 0 else ptrB3

                    def trq(p, half=half, ptrb=ptrb):
                        last = None
                        for j in range(4):
                            dk = half * 4 + j
                            last = p.transpose(ptrb[:, j * 128:(j + 1) * 128],
                                               xb_[:, dk * 128:(dk + 1) * 128], ident_b[:])
                        return last
                    dep = [hxb, prev_trev3] + ([prev_mm1] if half == 0 else [])
                    htr_pe = S.step("pe", dep, trq)

                    def evq(a, half=half, ptrb=ptrb):
                        return a.copy(yqD[:, half * 512:(half + 1) * 512], ptrb[:])
                    htr_ev = S.step("act", [htr_pe], evq)
                    prev_trev3 = htr_ev
                prev_trq_pe = htr_pe

                hml = {}
                hgl = {}
                hmx = None
                for hh in range(8):
                    def mm1(p, hh=hh):
                        pairs = [(yqD[:, dk * 128:(dk + 1) * 128],
                                  w1q_s[:, dk * H + hh * 512: dk * H + hh * 512 + 512])
                                 for dk in range(ND)]
                        return mm_acc(p, ph_[:, (hh % 2) * 512:(hh % 2 + 1) * 512], pairs)
                    dep = [htr_ev, hw12] + ([hgl[hh - 2]] if hh >= 2 else [])
                    hm = S.step("pe", dep, mm1)
                    hml[hh] = hm

                    def gel(a, hh=hh):
                        return a.activation(h_f[:, hh * 512:(hh + 1) * 512],
                                            ph_[:, (hh % 2) * 512:(hh % 2 + 1) * 512],
                                            ACT.Gelu, scale=f1c[:, 0:1])
                    gdep = [hm] + ([prev_hq2] if (hh == 0 and prev_hq2) else [])
                    hg = S.step("act", gdep, gel)
                    hgl[hh] = hg

                    def mx(v, hh=hh):
                        if hh == 0:
                            return v.reduce_max(m2c[:], h_f[:, 0:512], axis=AX.X,
                                                apply_absolute_value=True)
                        v.reduce_max(mtmp[:], h_f[:, hh * 512:(hh + 1) * 512], axis=AX.X,
                                     apply_absolute_value=True)
                        return v.tensor_max(m2c[:], m2c[:], mtmp[:])
                    hmx = S.step("dve", [hg], mx)
                prev_mm1 = hml[7]
                prev_gel = hgl[7]

                def q2s(v, ti=ti):
                    v.tensor_scalar(m2c[:], m2c[:], 1e-5, None, ALU.max)
                    v.reciprocal(r2c[:], m2c[:])
                    return v.tensor_scalar(f2c[:], m2c[:], s2bc[:, 0:1], 1.0 / 127.0,
                                           ALU.mult, ALU.mult)
                hq2a = S.step("dve", [hmx], q2s)

                hq2 = hq2a
                hpc_act = None
                for pc in range(8):
                    def qpc(v, pc=pc):
                        return v.tensor_scalar(hi_pc[:], h_f[:, pc * 512:(pc + 1) * 512],
                                               r2c[:, 0:1], 127.0, ALU.mult, ALU.mult)
                    hqp = S.step("dve", [hq2a] + ([hpc_act] if hpc_act else []), qpc)

                    def cpc(a, pc=pc):
                        return a.copy(hb_pc[:], hi_pc[:])
                    hpc_act = S.step("act", [hqp, prev_trq_pe if pc == 0 else None,
                                             prev_trev3], cpc)

                    ptrb = ptrA3 if pc % 2 == 0 else ptrB3

                    def trh(p, pc=pc, ptrb=ptrb):
                        last = None
                        for j in range(4):
                            last = p.transpose(ptrb[:, j * 128:(j + 1) * 128],
                                               hb_pc[:, j * 128:(j + 1) * 128], ident_b[:])
                        return last
                    dep = [hpc_act, prev_trev3] + ([prev_mm2] if pc == 0 else [])
                    htp = S.step("pe", dep, trh)

                    def evh(a, pc=pc, ptrb=ptrb):
                        return a.copy(hqD[:, pc * 512:(pc + 1) * 512], ptrb[:])
                    prev_trev3 = S.step("act", [htp], evh)
                    hq2 = prev_trev3
                prev_hq2 = hq2

                def mm2(p):
                    last = None
                    for eh in range(2):
                        pairs = [(hqD[:, hk * 128:(hk + 1) * 128],
                                  w2q_s[:, hk * D + eh * 512: hk * D + eh * 512 + 512])
                                 for hk in range(NH)]
                        last = mm_acc(p, po_[:, eh * 512:(eh + 1) * 512], pairs)
                    return last
                hm2 = S.step("pe", [hq2, hw12, prev_hs2], mm2)
                prev_mm2 = hm2

                def evy2b(v, ti=ti, y1b=y1b):
                    v.scalar_tensor_tensor(y2_t[:], po_[:], f2c[:, 0:1],
                                           y1b[:], ALU.mult, ALU.add)
                    v.bn_stats(bnb2[:, 0:6], y2_t[:, 0:512])
                    v.bn_stats(bnb2[:, 6:12], y2_t[:, 512:1024])
                    v.bn_aggr(bna2[:, 0:2], bnb2[:, 0:12])
                    return v.tensor_scalar(negm2[:], bna2[:, 0:1], -1.0, None, ALU.mult)
                he2 = S.step("dve", [hm2, prev_hs2], evy2b)
                prev_he2[ti % 2] = he2

                hs2 = S.step("act", [he2], lambda a: a.activation(
                    sd2[:], bna2[:, 1:2], ACT.Sqrt, bias=eps_c[:, 0:1]))
                prev_hs2 = hs2

                def fin(v):
                    v.reciprocal(rstd2[:], sd2[:])
                    return v.tensor_scalar(out_t[:], y2_t[:], negm2[:, 0:1],
                                           rstd2[:, 0:1], ALU.add, ALU.mult)
                hf = S.step("dve", [hs2, prev_stout], fin)

                def stout(e, dma, ti=ti):
                    dma(out=out_d[ti * 128:(ti + 1) * 128, :], in_=out_t[:])
                h_last_out = S.step("gp", [hf], stout, n_dma=1)
                prev_stout = h_last_out

            tgt = S.cnt["gp"]

            def gp_final(e):
                e.wait_ge(sem["gp"], tgt)
            block.gpsimd(gp_final)

    return nc


_CACHE = {}


def _get_nc(dbg=False):
    key = bool(dbg)
    if key not in _CACHE:
        _CACHE[key] = build_program(dbg=dbg)
    return _CACHE[key]


def _prep_inputs(inputs):
    x = np.asarray(inputs["L0_attn_x"], np.float32)
    Wq = np.asarray(inputs["L0_attn_Wq"], np.float32)
    bq = np.asarray(inputs["L0_attn_bq"], np.float32)
    Wk = np.asarray(inputs["L0_attn_Wk"], np.float32)
    bk = np.asarray(inputs["L0_attn_bk"], np.float32)
    Wv = np.asarray(inputs["L0_attn_Wv"], np.float32)
    bv = np.asarray(inputs["L0_attn_bv"], np.float32)
    Wo = np.asarray(inputs["L0_attn_Wo"], np.float32)
    bo = np.asarray(inputs["L0_attn_bo"], np.float32)
    w1 = np.asarray(inputs["L0_ffn_w1"], np.float32)
    w2 = np.asarray(inputs["L0_ffn_w2"], np.float32)

    shared = {
        "WqT": np.ascontiguousarray(Wq.T), "WkT": np.ascontiguousarray(Wk.T),
        "WvT": np.ascontiguousarray(Wv.T), "WoT": np.ascontiguousarray(Wo.T),
        "bq_t": np.ascontiguousarray(bq.reshape(ND, 128).T),
        "bk_t": np.ascontiguousarray(bk.reshape(ND, 128).T),
        "bv_b": np.ascontiguousarray(np.broadcast_to(bv, (128, D))),
        "bk_b": np.ascontiguousarray(np.broadcast_to(bk, (128, D))),
        "bo_b": np.ascontiguousarray(np.broadcast_to(bo, (128, D))),
        "w1T": np.ascontiguousarray(w1.T), "w2T": np.ascontiguousarray(w2.T),
        "triu": np.triu(np.ones((128, 128), np.float32)),
        "ident": np.eye(128, dtype=np.float32),
    }
    in_maps = []
    for core in range(8):
        b, q = divmod(core, 4)
        seg = x[b, q * TSEG:(q + 1) * TSEG, :]
        m = dict(shared)
        m["x_seg"] = np.ascontiguousarray(seg)
        m["xT_seg"] = np.ascontiguousarray(seg.T)
        pm = (np.arange(4) < q).astype(np.float32)
        m["pmask"] = np.ascontiguousarray(np.broadcast_to(pm, (128, 4)))
        in_maps.append(m)
    return in_maps


def kernel(**inputs):
    in_maps = _prep_inputs(inputs)
    nc = _get_nc(dbg=False)
    res = run_bass_kernel_spmd(nc, in_maps, core_ids=list(range(8)))
    out = np.empty((B, T, D), np.float32)
    for core in range(8):
        b, q = divmod(core, 4)
        out[b, q * TSEG:(q + 1) * TSEG, :] = np.asarray(res.results[core]["out"])
    return out


# version 07
# speedup vs baseline: 1.0532x; 1.0025x over previous
import numpy as np
import ml_dtypes
from contextlib import ExitStack
import concourse.bass as bass
import concourse.mybir as mybir
from concourse.bass_utils import run_bass_kernel_spmd

F32 = mybir.dt.float32
BF16 = mybir.dt.bfloat16
I32 = mybir.dt.int32
AX = mybir.AxisListType
ALU = mybir.AluOpType
ACT = mybir.ActivationFunctionType

B, T, D = 2, 16384, 1024
H = 4 * D
TSEG = T // 4
C = 128
G = 4
GT = G * C
NG = TSEG // GT
ND = D // 128
NH = H // 128
NT = TSEG // C
REPLICA_GROUPS = [[0, 1, 2, 3], [4, 5, 6, 7]]

ENG_ATTR = {"pe": "tensor", "dve": "vector", "act": "scalar", "sp": "sync",
            "gp": "gpsimd", "acd": "scalar"}


class _SerialEng:

    _PASSTHRU = {"wait_ge"}

    def __init__(self, eng, sem, state):
        self._eng = eng
        self._sem = sem
        self._state = state

    def __getattr__(self, name):
        fn = getattr(self._eng, name)
        if name in self._PASSTHRU or not callable(fn):
            return fn

        def wrapped(*a, **kw):
            st = self._state
            if st["issued"] > st["waited"]:
                self._eng.wait_ge(self._sem, st["issued"])
                st["waited"] = st["issued"]
            inst = fn(*a, **kw)
            inst.then_inc(self._sem, 1)
            st["issued"] += 1
            return inst
        return wrapped


class Sched:
    ENGS = ("pe", "dve", "act", "sp", "gp", "acd")
    SERIAL = {"dve", "act"}

    def __init__(self, nc, block, es):
        self.nc = nc
        self.block = block
        self.sem = {e: es.enter_context(nc.semaphore("sem_" + e)) for e in self.ENGS}
        self.sem["cc"] = es.enter_context(nc.semaphore("sem_cc"))
        self.cnt = {e: 0 for e in self.ENGS}
        self.cnt["cc"] = 0
        self.waited = {e: {e2: 0 for e2 in (*self.ENGS, "cc")} for e in self.ENGS}
        self.serial_state = {e: {"issued": 0, "waited": 0} for e in self.SERIAL}

    def step(self, eng, deps, body, n_dma=0):
        sem = self.sem
        waited = self.waited[eng]
        serial = eng in self.SERIAL
        use = []
        for d in deps:
            if d is None:
                continue
            de, dv = d
            if de == eng:
                if serial:
                    continue
            if dv > waited[de]:
                if de == eng and not n_dma and not serial:
                    continue
                use.append((de, dv))
                waited[de] = dv
        n_body_inst = [0]
        own_wait = 0
        if n_dma:
            own_wait = self.cnt[eng]
            if own_wait <= waited[eng]:
                own_wait = 0
            else:
                waited[eng] = own_wait
            self.cnt[eng] += 16 * n_dma
        issued = [0]

        def emit(e):
            if own_wait:
                e.wait_ge(sem[eng], own_wait)
            for de, dv in use:
                e.wait_ge(sem[de], dv)
            if n_dma:
                def dma(out, in_, **kw):
                    issued[0] += 1
                    return e.dma_start(out=out, in_=in_, **kw).then_inc(sem[eng], 16)
                body(e, dma)
                assert issued[0] == n_dma, f"declared {n_dma} dmas, issued {issued[0]}"
            elif serial:
                body(_SerialEng(e, sem[eng], self.serial_state[eng]))
            else:
                last = body(e)
                assert last is not None, "compute step must return its last instruction"
                last.then_inc(sem[eng], 1)

        if serial and not n_dma:
            getattr(self.block, ENG_ATTR[eng])(emit)
            self.cnt[eng] = self.serial_state[eng]["issued"]
            return (eng, self.cnt[eng])
        if not n_dma:
            self.cnt[eng] += 1
        getattr(self.block, ENG_ATTR[eng])(emit)
        return (eng, self.cnt[eng])

    def all_done(self):
        return [(e, self.cnt[e]) for e in self.ENGS]


def _finish(S, block):
    sem = S.sem
    cnts = {e: S.cnt[e] for e in S.ENGS}

    def gp_final(e):
        for e2 in S.ENGS:
            if cnts[e2]:
                e.wait_ge(sem[e2], cnts[e2])
    block.gpsimd(gp_final)


def mm_acc(pe, out, pairs):
    n = len(pairs)
    last = None
    for i, (l, r) in enumerate(pairs):
        last = pe.matmul(out, l, r, start=(i == 0), stop=(i == n - 1))
    return last


def hl3(lh, ll, rh, rl):
    return [(lh, rh), (lh, rl), (ll, rh)]


class _StopEmit(Exception):
    pass


def build_program(dbg=False, upto='p3b'):
    p2sub = upto.split('@', 1)[1] if upto.startswith('p2@') else None
    if p2sub:
        upto = 'p2'
    nc = bass.Bass("TRN2", target_bir_lowering=False)

    def dram_in(name, shape, dt=F32):
        return nc.dram_tensor(name, shape, dt, kind="ExternalInput").ap()

    def dram_scratch(name, shape, dt=F32):
        if dbg:
            return nc.dram_tensor(name, shape, dt, kind="ExternalOutput").ap()
        return nc.dram_tensor(name, shape, dt).ap()

    x_d = dram_in("x_seg", [TSEG, D])
    xT_d = dram_in("xT_seg", [D, TSEG])
    WT_d = {w: dram_in(w, [D, D]) for w in ("WqT", "WkT", "WvT", "WoT")}
    bq_d = dram_in("bq_t", [128, ND]); bk_d = dram_in("bk_t", [128, ND])
    bvb_d = dram_in("bv_b", [128, D]); bob_d = dram_in("bo_b", [128, D])
    bkb_d = dram_in("bk_b", [128, D])
    w1T_d = dram_in("w1T", [D, H]); w2T_d = dram_in("w2T", [H, D])
    triu_d = dram_in("triu", [128, 128])
    ident_d = dram_in("ident", [128, 128])
    pmask_d = dram_in("pmask", [128, 4])
    out_d = nc.dram_tensor("out", [TSEG, D], F32, kind="ExternalOutput").ap()

    spill_k = dram_scratch("spill_k", [NG, 2, 128, G * D], BF16)
    spill_v = dram_scratch("spill_v", [NG, 2, 128, G * D], BF16)
    spill_q = dram_scratch("spill_q", [NG, 2, 128, ND * GT], BF16)
    prefix_s = dram_scratch("prefix_s", [NG, 128, ND * D])
    prefix_z = dram_scratch("prefix_z", [NG, 128, ND])
    y1_dram = dram_scratch("y1_spill", [TSEG, D])
    hq_dram = dram_scratch("hq_sp", [NT, 128, H], BF16)
    w1q_dram = dram_scratch("w1q", [128, ND * H], BF16)
    w2q_dram = dram_scratch("w2q", [128, NH * D], BF16)
    sstart_d = dram_scratch("sstart", [128, ND * D])
    cc_in = nc.dram_tensor("cc_in", [D + 1, D], F32).ap()
    cc_out = nc.dram_tensor("cc_out", [4 * (D + 1), D], F32).ap()

    es = ExitStack()
    with es:
        block = es.enter_context(nc.Block())
        S = Sched(nc, block, es)
        sem = S.sem

        def SB(scope, name, shape, dt=F32):
            return scope.enter_context(nc.sbuf_tensor(name, shape, dt))

        def PS(scope, name, shape, dt=F32):
            return scope.enter_context(nc.psum_tensor(name, shape, dt))

        ones_bf = SB(es, "ones_bf", [128, 128], BF16)
        ident_f = SB(es, "ident_f", [128, 128], F32)
        ident_b = SB(es, "ident_b", [128, 128], BF16)
        triu_f = SB(es, "triu_f", [128, 128], F32)
        s1bc = SB(es, "s1bc", [128, 1]); s2bc = SB(es, "s2bc", [128, 1])
        thr1 = SB(es, "thr1", [128, 1]); nthr1 = SB(es, "nthr1", [128, 1])
        thr2 = SB(es, "thr2", [128, 1]); nthr2 = SB(es, "nthr2", [128, 1])
        bq_s = SB(es, "bq_s", [128, ND]); bk_s = SB(es, "bk_s", [128, ND])
        bvb_s = SB(es, "bvb_s", [128, D]); bob_s = SB(es, "bob_s", [128, D])
        bkb_s = SB(es, "bkb_s", [128, D])
        pmask_s = SB(es, "pmask_s", [128, 4])
        z_acc = SB(es, "z_acc", [128, ND]); z_start = SB(es, "z_start", [128, ND])
        zg = SB(es, "zg", [128, ND])
        zg_h = SB(es, "zg_h", [128, ND], BF16); zg_l = SB(es, "zg_l", [128, ND], BF16)
        f2all = SB(es, "f2all", [128, NT])
        eps_c = SB(es, "eps_c", [128, 1])
        s_scope = ExitStack()
        Smaster = SB(s_scope, "Smaster", [128, ND * D])

        def ld_const(e, dma):
            dma(out=ident_f[:], in_=ident_d[:])
            dma(out=triu_f[:], in_=triu_d[:])
            dma(out=bq_s[:], in_=bq_d[:])
            dma(out=bk_s[:], in_=bk_d[:])
            dma(out=bvb_s[:], in_=bvb_d[:])
            dma(out=bob_s[:], in_=bob_d[:])
            dma(out=bkb_s[:], in_=bkb_d[:])
            dma(out=pmask_s[:], in_=pmask_d[:])
        h_const = S.step("sp", [], ld_const, n_dma=8)

        def init_body(v):
            v.memset(ones_bf[:], 1.0)
            v.memset(eps_c[:], 1e-5)
            v.memset(z_acc[:], 0.0)
            v.memset(Smaster[:], 0.0)
            return v.tensor_copy(ident_b[:], ident_f[:])
        h_init = S.step("dve", [h_const], init_body)

        wkv_scope = ExitStack()
        wk_h = SB(wkv_scope, "wk_h", [128, ND * D], BF16)
        wk_l = SB(wkv_scope, "wk_l", [128, ND * D], BF16)
        wv_h = SB(wkv_scope, "wv_h", [128, ND * D], BF16)
        wv_l = SB(wkv_scope, "wv_l", [128, ND * D], BF16)
        wsh_scope = ExitStack()
        wstage_half = SB(wsh_scope, "wstage_half", [128, ND * D // 2])
        hws = [h_init]
        for (wd, wh, wl) in [(WT_d["WkT"], wk_h, wk_l), (WT_d["WvT"], wv_h, wv_l)]:
            for hf in range(2):
                def ldw(e, dma, wd=wd, hf=hf):
                    dma(out=wstage_half[:].rearrange("p (dt o) -> p dt o", dt=ND // 2),
                        in_=wd[hf * 512:(hf + 1) * 512, :]
                        .rearrange("(dt p) o -> p dt o", p=128))
                hlw = S.step("acd", hws, ldw, n_dma=1)

                def splw_h(a, wh=wh, hf=hf):
                    return a.copy(wh[:, hf * (ND // 2) * D:(hf + 1) * (ND // 2) * D],
                                  wstage_half[:])
                hwa = S.step("act", [hlw], splw_h)

                def splw_l(v, wh=wh, wl=wl, hf=hf):
                    return v.tensor_sub(wl[:, hf * (ND // 2) * D:(hf + 1) * (ND // 2) * D],
                                        wstage_half[:],
                                        wh[:, hf * (ND // 2) * D:(hf + 1) * (ND // 2) * D])
                hwl = S.step("dve", [hwa], splw_l)
                hws = [hwa, hwl]
        h_wkv = hws[1]
        wsh_scope.close()

        p0 = ExitStack()
        with p0:
            wpiece = SB(p0, "wpiece", [128, H])
            wpiece2 = SB(p0, "wpiece2", [128, H])
            tq2_ = SB(p0, "tq2_", [128, H], BF16)
            wsum = SB(p0, "wsum", [128, 1]); wtmp1 = SB(p0, "wtmp1", [128, 1])
            sh_ = SB(p0, "sh_", [128, 1], BF16); sl_ = SB(p0, "sl_", [128, 1], BF16)
            sc_row = SB(p0, "sc_row", [1, 2])
            sc_h = SB(p0, "sc_h", [1, 2], BF16); sc_l = SB(p0, "sc_l", [1, 2], BF16)
            cge = SB(p0, "cge", [128, H], BF16)
            tq_ = SB(p0, "tq_", [128, H], BF16)
            ps_s = PS(p0, "ps_s", [1, 2])
            ps_b = PS(p0, "ps_b", [128, 2])

            prev_wi_spill = None
            for wi, (wT_dd, npieces, pw, wq_dram, sbc, thr, nthr) in enumerate([
                    (w1T_d, ND, H, w1q_dram, s1bc, thr1, nthr1),
                    (w2T_d, NH, D, w2q_dram, s2bc, thr2, nthr2)]):
                hz = S.step("dve", [h_init], lambda v: v.memset(wsum[:], 0.0))
                hrs = [hz, hz]
                for pi in range(npieces):
                    buf = wpiece if pi % 2 == 0 else wpiece2

                    def ldp(e, dma, pi=pi, wT_dd=wT_dd, pw=pw, buf=buf):
                        dma(out=buf[:, :pw], in_=wT_dd[pi * 128:(pi + 1) * 128, :])
                    hl = S.step("sp", [hrs[pi % 2], h_wkv], ldp, n_dma=1)

                    def red(v, pw=pw, buf=buf):
                        v.reduce_sum(wtmp1[:], buf[:, :pw], axis=AX.X,
                                     apply_absolute_value=True)
                        return v.tensor_add(wsum[:], wsum[:], wtmp1[:])
                    hrs[pi % 2] = S.step("dve", [hl], red)
                hr = hrs[(npieces - 1) % 2]

                def spl(v):
                    v.tensor_copy(sh_[:], wsum[:])
                    return v.tensor_sub(sl_[:], wsum[:], sh_[:])
                hsp = S.step("dve", [hr], spl)
                hmm = S.step("pe", [hsp], lambda p, wi=wi: mm_acc(
                    p, ps_s[0:1, wi:wi + 1],
                    [(sh_[:], ones_bf[:, 0:1]), (sl_[:], ones_bf[:, 0:1])]))
                nelem = float(npieces * 128 * pw)

                def mean(v, wi=wi, nelem=nelem):
                    v.tensor_scalar(sc_row[0:1, wi:wi + 1], ps_s[0:1, wi:wi + 1],
                                    1.0 / nelem, 1e-5, ALU.mult, ALU.max)
                    v.tensor_copy(sc_h[0:1, wi:wi + 1], sc_row[0:1, wi:wi + 1])
                    return v.tensor_sub(sc_l[0:1, wi:wi + 1], sc_row[0:1, wi:wi + 1],
                                        sc_h[0:1, wi:wi + 1])
                hmean = S.step("dve", [hmm], mean)
                hbc = S.step("pe", [hmean], lambda p, wi=wi: mm_acc(
                    p, ps_b[:, wi:wi + 1],
                    [(ones_bf[0:1, :], sc_h[0:1, wi:wi + 1]),
                     (ones_bf[0:1, :], sc_l[0:1, wi:wi + 1])]))

                def mkthr(v, sbc=sbc, thr=thr, nthr=nthr, wi=wi):
                    v.tensor_copy(sbc[:], ps_b[:, wi:wi + 1])
                    v.tensor_scalar(thr[:], sbc[:], 0.5, None, ALU.mult)
                    return v.tensor_scalar(nthr[:], sbc[:], -0.5, None, ALU.mult)
                hthr = S.step("dve", [hbc], mkthr)

                hcs = [hthr, hthr]
                hqs = [prev_wi_spill, prev_wi_spill]
                hq = hthr
                for pi in range(npieces):
                    buf = wpiece if pi % 2 == 0 else wpiece2
                    tqb = tq_ if pi % 2 == 0 else tq2_

                    def ldp2(e, dma, pi=pi, wT_dd=wT_dd, pw=pw, buf=buf):
                        dma(out=buf[:, :pw], in_=wT_dd[pi * 128:(pi + 1) * 128, :])
                    hl = S.step("sp", [hcs[pi % 2], h_wkv,
                                 prev_wi_spill if pi < 2 else None],
                                ldp2, n_dma=1)

                    def qz(v, pw=pw, thr=thr, nthr=nthr, buf=buf, tqb=tqb):
                        v.tensor_scalar(cge[:, :pw], buf[:, :pw], nthr[:, 0:1], None,
                                        ALU.is_le)
                        return v.scalar_tensor_tensor(
                            tqb[:, :pw], buf[:, :pw], thr[:, 0:1], cge[:, :pw],
                            ALU.is_ge, ALU.subtract)
                    hcmp = S.step("dve", [hl, hqs[pi % 2]], qz)
                    hcs[pi % 2] = hcmp

                    def stq(e, dma, pi=pi, wq_dram=wq_dram, pw=pw, tqb=tqb):
                        dma(out=wq_dram[:, pi * pw:(pi + 1) * pw], in_=tqb[:, :pw])
                    hq = S.step("gp", [hcmp], stq, n_dma=1)
                    hqs[pi % 2] = hq
                prev_wi_spill = hq
            h_p0_done = S.all_done()
        if upto == 'p0':
            _finish(S, block); s_scope.close(); return nc

        p1 = ExitStack()
        with p1:
            xT_st = SB(p1, "xT_st", [128, ND * GT])
            xh = SB(p1, "xh", [128, ND * GT], BF16); xl = SB(p1, "xl", [128, ND * GT], BF16)
            kphi_e = SB(p1, "kphi_e", [128, D]); kphi_r = SB(p1, "kphi_r", [128, D])
            kpT_f = SB(p1, "kpT_f", [128, D])
            kT_h = SB(p1, "kT_h", [128, G * D], BF16); kT_l = SB(p1, "kT_l", [128, G * D], BF16)
            vT_h = SB(p1, "vT_h", [128, G * D], BF16); vT_l = SB(p1, "vT_l", [128, G * D], BF16)
            bkr_h = SB(p1, "bkr_h", [1, D], BF16); bkr_l = SB(p1, "bkr_l", [1, D], BF16)
            bvr_h = SB(p1, "bvr_h", [1, D], BF16); bvr_l = SB(p1, "bvr_l", [1, D], BF16)
            pk = PS(p1, "pk", [128, 2 * D])
            pv = PS(p1, "pv", [128, D])
            pkv = PS(p1, "pkv", [128, 2 * 512])

            def mk_rows_h(a):
                a.copy(bkr_h[:], bkb_s[0:1, :])
                return a.copy(bvr_h[:], bvb_s[0:1, :])
            hra = S.step("act", [h_const], mk_rows_h)

            def mk_rows_l(v):
                v.tensor_sub(bkr_l[:], bkb_s[0:1, :], bkr_h[:])
                return v.tensor_sub(bvr_l[:], bvb_s[0:1, :], bvr_h[:])
            hrl = S.step("dve", [hra], mk_rows_l)

            prev_split = h_wkv
            prev_x_pe = None
            prev_kv_done = h_init
            prev_spk = None; prev_spv = None
            prev_kv_pe = None
            for g in range(NG):
                def ldx(e, dma, g=g):
                    dma(out=xT_st[:].rearrange("p (dt t) -> p dt t", dt=ND),
                        in_=xT_d[:, g * GT:(g + 1) * GT]
                        .rearrange("(dt p) t -> p dt t", p=128))
                hx = S.step("sp", [prev_split] + (h_p0_done if g == 0 else []),
                            ldx, n_dma=1)

                hxa = S.step("act", [hx, prev_x_pe], lambda a: a.copy(xh[:], xT_st[:]))
                hxs = S.step("dve", [hxa], lambda v: v.tensor_sub(xl[:], xT_st[:], xh[:]))

                def spz(e, dma, g=g):
                    dma(out=prefix_z[g], in_=z_acc[:])
                    dma(out=prefix_s[g], in_=Smaster[:])
                hspill = S.step("gp", [prev_kv_done], spz, n_dma=2)

                hk_stt = None
                hk_hi = None
                hk_lo = None
                hz_mm = None
                zdeps = []
                for tt in range(G):
                    def mmk(p, tt=tt):
                        slot = tt % 2
                        last = None
                        for eh in range(2):
                            pairs = []
                            for dk in range(ND):
                                lh = xh[:, dk * GT + tt * 128: dk * GT + tt * 128 + 128]
                                ll = xl[:, dk * GT + tt * 128: dk * GT + tt * 128 + 128]
                                rh = wk_h[:, dk * D + eh * 512: dk * D + eh * 512 + 512]
                                rl = wk_l[:, dk * D + eh * 512: dk * D + eh * 512 + 512]
                                pairs += hl3(lh, ll, rh, rl)
                            pairs.append((ones_bf[0:1, :], bkr_h[0:1, eh * 512:(eh + 1) * 512]))
                            pairs.append((ones_bf[0:1, :], bkr_l[0:1, eh * 512:(eh + 1) * 512]))
                            last = mm_acc(p, pk[:, slot * D + eh * 512: slot * D + eh * 512 + 512],
                                          pairs)
                        return last
                    dep = [hxs, h_wkv, hrl]
                    if tt >= 2:
                        dep.append(hk_evac2[tt - 2])
                    hmk = S.step("pe", dep, mmk)

                    def ek_e(a, tt=tt):
                        return a.activation(kphi_e[:], pk[:, (tt % 2) * D:(tt % 2 + 1) * D],
                                            ACT.Exp)
                    dep = [hmk] + ([hk_stt] if hk_stt else [])
                    hea = S.step("act", dep, ek_e)

                    def ek_r(a, tt=tt):
                        return a.activation(kphi_r[:], pk[:, (tt % 2) * D:(tt % 2 + 1) * D],
                                            ACT.Relu)
                    her = S.step("act", [hea], ek_r)

                    def ek_stt(v, tt=tt):
                        return v.scalar_tensor_tensor(kpT_f[:], kphi_e[:], 1.0, kphi_r[:],
                                                      ALU.min, ALU.add)
                    hk_stt = S.step("dve", [her] + ([hk_hi] if hk_hi else []), ek_stt)

                    def ek_hi(a, tt=tt):
                        return a.copy(kT_h[:, tt * D:(tt + 1) * D], kpT_f[:])
                    hk_hi = S.step("act", [hk_stt, prev_spk, prev_kv_pe], ek_hi)

                    def ek_lo(v, tt=tt):
                        return v.tensor_sub(kT_l[:, tt * D:(tt + 1) * D], kpT_f[:],
                                            kT_h[:, tt * D:(tt + 1) * D])
                    hk_lo = S.step("dve", [hk_hi], ek_lo)
                    if tt == 0:
                        hk_evac2 = {}
                    hk_evac2[tt] = hk_lo

                    zdeps.append((hk_lo, hk_hi))

                def spk(e, dma, g=g):
                    dma(out=spill_k[g, 0], in_=kT_h[:])
                    dma(out=spill_k[g, 1], in_=kT_l[:])
                hspk = S.step("gp", [hk_lo, hk_hi], spk, n_dma=2)

                hv_hi = None
                hv_lo = None
                for tt in range(G):
                    def mmv(p, tt=tt):
                        last = None
                        for eh in range(2):
                            pairs = []
                            for dk in range(ND):
                                lh = xh[:, dk * GT + tt * 128: dk * GT + tt * 128 + 128]
                                ll = xl[:, dk * GT + tt * 128: dk * GT + tt * 128 + 128]
                                rh = wv_h[:, dk * D + eh * 512: dk * D + eh * 512 + 512]
                                rl = wv_l[:, dk * D + eh * 512: dk * D + eh * 512 + 512]
                                pairs += hl3(lh, ll, rh, rl)
                            pairs.append((ones_bf[0:1, :], bvr_h[0:1, eh * 512:(eh + 1) * 512]))
                            pairs.append((ones_bf[0:1, :], bvr_l[0:1, eh * 512:(eh + 1) * 512]))
                            last = mm_acc(p, pv[:, eh * 512:(eh + 1) * 512], pairs)
                        return last
                    dep = [hxs, h_wkv, hrl]
                    if tt >= 1:
                        dep.append(hv_lo)
                    hmv = S.step("pe", dep, mmv)

                    def zmm(p, tt=tt):
                        last = None
                        for dt in range(ND):
                            for j, pl in enumerate((kT_h, kT_l)):
                                st = (tt == 0 and dt == 0 and j == 0)
                                sp_ = (tt == G - 1 and dt == ND - 1 and j == 1)
                                last = p.matmul(pkv[:, dt:dt + 1],
                                                pl[:, tt * D + dt * 128: tt * D + dt * 128 + 128],
                                                ones_bf[:, 0:1], start=st, stop=sp_)
                        return last
                    zdep = list(zdeps[tt])
                    if tt == 0:
                        zdep.append(prev_kv_done)
                    hz_mm = S.step("pe", zdep, zmm)

                    def ev_hi(a, tt=tt):
                        return a.copy(vT_h[:, tt * D:(tt + 1) * D], pv[:])
                    hv_hi = S.step("act", [hmv, prev_spv, prev_kv_pe], ev_hi)

                    def ev_lo(v, tt=tt):
                        return v.tensor_sub(vT_l[:, tt * D:(tt + 1) * D], pv[:],
                                            vT_h[:, tt * D:(tt + 1) * D])
                    hv_lo = S.step("dve", [hmv, hv_hi], ev_lo)

                def spv(e, dma, g=g):
                    dma(out=spill_v[g, 0], in_=vT_h[:])
                    dma(out=spill_v[g, 1], in_=vT_l[:])
                hspv = S.step("gp", [hv_lo, hv_hi], spv, n_dma=2)

                def evz(v):
                    return v.tensor_add(z_acc[:], z_acc[:], pkv[:, 0:ND])
                hz_ev = S.step("dve", [hz_mm, hspill], evz)

                hkv_ev = []
                hm_last = None
                for dt in range(ND):
                    for eh in range(2):
                        idx = dt * 2 + eh

                        def mmkv(p, dt=dt, eh=eh, idx=idx):
                            pairs = []
                            for tc in range(G):
                                lh = kT_h[:, tc * D + dt * 128: tc * D + dt * 128 + 128]
                                ll = kT_l[:, tc * D + dt * 128: tc * D + dt * 128 + 128]
                                rh = vT_h[:, tc * D + eh * 512: tc * D + eh * 512 + 512]
                                rl = vT_l[:, tc * D + eh * 512: tc * D + eh * 512 + 512]
                                pairs += hl3(lh, ll, rh, rl)
                            slot = idx % 2
                            return mm_acc(p, pkv[:, slot * 512:(slot + 1) * 512], pairs)
                        dep = [hk_lo, hk_hi, hv_lo, hv_hi, hz_ev]
                        if idx >= 2:
                            dep.append(hkv_ev[idx - 2])
                        hm_last = S.step("pe", dep, mmkv)

                        def evkv(v, dt=dt, eh=eh, idx=idx):
                            slot = idx % 2
                            dst = Smaster[:, dt * D + eh * 512: dt * D + eh * 512 + 512]
                            return v.tensor_add(dst, dst, pkv[:, slot * 512:(slot + 1) * 512])
                        hkv_ev.append(S.step("dve", [hm_last, hspill], evkv))

                prev_kv_done = hkv_ev[-1]
                prev_kv_pe = hm_last
                prev_x_pe = hm_last
                prev_split = hxs
                prev_spk = hspk; prev_spv = hspv

            h_p1a_done = S.all_done()

        wkv_scope.close()
        if upto == 'p1a':
            _finish(S, block); s_scope.close(); return nc

        def wr_cc(e, dma):
            for dt in range(ND):
                dma(out=cc_in[dt * 128:(dt + 1) * 128, :],
                    in_=Smaster[:, dt * D:(dt + 1) * D])
            for dt in range(ND):
                dma(out=cc_in[D:D + 1, dt * 128:(dt + 1) * 128],
                    in_=z_acc[:, dt:dt + 1])
        h_wrcc = S.step("gp", h_p1a_done, wr_cc, n_dma=16)

        _wr_val = h_wrcc[1]

        def _cc_emit(e):
            e.wait_ge(S.sem["gp"], _wr_val)
            e.collective_compute(
                "AllGather", ALU.bypass, replica_groups=REPLICA_GROUPS,
                ins=[cc_in[:]], outs=[cc_out[:]]).then_inc(S.sem["cc"], 1)
        block.gpsimd(_cc_emit)
        S.cnt["cc"] = 1
        h_cc = ("cc", 1)

        p1b = ExitStack()
        with p1b:
            wq_h = SB(p1b, "wq_h", [128, ND * D], BF16); wq_l = SB(p1b, "wq_l", [128, ND * D], BF16)
            ws2_scope = ExitStack()
            wstage2 = SB(ws2_scope, "wstage2", [128, ND * D])

            def ldwq(e, dma):
                dma(out=wstage2[:].rearrange("p (dt o) -> p dt o", dt=ND),
                    in_=WT_d["WqT"].rearrange("(dt p) o -> p dt o", p=128))
            hlq = S.step("sp", h_p1a_done, ldwq, n_dma=1)

            def splwq(v):
                v.tensor_copy(wq_h[:], wstage2[:])
                return v.tensor_sub(wq_l[:], wstage2[:], wq_h[:])
            hwq = S.step("dve", [hlq], splwq)
            ws2_scope.close()

            xT_st2 = SB(p1b, "xT_st2", [128, ND * GT])
            xh2 = SB(p1b, "xh2", [128, ND * GT], BF16); xl2 = SB(p1b, "xl2", [128, ND * GT], BF16)
            qD_f = SB(p1b, "qD_f", [128, ND * GT])
            q_h = SB(p1b, "q_h", [128, ND * GT], BF16); q_l = SB(p1b, "q_l", [128, ND * GT], BF16)
            ephi2 = SB(p1b, "ephi2", [128, GT]); rphi2 = SB(p1b, "rphi2", [128, GT])
            pq = PS(p1b, "pq", [128, 2 * GT])
            px = ExitStack()
            Ttmp = SB(px, "Ttmp", [128, ND * D])
            ztmp8 = SB(px, "ztmp8", [128, ND])
            hacc_x2 = None
            h_spss = None

            prev_split = hwq
            prev_x_pe = None
            prev_spq = None
            for g in range(NG):
                def ldx2(e, dma, g=g):
                    dma(out=xT_st2[:].rearrange("p (dt t) -> p dt t", dt=ND),
                        in_=xT_d[:, g * GT:(g + 1) * GT]
                        .rearrange("(dt p) t -> p dt t", p=128))
                hx = S.step("sp", [prev_split], ldx2, n_dma=1)

                def splx2(v):
                    v.tensor_copy(xh2[:], xT_st2[:])
                    return v.tensor_sub(xl2[:], xT_st2[:], xh2[:])
                hxs = S.step("dve", [hx, hwq, prev_x_pe], splx2)

                hq_evac = []
                hmq = None
                for ot in range(ND):
                    def mmq(p, ot=ot):
                        pairs = []
                        for dk in range(ND):
                            lh = wq_h[:, dk * D + ot * 128: dk * D + ot * 128 + 128]
                            ll = wq_l[:, dk * D + ot * 128: dk * D + ot * 128 + 128]
                            rh = xh2[:, dk * GT:(dk + 1) * GT]
                            rl = xl2[:, dk * GT:(dk + 1) * GT]
                            pairs += hl3(lh, ll, rh, rl)
                        return mm_acc(p, pq[:, (ot % 2) * GT:(ot % 2 + 1) * GT], pairs)
                    dep = [hxs] + ([hq_evac[ot - 2][1]] if ot >= 2 else [])
                    hmq = S.step("pe", dep, mmq)

                    def eq_act(a, ot=ot):
                        a.activation(ephi2[:], pq[:, (ot % 2) * GT:(ot % 2 + 1) * GT],
                                     ACT.Exp, bias=bq_s[:, ot:ot + 1])
                        return a.activation(rphi2[:], pq[:, (ot % 2) * GT:(ot % 2 + 1) * GT],
                                            ACT.Relu, bias=bq_s[:, ot:ot + 1])
                    heq = S.step("act", [hmq] + ([hq_evac[ot - 1][1]] if ot >= 1 else []),
                                 eq_act)

                    def eq_stt(v, ot=ot):
                        return v.scalar_tensor_tensor(qD_f[:, ot * GT:(ot + 1) * GT],
                                                      ephi2[:], 1.0, rphi2[:],
                                                      ALU.min, ALU.add)
                    hst = S.step("dve", [heq, prev_spq, prev_x_pe], eq_stt)

                    def eq_hi(a, ot=ot):
                        return a.copy(q_h[:, ot * GT:(ot + 1) * GT],
                                      qD_f[:, ot * GT:(ot + 1) * GT])
                    hhi = S.step("act", [hst], eq_hi)

                    def eq_lo(v, ot=ot):
                        return v.tensor_sub(q_l[:, ot * GT:(ot + 1) * GT],
                                            qD_f[:, ot * GT:(ot + 1) * GT],
                                            q_h[:, ot * GT:(ot + 1) * GT])
                    hev = S.step("dve", [hhi], eq_lo)
                    hq_evac.append((hmq, hev))

                def spq(e, dma, g=g):
                    dma(out=spill_q[g, 0], in_=q_h[:])
                    dma(out=spill_q[g, 1], in_=q_l[:])
                prev_spq = S.step("gp", [hq_evac[-1][1]], spq, n_dma=2)
                prev_split = hxs
                prev_x_pe = hmq

                if 3 <= g <= 6:
                    r = g - 3

                    def ldT(e, dma, r=r):
                        base = r * (D + 1)
                        dma(out=Ttmp[:].rearrange("p (dt e) -> p dt e", dt=ND),
                            in_=cc_out[base:base + D, :]
                            .rearrange("(dt p) e -> p dt e", p=128))
                        dma(out=ztmp8[:],
                            in_=cc_out[base + D:base + D + 1, :]
                            .rearrange("r (dt p) -> p (r dt)", p=128, r=1),
                            allow_slow_non_contiguous=True)
                    hl = S.step("acd", [h_cc] + ([hacc_x2] if hacc_x2 else []),
                                ldT, n_dma=2)

                    def accT(v, r=r):
                        if r == 0:
                            v.tensor_scalar(Smaster[:], Ttmp[:], pmask_s[:, 0:1],
                                            None, ALU.mult)
                            return v.tensor_scalar(z_start[:], ztmp8[:],
                                                   pmask_s[:, 0:1], None, ALU.mult)
                        v.scalar_tensor_tensor(Smaster[:], Ttmp[:], pmask_s[:, r:r + 1],
                                               Smaster[:], ALU.mult, ALU.add)
                        return v.scalar_tensor_tensor(z_start[:], ztmp8[:],
                                                      pmask_s[:, r:r + 1],
                                                      z_start[:], ALU.mult, ALU.add)
                    hacc_x2 = S.step("dve", [hl], accT)
                    if r == 3:
                        def spss(e, dma):
                            dma(out=sstart_d[:], in_=Smaster[:])
                        h_spss = S.step("acd", [hacc_x2], spss, n_dma=1)

            h_p1b_done = S.all_done()
            px.close()

        if upto == 'p1b':
            _finish(S, block); s_scope.close(); return nc

        h_x2_done = h_p1b_done
        s_scope.close()
        if upto == 'x2':
            _finish(S, block); return nc

        p2 = ExitStack()
        with p2:
            wo_h = SB(p2, "wo_h", [128, ND * D], BF16); wo_l = SB(p2, "wo_l", [128, ND * D], BF16)
            ws3_scope = ExitStack()
            wstage3 = SB(ws3_scope, "wstage3", [128, ND * D])

            def ldwo(e, dma):
                dma(out=wstage3[:].rearrange("p (dt o) -> p dt o", dt=ND),
                    in_=WT_d["WoT"].rearrange("(dt p) o -> p dt o", p=128))
            hlo = S.step("sp", h_x2_done, ldwo, n_dma=1)

            def splwo(v):
                v.tensor_copy(wo_h[:], wstage3[:])
                return v.tensor_sub(wo_l[:], wstage3[:], wo_h[:])
            hwo = S.step("dve", [hlo], splwo)
            ws3_scope.close()
            if p2sub == 'w':
                _finish(S, block)
                return nc

            Sgrp = SB(p2, "Sgrp", [128, ND * D])
            s_h = SB(p2, "s_h", [128, ND * D], BF16); s_l = SB(p2, "s_l", [128, ND * D], BF16)
            q2_h = SB(p2, "q2_h", [128, ND * GT], BF16); q2_l = SB(p2, "q2_l", [128, ND * GT], BF16)
            k2_h = SB(p2, "k2_h", [128, G * D], BF16); k2_l = SB(p2, "k2_l", [128, G * D], BF16)
            v2_h = SB(p2, "v2_h", [128, G * D], BF16); v2_l = SB(p2, "v2_l", [128, G * D], BF16)
            kd_h = SB(p2, "kd_h", [128, ND * 128], BF16); kd_l = SB(p2, "kd_l", [128, ND * 128], BF16)
            aT_h = SB(p2, "aT_h", [128, G * GT], BF16); aT_l = SB(p2, "aT_l", [128, G * GT], BF16)
            n_h = SB(p2, "n_h", [128, ND * GT], BF16); n_l = SB(p2, "n_l", [128, ND * GT], BF16)
            atmp = SB(p2, "atmp", [128, 128])
            dcol = SB(p2, "dcol", [128, G])
            recip_den = SB(p2, "recip_den", [128, G])
            x_res = SB(p2, "x_res", [128, D])
            y_t = SB(p2, "y_t", [128, D])
            y1_t = SB(p2, "y1_t", [128, D])
            zgtmp = SB(p2, "zgtmp", [128, ND])
            bnb = SB(p2, "bnb", [128, 12]); bna = SB(p2, "bna", [128, 2])
            sd_c = SB(p2, "sd_c", [128, 1]); rstd_c = SB(p2, "rstd_c", [128, 1])
            negm_c = SB(p2, "negm_c", [128, 1])
            pA = PS(p2, "pA", [128, 512])
            pden = PS(p2, "pden", [128, G])
            pnum2 = PS(p2, "pnum2", [128, 2 * 512])
            pwo = PS(p2, "pwo", [128, D])
            ptrbA = PS(p2, "ptrbA", [128, 512], BF16)
            ptrbB = PS(p2, "ptrbB", [128, 512], BF16)

            h_prev_group_end = None
            prev_hey = None
            prev_hsd = None
            prev_sty1 = None
            prev_hxr_reader = None
            prev_trev = None
            prev_mn_pe = None
            h_ldsz = None
            h_accS = None
            for g in range(NG if not p2sub else 1):
              try:
                  if h_ldsz is None:
                      def ldsz0(e, dma, g=g):
                          dma(out=Sgrp[:], in_=prefix_s[g])
                          dma(out=zgtmp[:], in_=prefix_z[g])
                      h_ldsz = S.step("sp", h_x2_done + [hwo], ldsz0, n_dma=2)

                      def accS0(e, dma):
                          for i in range(4):
                              dma(out=Sgrp[:, i * 2048:(i + 1) * 2048],
                                  in_=sstart_d[:, i * 2048:(i + 1) * 2048],
                                  accum_op=ALU.add)
                      h_accS = S.step("gp", [h_ldsz], accS0, n_dma=4)

                  def ldkqv(e, dma, g=g):
                      dma(out=q2_h[:], in_=spill_q[g, 0]); dma(out=q2_l[:], in_=spill_q[g, 1])
                      dma(out=k2_h[:], in_=spill_k[g, 0]); dma(out=k2_l[:], in_=spill_k[g, 1])
                      dma(out=v2_h[:], in_=spill_v[g, 0]); dma(out=v2_l[:], in_=spill_v[g, 1])
                  hld = S.step("sp", [prev_mn_pe], ldkqv, n_dma=6)
                  if p2sub == 'ld0':
                      raise _StopEmit

                  hz0 = S.step("dve", [h_ldsz, prev_mn_pe], lambda v: v.tensor_add(
                      zg[:], zgtmp[:], z_start[:]))

                  def mks_hi(a):
                      a.copy(s_h[:], Sgrp[:])
                      return a.copy(zg_h[:], zg[:])
                  hSa = S.step("act", [h_ldsz, h_accS, hz0, prev_mn_pe], mks_hi)

                  def mks_lo(v):
                      v.tensor_sub(s_l[:], Sgrp[:], s_h[:])
                      return v.tensor_sub(zg_l[:], zg[:], zg_h[:])
                  hS = S.step("dve", [hSa], mks_lo)
                  if p2sub == 'ld':
                      raise _StopEmit

                  if g + 1 < NG:
                      def ldsz(e, dma, g=g):
                          dma(out=Sgrp[:], in_=prefix_s[g + 1])
                          dma(out=zgtmp[:], in_=prefix_z[g + 1])
                      h_ldsz = S.step("sp", [hS], ldsz, n_dma=2)

                      def accSn(e, dma):
                          for i in range(4):
                              dma(out=Sgrp[:, i * 2048:(i + 1) * 2048],
                                  in_=sstart_d[:, i * 2048:(i + 1) * 2048],
                                  accum_op=ALU.add)
                      h_accS = S.step("gp", [h_ldsz], accSn, n_dma=4)

                  hA_ev = []
                  for st_ in range(G):
                      htr_ev = None
                      for (ksrc, kdst) in ((k2_h, kd_h), (k2_l, kd_l)):
                          for half in range(2):
                              ptrb = ptrbA if half == 0 else ptrbB

                              def trk2(p, st_=st_, half=half, ksrc=ksrc, ptrb=ptrb):
                                  last = None
                                  for j in range(4):
                                      dk = half * 4 + j
                                      last = p.transpose(
                                          ptrb[:, j * 128:(j + 1) * 128],
                                          ksrc[:, st_ * D + dk * 128: st_ * D + dk * 128 + 128],
                                          ident_b[:])
                                  return last
                              ht = S.step("pe", [hld, prev_trev], trk2)

                              def evt(a, half=half, kdst=kdst, ptrb=ptrb):
                                  return a.copy(kdst[:, half * 512:(half + 1) * 512],
                                                ptrb[:])
                              prev_trev = S.step("act", [ht], evt)
                          htr_ev = prev_trev

                      ncols = GT - st_ * 128

                      def mma(p, st_=st_, ncols=ncols):
                          pairs = []
                          for dk in range(ND):
                              lh = kd_h[:, dk * 128:(dk + 1) * 128]
                              ll = kd_l[:, dk * 128:(dk + 1) * 128]
                              rh = q2_h[:, dk * GT + st_ * 128: dk * GT + GT]
                              rl = q2_l[:, dk * GT + st_ * 128: dk * GT + GT]
                              pairs += hl3(lh, ll, rh, rl)
                          return mm_acc(p, pA[:, 0:ncols], pairs)
                      dep = [htr_ev, hS]
                      if hA_ev:
                          dep.append(hA_ev[-1])
                      hma = S.step("pe", dep, mma)

                      def eva(v, st_=st_, ncols=ncols):
                          base = st_ * GT + st_ * 128
                          v.tensor_mul(atmp[:], pA[:, 0:128], triu_f[:])
                          v.tensor_copy(aT_h[:, base:base + 128], atmp[:])
                          v.scalar_tensor_tensor(aT_l[:, base:base + 128], atmp[:],
                                                 1.0, aT_h[:, base:base + 128],
                                                 ALU.mult, ALU.subtract)
                          if ncols > 128:
                              v.tensor_copy(aT_h[:, base + 128: st_ * GT + GT],
                                            pA[:, 128:ncols])
                              return v.scalar_tensor_tensor(
                                  aT_l[:, base + 128: st_ * GT + GT], pA[:, 128:ncols],
                                  1.0, aT_h[:, base + 128: st_ * GT + GT],
                                  ALU.mult, ALU.subtract)
                          return v.engine_nop()
                      hA_ev.append(S.step("dve", [hma], eva))
                  if p2sub == 'A':
                      raise _StopEmit


                  hevden = None
                  def emit_den():
                      nonlocal hevden
                      def mden(p):
                          last = None
                          first = True
                          for cq in range(G):
                              out = pden[:, cq:cq + 1]
                              for st_ in range(cq + 1):
                                  base = st_ * GT + cq * 128
                                  for pl in (aT_h, aT_l):
                                      last = p.matmul(out, pl[:, base:base + 128],
                                                      ones_bf[:, 0:1],
                                                      start=first, stop=False)
                                      first = False
                              for dk in range(ND):
                                  qh_sl = q2_h[:, dk * GT + cq * 128: dk * GT + cq * 128 + 128]
                                  ql_sl = q2_l[:, dk * GT + cq * 128: dk * GT + cq * 128 + 128]
                                  for (qp, zp) in ((qh_sl, zg_h), (ql_sl, zg_h), (qh_sl, zg_l)):
                                      isl = (cq == G - 1 and dk == ND - 1 and
                                             qp is qh_sl and zp is zg_l)
                                      last = p.matmul(out, qp, zp[:, dk:dk + 1],
                                                      start=False, stop=isl)
                          return last
                      hmden = S.step("pe", hA_ev + [hS], mden)

                      def evden(v):
                          v.tensor_scalar(dcol[:], pden[:], 1e-6, None, ALU.add)
                          return v.reciprocal(recip_den[:], dcol[:])
                      hevden = S.step("dve", [hmden], evden)

                  h_ev_lo = {}
                  h_ev_hi = {}
                  hmn = None
                  for et in range(ND):
                      def mnum(p, et=et):
                          slot = et % 2
                          pairs = []
                          for dk in range(ND):
                              sh_sl = s_h[:, dk * D + et * 128: dk * D + et * 128 + 128]
                              sl_sl = s_l[:, dk * D + et * 128: dk * D + et * 128 + 128]
                              qh_sl = q2_h[:, dk * GT:(dk + 1) * GT]
                              ql_sl = q2_l[:, dk * GT:(dk + 1) * GT]
                              pairs += hl3(sh_sl, sl_sl, qh_sl, ql_sl)
                          n = len(pairs)
                          last = None
                          for i, (l, r) in enumerate(pairs):
                              last = p.matmul(pnum2[:, slot * 512:(slot + 1) * 512],
                                              l, r, start=(i == 0), stop=False)
                          for st_ in range(G):
                              vh_sl = v2_h[:, st_ * D + et * 128: st_ * D + et * 128 + 128]
                              vl_sl = v2_l[:, st_ * D + et * 128: st_ * D + et * 128 + 128]
                              ah_sl = aT_h[:, st_ * GT + st_ * 128: st_ * GT + GT]
                              al_sl = aT_l[:, st_ * GT + st_ * 128: st_ * GT + GT]
                              ipairs = [(vh_sl, ah_sl), (vh_sl, al_sl), (vl_sl, ah_sl)]
                              for j, (l, r) in enumerate(ipairs):
                                  isl = (st_ == G - 1 and j == 2)
                                  last = p.matmul(
                                      pnum2[:, slot * 512 + st_ * 128:(slot + 1) * 512],
                                      l, r, start=False, stop=isl)
                          return last
                      dep = [hS, hA_ev[-1]]
                      if et >= 2:
                          dep += [h_ev_lo[et - 2], h_ev_hi[et - 2]]
                      hmn = S.step("pe", dep, mnum)

                      def ev_hi(a, et=et):
                          return a.copy(n_h[:, et * GT:(et + 1) * GT],
                                        pnum2[:, (et % 2) * 512:(et % 2 + 1) * 512])
                      h_ev_hi[et] = S.step("act", [hmn], ev_hi)

                      def ev_lo(v, et=et):
                          return v.scalar_tensor_tensor(
                              n_l[:, et * GT:(et + 1) * GT],
                              pnum2[:, (et % 2) * 512:(et % 2 + 1) * 512],
                              1.0, n_h[:, et * GT:(et + 1) * GT],
                              ALU.mult, ALU.subtract)
                      h_ev_lo[et] = S.step("dve", [hmn, h_ev_hi[et]], ev_lo)
                      if et == 0:
                          emit_den()
                  prev_mn_pe = hmn
                  if p2sub == 'num':
                      raise _StopEmit

                  for cq in range(G):
                      def mwo(p, cq=cq):
                          last = None
                          for eh in range(2):
                              pairs = []
                              for ek in range(ND):
                                  lh = n_h[:, ek * GT + cq * 128: ek * GT + cq * 128 + 128]
                                  ll = n_l[:, ek * GT + cq * 128: ek * GT + cq * 128 + 128]
                                  rh = wo_h[:, ek * D + eh * 512: ek * D + eh * 512 + 512]
                                  rl = wo_l[:, ek * D + eh * 512: ek * D + eh * 512 + 512]
                                  pairs += hl3(lh, ll, rh, rl)
                              last = mm_acc(p, pwo[:, eh * 512:(eh + 1) * 512], pairs)
                          return last
                      hmwo = S.step("pe", [h_ev_lo[ND - 1], h_ev_hi[ND - 1], prev_hey], mwo)
                      if p2sub == 'wo':
                          raise _StopEmit

                      def ldxr(e, dma, g=g, cq=cq):
                          t0 = (g * G + cq) * 128
                          dma(out=x_res[:], in_=x_d[t0:t0 + 128, :])
                      hxr = S.step("sp", [prev_hxr_reader], ldxr, n_dma=1)

                      def evy(v, cq=cq):
                          v.scalar_tensor_tensor(y_t[:], pwo[:], recip_den[:, cq:cq + 1],
                                                 bob_s[:], ALU.mult, ALU.add)
                          v.tensor_add(y_t[:], y_t[:], x_res[:])
                          v.bn_stats(bnb[:, 0:6], y_t[:, 0:512])
                          v.bn_stats(bnb[:, 6:12], y_t[:, 512:1024])
                          v.bn_aggr(bna[:, 0:2], bnb[:, 0:12])
                          return v.tensor_scalar(negm_c[:], bna[:, 0:1], -1.0, None, ALU.mult)
                      hey = S.step("dve", [hmwo, hxr, hevden, prev_hsd], evy)
                      prev_hey = hey
                      prev_hxr_reader = hey

                      hsd = S.step("act", [hey], lambda a: a.activation(
                          sd_c[:], bna[:, 1:2], ACT.Sqrt, bias=eps_c[:, 0:1]))
                      prev_hsd = hsd

                      def evy2(v):
                          v.reciprocal(rstd_c[:], sd_c[:])
                          return v.tensor_scalar(y1_t[:], y_t[:], negm_c[:, 0:1],
                                                 rstd_c[:, 0:1], ALU.add, ALU.mult)
                      hy1 = S.step("dve", [hsd, prev_sty1], evy2)

                      def sty1(e, dma, g=g, cq=cq):
                          t0 = (g * G + cq) * 128
                          dma(out=y1_dram[t0:t0 + 128, :], in_=y1_t[:])
                      prev_sty1 = S.step("gp", [hy1], sty1, n_dma=1)

                  h_prev_group_end = hy1
              except _StopEmit:
                pass

            h_p2_done = S.all_done()
            if p2sub:
                pass

        if upto == 'p2':
            _finish(S, block); return nc

        p3 = ExitStack()
        with p3:
            w1q_s = SB(p3, "w1q_s", [128, ND * H], BF16)
            w2q_s = SB(p3, "w2q_s", [128, NH * D], BF16)
            y1_t0 = SB(p3, "y1_t0", [128, D]); y1_t1 = SB(p3, "y1_t1", [128, D])
            m1c = SB(p3, "m1c", [128, 1]); r1c = SB(p3, "r1c", [128, 1])
            f1c = SB(p3, "f1c", [128, 1])
            m2c = SB(p3, "m2c", [128, 1]); r2c = SB(p3, "r2c", [128, 1])
            f2c = SB(p3, "f2c", [128, 1])
            mtmp = SB(p3, "mtmp", [128, 1])
            xi_ = SB(p3, "xi_", [128, D], I32)
            xb_ = SB(p3, "xb_", [128, D], BF16)
            yqD = SB(p3, "yqD", [128, ND * 128], BF16)
            h_f = SB(p3, "h_f", [128, H])
            hi_pc0 = SB(p3, "hi_pc0", [128, 512], I32)
            hi_pc1 = SB(p3, "hi_pc1", [128, 512], I32)
            hb_pc0 = SB(p3, "hb_pc0", [128, 512], BF16)
            hb_pc1 = SB(p3, "hb_pc1", [128, 512], BF16)
            hqD = SB(p3, "hqD", [128, NH * 128], BF16)
            y2_t = SB(p3, "y2_t", [128, D])
            out_t = SB(p3, "out_t", [128, D])
            bnb2 = SB(p3, "bnb2", [128, 12]); bna2 = SB(p3, "bna2", [128, 2])
            sd2 = SB(p3, "sd2", [128, 1]); rstd2 = SB(p3, "rstd2", [128, 1])
            negm2 = SB(p3, "negm2", [128, 1])
            ph_ = PS(p3, "ph_", [128, 2 * 512])
            po_ = PS(p3, "po_", [128, D])
            ptrA3 = PS(p3, "ptrA3", [128, 512], BF16)
            ptrB3 = PS(p3, "ptrB3", [128, 512], BF16)

            def ldw12(e, dma):
                dma(out=w1q_s[:], in_=w1q_dram[:])
                dma(out=w2q_s[:], in_=w2q_dram[:])
            hw12 = S.step("sp", h_p2_done, ldw12, n_dma=2)

            prev_q1 = None
            prev_trq_pe = None
            prev_xb_act = None
            prev_gel = None
            prev_hq2 = None
            prev_mm1 = None
            prev_mm2 = None
            prev_hs2 = None
            prev_stout = None
            prev_trev3 = None
            prev_he2 = {0: None, 1: None}
            h_last_out = None
            hly = {}
            for ti in range(NT):
                y1b = y1_t0 if ti % 2 == 0 else y1_t1
                if ti == 0:
                    def ldy0(e, dma):
                        dma(out=y1_t0[:], in_=y1_dram[0:128, :])
                    hly[0] = S.step("sp", [hw12], ldy0, n_dma=1)

                if ti + 1 < NT:
                    def ldyn(e, dma, ti=ti):
                        nb = y1_t1 if (ti + 1) % 2 == 1 else y1_t0
                        dma(out=nb[:], in_=y1_dram[(ti + 1) * 128:(ti + 2) * 128, :])
                    hly[ti + 1] = S.step("sp", [hly[ti], prev_he2[(ti + 1) % 2]],
                                         ldyn, n_dma=1)

                def q1(v, ti=ti, y1b=y1b):
                    v.reduce_max(m1c[:], y1b[:], axis=AX.X, apply_absolute_value=True)
                    v.tensor_scalar(m1c[:], m1c[:], 1e-5, None, ALU.max)
                    v.reciprocal(r1c[:], m1c[:])
                    v.tensor_scalar(f1c[:], m1c[:], s1bc[:, 0:1], 1.0 / 127.0,
                                    ALU.mult, ALU.mult)
                    return v.tensor_scalar(xi_[:], y1b[:], r1c[:, 0:1], 127.0,
                                           ALU.mult, ALU.mult)
                hq1 = S.step("dve", [hly[ti], prev_trq_pe, prev_gel], q1)
                prev_q1 = hq1

                hxb = S.step("act", [hq1, prev_trq_pe], lambda a: a.copy(xb_[:], xi_[:]))

                htr_ev = None
                htr_pe = None
                for half in range(2):
                    ptrb = ptrA3 if half == 0 else ptrB3

                    def trq(p, half=half, ptrb=ptrb):
                        last = None
                        for j in range(4):
                            dk = half * 4 + j
                            last = p.transpose(ptrb[:, j * 128:(j + 1) * 128],
                                               xb_[:, dk * 128:(dk + 1) * 128], ident_b[:])
                        return last
                    dep = [hxb, prev_trev3] + ([prev_mm1] if half == 0 else [])
                    htr_pe = S.step("pe", dep, trq)

                    def evq(a, half=half, ptrb=ptrb):
                        return a.copy(yqD[:, half * 512:(half + 1) * 512], ptrb[:])
                    htr_ev = S.step("act", [htr_pe], evq)
                    prev_trev3 = htr_ev
                prev_trq_pe = htr_pe

                hml = {}
                hgl = {}
                hmx = None
                for hh in range(8):
                    def mm1(p, hh=hh):
                        pairs = [(yqD[:, dk * 128:(dk + 1) * 128],
                                  w1q_s[:, dk * H + hh * 512: dk * H + hh * 512 + 512])
                                 for dk in range(ND)]
                        return mm_acc(p, ph_[:, (hh % 2) * 512:(hh % 2 + 1) * 512], pairs)
                    dep = [htr_ev, hw12] + ([hgl[hh - 2]] if hh >= 2 else [])
                    hm = S.step("pe", dep, mm1)
                    hml[hh] = hm

                    def gel(a, hh=hh):
                        return a.activation(h_f[:, hh * 512:(hh + 1) * 512],
                                            ph_[:, (hh % 2) * 512:(hh % 2 + 1) * 512],
                                            ACT.Gelu, scale=f1c[:, 0:1])
                    gdep = [hm] + ([prev_hq2] if (hh == 0 and prev_hq2) else [])
                    hg = S.step("act", gdep, gel)
                    hgl[hh] = hg

                    def mx(v, hh=hh):
                        if hh == 0:
                            return v.reduce_max(m2c[:], h_f[:, 0:512], axis=AX.X,
                                                apply_absolute_value=True)
                        v.reduce_max(mtmp[:], h_f[:, hh * 512:(hh + 1) * 512], axis=AX.X,
                                     apply_absolute_value=True)
                        return v.tensor_max(m2c[:], m2c[:], mtmp[:])
                    hmx = S.step("dve", [hg], mx)
                prev_mm1 = hml[7]
                prev_gel = hgl[7]

                def q2s(v, ti=ti):
                    v.tensor_scalar(m2c[:], m2c[:], 1e-5, None, ALU.max)
                    v.reciprocal(r2c[:], m2c[:])
                    return v.tensor_scalar(f2c[:], m2c[:], s2bc[:, 0:1], 1.0 / 127.0,
                                           ALU.mult, ALU.mult)
                hq2a = S.step("dve", [hmx], q2s)

                hq2 = hq2a
                hpc_acts = {}
                htps = {}
                for pc in range(8):
                    hip = hi_pc0 if pc % 2 == 0 else hi_pc1
                    hbp = hb_pc0 if pc % 2 == 0 else hb_pc1

                    def qpc(v, pc=pc, hip=hip):
                        return v.tensor_scalar(hip[:], h_f[:, pc * 512:(pc + 1) * 512],
                                               r2c[:, 0:1], 127.0, ALU.mult, ALU.mult)
                    hqp = S.step("dve", [hq2a] + ([hpc_acts.get(pc - 2)] if pc >= 2 else []),
                                 qpc)

                    def cpc(a, pc=pc, hip=hip, hbp=hbp):
                        return a.copy(hbp[:], hip[:])
                    hpc_act = S.step("act", [hqp, prev_trq_pe if pc == 0 else None,
                                             htps.get(pc - 2), prev_trev3], cpc)
                    hpc_acts[pc] = hpc_act

                    ptrb = ptrA3 if pc % 2 == 0 else ptrB3

                    def trh(p, pc=pc, ptrb=ptrb, hbp=hbp):
                        last = None
                        for j in range(4):
                            last = p.transpose(ptrb[:, j * 128:(j + 1) * 128],
                                               hbp[:, j * 128:(j + 1) * 128], ident_b[:])
                        return last
                    dep = [hpc_act, prev_trev3] + ([prev_mm2] if pc == 0 else [])
                    htp = S.step("pe", dep, trh)
                    htps[pc] = htp

                    def evh(a, pc=pc, ptrb=ptrb):
                        return a.copy(hqD[:, pc * 512:(pc + 1) * 512], ptrb[:])
                    prev_trev3 = S.step("act", [htp], evh)
                    hq2 = prev_trev3
                prev_hq2 = hq2

                def mm2(p):
                    last = None
                    for eh in range(2):
                        pairs = [(hqD[:, hk * 128:(hk + 1) * 128],
                                  w2q_s[:, hk * D + eh * 512: hk * D + eh * 512 + 512])
                                 for hk in range(NH)]
                        last = mm_acc(p, po_[:, eh * 512:(eh + 1) * 512], pairs)
                    return last
                hm2 = S.step("pe", [hq2, hw12, prev_hs2], mm2)
                prev_mm2 = hm2

                def evy2b(v, ti=ti, y1b=y1b):
                    v.scalar_tensor_tensor(y2_t[:], po_[:], f2c[:, 0:1],
                                           y1b[:], ALU.mult, ALU.add)
                    v.bn_stats(bnb2[:, 0:6], y2_t[:, 0:512])
                    v.bn_stats(bnb2[:, 6:12], y2_t[:, 512:1024])
                    v.bn_aggr(bna2[:, 0:2], bnb2[:, 0:12])
                    return v.tensor_scalar(negm2[:], bna2[:, 0:1], -1.0, None, ALU.mult)
                he2 = S.step("dve", [hm2, prev_hs2], evy2b)
                prev_he2[ti % 2] = he2

                hs2 = S.step("act", [he2], lambda a: a.activation(
                    sd2[:], bna2[:, 1:2], ACT.Sqrt, bias=eps_c[:, 0:1]))
                prev_hs2 = hs2

                def fin(v):
                    v.reciprocal(rstd2[:], sd2[:])
                    return v.tensor_scalar(out_t[:], y2_t[:], negm2[:, 0:1],
                                           rstd2[:, 0:1], ALU.add, ALU.mult)
                hf = S.step("dve", [hs2, prev_stout], fin)

                def stout(e, dma, ti=ti):
                    dma(out=out_d[ti * 128:(ti + 1) * 128, :], in_=out_t[:])
                h_last_out = S.step("gp", [hf], stout, n_dma=1)
                prev_stout = h_last_out

            tgt = S.cnt["gp"]

            def gp_final(e):
                e.wait_ge(sem["gp"], tgt)
            block.gpsimd(gp_final)

    return nc


_CACHE = {}


def _get_nc(dbg=False):
    key = bool(dbg)
    if key not in _CACHE:
        _CACHE[key] = build_program(dbg=dbg)
    return _CACHE[key]


def _prep_inputs(inputs):
    x = np.asarray(inputs["L0_attn_x"], np.float32)
    Wq = np.asarray(inputs["L0_attn_Wq"], np.float32)
    bq = np.asarray(inputs["L0_attn_bq"], np.float32)
    Wk = np.asarray(inputs["L0_attn_Wk"], np.float32)
    bk = np.asarray(inputs["L0_attn_bk"], np.float32)
    Wv = np.asarray(inputs["L0_attn_Wv"], np.float32)
    bv = np.asarray(inputs["L0_attn_bv"], np.float32)
    Wo = np.asarray(inputs["L0_attn_Wo"], np.float32)
    bo = np.asarray(inputs["L0_attn_bo"], np.float32)
    w1 = np.asarray(inputs["L0_ffn_w1"], np.float32)
    w2 = np.asarray(inputs["L0_ffn_w2"], np.float32)

    shared = {
        "WqT": np.ascontiguousarray(Wq.T), "WkT": np.ascontiguousarray(Wk.T),
        "WvT": np.ascontiguousarray(Wv.T), "WoT": np.ascontiguousarray(Wo.T),
        "bq_t": np.ascontiguousarray(bq.reshape(ND, 128).T),
        "bk_t": np.ascontiguousarray(bk.reshape(ND, 128).T),
        "bv_b": np.ascontiguousarray(np.broadcast_to(bv, (128, D))),
        "bk_b": np.ascontiguousarray(np.broadcast_to(bk, (128, D))),
        "bo_b": np.ascontiguousarray(np.broadcast_to(bo, (128, D))),
        "w1T": np.ascontiguousarray(w1.T), "w2T": np.ascontiguousarray(w2.T),
        "triu": np.triu(np.ones((128, 128), np.float32)),
        "ident": np.eye(128, dtype=np.float32),
    }
    in_maps = []
    for core in range(8):
        b, q = divmod(core, 4)
        seg = x[b, q * TSEG:(q + 1) * TSEG, :]
        m = dict(shared)
        m["x_seg"] = np.ascontiguousarray(seg)
        m["xT_seg"] = np.ascontiguousarray(seg.T)
        pm = (np.arange(4) < q).astype(np.float32)
        m["pmask"] = np.ascontiguousarray(np.broadcast_to(pm, (128, 4)))
        in_maps.append(m)
    return in_maps


def kernel(**inputs):
    in_maps = _prep_inputs(inputs)
    nc = _get_nc(dbg=False)
    res = run_bass_kernel_spmd(nc, in_maps, core_ids=list(range(8)))
    out = np.empty((B, T, D), np.float32)
    for core in range(8):
        b, q = divmod(core, 4)
        out[b, q * TSEG:(q + 1) * TSEG, :] = np.asarray(res.results[core]["out"])
    return out


# version 09
# speedup vs baseline: 1.1123x; 1.0561x over previous
import numpy as np
import ml_dtypes
from contextlib import ExitStack
import concourse.bass as bass
import concourse.mybir as mybir
from concourse.bass_utils import run_bass_kernel_spmd

F32 = mybir.dt.float32
BF16 = mybir.dt.bfloat16
I32 = mybir.dt.int32
AX = mybir.AxisListType
ALU = mybir.AluOpType
ACT = mybir.ActivationFunctionType

B, T, D = 2, 16384, 1024
H = 4 * D
TSEG = T // 4
C = 128
G = 4
GT = G * C
NG = TSEG // GT
ND = D // 128
NH = H // 128
NT = TSEG // C
REPLICA_GROUPS = [[0, 1, 2, 3], [4, 5, 6, 7]]

ENG_ATTR = {"pe": "tensor", "dve": "vector", "act": "scalar", "sp": "sync",
            "gp": "gpsimd", "acd": "scalar"}


class _SerialEng:

    _PASSTHRU = {"wait_ge"}

    def __init__(self, eng, sem, state):
        self._eng = eng
        self._sem = sem
        self._state = state

    def __getattr__(self, name):
        fn = getattr(self._eng, name)
        if name in self._PASSTHRU or not callable(fn):
            return fn

        def wrapped(*a, **kw):
            st = self._state
            if st["issued"] > st["waited"]:
                self._eng.wait_ge(self._sem, st["issued"])
                st["waited"] = st["issued"]
            inst = fn(*a, **kw)
            inst.then_inc(self._sem, 1)
            st["issued"] += 1
            return inst
        return wrapped


class Sched:
    ENGS = ("pe", "dve", "act", "sp", "gp", "acd")
    SERIAL = {"dve", "act"}

    def __init__(self, nc, block, es):
        self.nc = nc
        self.block = block
        self.sem = {e: es.enter_context(nc.semaphore("sem_" + e)) for e in self.ENGS}
        self.sem["cc"] = es.enter_context(nc.semaphore("sem_cc"))
        self.cnt = {e: 0 for e in self.ENGS}
        self.cnt["cc"] = 0
        self.waited = {e: {e2: 0 for e2 in (*self.ENGS, "cc")} for e in self.ENGS}
        self.serial_state = {e: {"issued": 0, "waited": 0} for e in self.SERIAL}

    def step(self, eng, deps, body, n_dma=0):
        sem = self.sem
        waited = self.waited[eng]
        serial = eng in self.SERIAL
        use = []
        for d in deps:
            if d is None:
                continue
            de, dv = d
            if de == eng:
                if serial:
                    continue
            if dv > waited[de]:
                if de == eng and not n_dma and not serial:
                    continue
                use.append((de, dv))
                waited[de] = dv
        n_body_inst = [0]
        own_wait = 0
        if n_dma:
            own_wait = self.cnt[eng]
            if own_wait <= waited[eng]:
                own_wait = 0
            else:
                waited[eng] = own_wait
            self.cnt[eng] += 16 * n_dma
        issued = [0]

        def emit(e):
            if own_wait:
                e.wait_ge(sem[eng], own_wait)
            for de, dv in use:
                e.wait_ge(sem[de], dv)
            if n_dma:
                def dma(out, in_, **kw):
                    issued[0] += 1
                    return e.dma_start(out=out, in_=in_, **kw).then_inc(sem[eng], 16)
                body(e, dma)
                assert issued[0] == n_dma, f"declared {n_dma} dmas, issued {issued[0]}"
            elif serial:
                body(_SerialEng(e, sem[eng], self.serial_state[eng]))
            else:
                last = body(e)
                assert last is not None, "compute step must return its last instruction"
                last.then_inc(sem[eng], 1)

        if serial and not n_dma:
            getattr(self.block, ENG_ATTR[eng])(emit)
            self.cnt[eng] = self.serial_state[eng]["issued"]
            return (eng, self.cnt[eng])
        if not n_dma:
            self.cnt[eng] += 1
        getattr(self.block, ENG_ATTR[eng])(emit)
        return (eng, self.cnt[eng])

    def all_done(self):
        return [(e, self.cnt[e]) for e in self.ENGS]


def _finish(S, block):
    sem = S.sem
    cnts = {e: S.cnt[e] for e in S.ENGS}

    def gp_final(e):
        for e2 in S.ENGS:
            if cnts[e2]:
                e.wait_ge(sem[e2], cnts[e2])
    block.gpsimd(gp_final)


def mm_acc(pe, out, pairs):
    n = len(pairs)
    last = None
    for i, (l, r) in enumerate(pairs):
        last = pe.matmul(out, l, r, start=(i == 0), stop=(i == n - 1))
    return last


def hl3(lh, ll, rh, rl):
    return [(lh, rh), (lh, rl), (ll, rh)]


class _StopEmit(Exception):
    pass


def build_program(dbg=False, upto='p3b'):
    p2sub = upto.split('@', 1)[1] if upto.startswith('p2@') else None
    if p2sub:
        upto = 'p2'
    nc = bass.Bass("TRN2", target_bir_lowering=False)

    def dram_in(name, shape, dt=F32):
        return nc.dram_tensor(name, shape, dt, kind="ExternalInput").ap()

    def dram_scratch(name, shape, dt=F32):
        if dbg:
            return nc.dram_tensor(name, shape, dt, kind="ExternalOutput").ap()
        return nc.dram_tensor(name, shape, dt).ap()

    x_d = dram_in("x_seg", [TSEG, D])
    xT_d = dram_in("xT_seg", [D, TSEG])
    WT_d = {w: dram_in(w, [D, D]) for w in ("WqT", "WkT", "WvT", "WoT")}
    bq_d = dram_in("bq_t", [128, ND]); bk_d = dram_in("bk_t", [128, ND])
    bvb_d = dram_in("bv_b", [128, D]); bob_d = dram_in("bo_b", [128, D])
    bkb_d = dram_in("bk_b", [128, D])
    w1T_d = dram_in("w1T", [D, H]); w2T_d = dram_in("w2T", [H, D])
    triu_d = dram_in("triu", [128, 128])
    ident_d = dram_in("ident", [128, 128])
    pmask_d = dram_in("pmask", [128, 4])
    out_d = nc.dram_tensor("out", [TSEG, D], F32, kind="ExternalOutput").ap()

    spill_k = dram_scratch("spill_k", [NG, 2, 128, G * D], BF16)
    spill_v = dram_scratch("spill_v", [NG, 2, 128, G * D], BF16)
    spill_q = dram_scratch("spill_q", [NG, 2, 128, ND * GT], BF16)
    prefix_s = dram_scratch("prefix_s", [NG, 128, ND * D])
    prefix_z = dram_scratch("prefix_z", [NG, 128, ND])
    y1_dram = dram_scratch("y1_spill", [TSEG, D])
    hq_dram = dram_scratch("hq_sp", [NT, 128, H], BF16)
    w1q_dram = dram_scratch("w1q", [128, ND * H], BF16)
    w2q_dram = dram_scratch("w2q", [128, NH * D], BF16)
    sstart_d = dram_scratch("sstart", [128, ND * D])
    cc_in = nc.dram_tensor("cc_in", [D + 1, D], F32).ap()
    cc_out = nc.dram_tensor("cc_out", [4 * (D + 1), D], F32).ap()

    es = ExitStack()
    with es:
        block = es.enter_context(nc.Block())
        S = Sched(nc, block, es)
        sem = S.sem

        def SB(scope, name, shape, dt=F32):
            return scope.enter_context(nc.sbuf_tensor(name, shape, dt))

        def PS(scope, name, shape, dt=F32):
            return scope.enter_context(nc.psum_tensor(name, shape, dt))

        ones_bf = SB(es, "ones_bf", [128, 128], BF16)
        ident_f = SB(es, "ident_f", [128, 128], F32)
        ident_b = SB(es, "ident_b", [128, 128], BF16)
        triu_f = SB(es, "triu_f", [128, 128], F32)
        s1bc = SB(es, "s1bc", [128, 1]); s2bc = SB(es, "s2bc", [128, 1])
        thr1 = SB(es, "thr1", [128, 1]); nthr1 = SB(es, "nthr1", [128, 1])
        thr2 = SB(es, "thr2", [128, 1]); nthr2 = SB(es, "nthr2", [128, 1])
        bq_s = SB(es, "bq_s", [128, ND]); bk_s = SB(es, "bk_s", [128, ND])
        bvb_s = SB(es, "bvb_s", [128, D]); bob_s = SB(es, "bob_s", [128, D])
        bkb_s = SB(es, "bkb_s", [128, D])
        pmask_s = SB(es, "pmask_s", [128, 4])
        z_acc = SB(es, "z_acc", [128, ND]); z_start = SB(es, "z_start", [128, ND])
        zg = SB(es, "zg", [128, ND])
        zg_h = SB(es, "zg_h", [128, ND], BF16); zg_l = SB(es, "zg_l", [128, ND], BF16)
        f2all = SB(es, "f2all", [128, NT])
        eps_c = SB(es, "eps_c", [128, 1])
        s_scope = ExitStack()
        Smaster = SB(s_scope, "Smaster", [128, ND * D])

        def ld_const(e, dma):
            dma(out=ident_f[:], in_=ident_d[:])
            dma(out=triu_f[:], in_=triu_d[:])
            dma(out=bq_s[:], in_=bq_d[:])
            dma(out=bk_s[:], in_=bk_d[:])
            dma(out=bvb_s[:], in_=bvb_d[:])
            dma(out=bob_s[:], in_=bob_d[:])
            dma(out=bkb_s[:], in_=bkb_d[:])
            dma(out=pmask_s[:], in_=pmask_d[:])
        h_const = S.step("sp", [], ld_const, n_dma=8)

        def init_body(v):
            v.memset(ones_bf[:], 1.0)
            v.memset(eps_c[:], 1e-5)
            v.memset(z_acc[:], 0.0)
            v.memset(Smaster[:], 0.0)
            return v.tensor_copy(ident_b[:], ident_f[:])
        h_init = S.step("dve", [h_const], init_body)

        wkv_scope = ExitStack()
        wk_h = SB(wkv_scope, "wk_h", [128, ND * D], BF16)
        wk_l = SB(wkv_scope, "wk_l", [128, ND * D], BF16)
        wv_h = SB(wkv_scope, "wv_h", [128, ND * D], BF16)
        wv_l = SB(wkv_scope, "wv_l", [128, ND * D], BF16)
        wsh_scope = ExitStack()
        wstage_half = SB(wsh_scope, "wstage_half", [128, ND * D // 2])
        hws = [h_init]
        for (wd, wh, wl) in [(WT_d["WkT"], wk_h, wk_l), (WT_d["WvT"], wv_h, wv_l)]:
            for hf in range(2):
                def ldw(e, dma, wd=wd, hf=hf):
                    dma(out=wstage_half[:].rearrange("p (dt o) -> p dt o", dt=ND // 2),
                        in_=wd[hf * 512:(hf + 1) * 512, :]
                        .rearrange("(dt p) o -> p dt o", p=128))
                hlw = S.step("acd", hws, ldw, n_dma=1)

                def splw_h(a, wh=wh, hf=hf):
                    return a.copy(wh[:, hf * (ND // 2) * D:(hf + 1) * (ND // 2) * D],
                                  wstage_half[:])
                hwa = S.step("act", [hlw], splw_h)

                def splw_l(v, wh=wh, wl=wl, hf=hf):
                    return v.tensor_sub(wl[:, hf * (ND // 2) * D:(hf + 1) * (ND // 2) * D],
                                        wstage_half[:],
                                        wh[:, hf * (ND // 2) * D:(hf + 1) * (ND // 2) * D])
                hwl = S.step("dve", [hwa], splw_l)
                hws = [hwa, hwl]
        h_wkv = hws[1]
        wsh_scope.close()

        p0 = ExitStack()
        with p0:
            wpiece = SB(p0, "wpiece", [128, H])
            wpiece2 = SB(p0, "wpiece2", [128, H])
            tq2_ = SB(p0, "tq2_", [128, H], BF16)
            wsum = SB(p0, "wsum", [128, 1]); wtmp1 = SB(p0, "wtmp1", [128, 1])
            sh_ = SB(p0, "sh_", [128, 1], BF16); sl_ = SB(p0, "sl_", [128, 1], BF16)
            sc_row = SB(p0, "sc_row", [1, 2])
            sc_h = SB(p0, "sc_h", [1, 2], BF16); sc_l = SB(p0, "sc_l", [1, 2], BF16)
            cge = SB(p0, "cge", [128, H], BF16)
            tq_ = SB(p0, "tq_", [128, H], BF16)
            ps_s = PS(p0, "ps_s", [1, 2])
            ps_b = PS(p0, "ps_b", [128, 2])

            prev_wi_spill = None
            for wi, (wT_dd, npieces, pw, wq_dram, sbc, thr, nthr) in enumerate([
                    (w1T_d, ND, H, w1q_dram, s1bc, thr1, nthr1),
                    (w2T_d, NH, D, w2q_dram, s2bc, thr2, nthr2)]):
                hz = S.step("dve", [h_init], lambda v: v.memset(wsum[:], 0.0))
                hrs = [hz, hz]
                for pi in range(npieces):
                    buf = wpiece if pi % 2 == 0 else wpiece2

                    def ldp(e, dma, pi=pi, wT_dd=wT_dd, pw=pw, buf=buf):
                        dma(out=buf[:, :pw], in_=wT_dd[pi * 128:(pi + 1) * 128, :])
                    hl = S.step("sp", [hrs[pi % 2], h_wkv], ldp, n_dma=1)

                    def red(v, pw=pw, buf=buf):
                        v.reduce_sum(wtmp1[:], buf[:, :pw], axis=AX.X,
                                     apply_absolute_value=True)
                        return v.tensor_add(wsum[:], wsum[:], wtmp1[:])
                    hrs[pi % 2] = S.step("dve", [hl], red)
                hr = hrs[(npieces - 1) % 2]

                def spl(v):
                    v.tensor_copy(sh_[:], wsum[:])
                    return v.tensor_sub(sl_[:], wsum[:], sh_[:])
                hsp = S.step("dve", [hr], spl)
                hmm = S.step("pe", [hsp], lambda p, wi=wi: mm_acc(
                    p, ps_s[0:1, wi:wi + 1],
                    [(sh_[:], ones_bf[:, 0:1]), (sl_[:], ones_bf[:, 0:1])]))
                nelem = float(npieces * 128 * pw)

                def mean(v, wi=wi, nelem=nelem):
                    v.tensor_scalar(sc_row[0:1, wi:wi + 1], ps_s[0:1, wi:wi + 1],
                                    1.0 / nelem, 1e-5, ALU.mult, ALU.max)
                    v.tensor_copy(sc_h[0:1, wi:wi + 1], sc_row[0:1, wi:wi + 1])
                    return v.tensor_sub(sc_l[0:1, wi:wi + 1], sc_row[0:1, wi:wi + 1],
                                        sc_h[0:1, wi:wi + 1])
                hmean = S.step("dve", [hmm], mean)
                hbc = S.step("pe", [hmean], lambda p, wi=wi: mm_acc(
                    p, ps_b[:, wi:wi + 1],
                    [(ones_bf[0:1, :], sc_h[0:1, wi:wi + 1]),
                     (ones_bf[0:1, :], sc_l[0:1, wi:wi + 1])]))

                def mkthr(v, sbc=sbc, thr=thr, nthr=nthr, wi=wi):
                    v.tensor_copy(sbc[:], ps_b[:, wi:wi + 1])
                    v.tensor_scalar(thr[:], sbc[:], 0.5, None, ALU.mult)
                    return v.tensor_scalar(nthr[:], sbc[:], -0.5, None, ALU.mult)
                hthr = S.step("dve", [hbc], mkthr)

                hcs = [hthr, hthr]
                hqs = [prev_wi_spill, prev_wi_spill]
                hq = hthr
                for pi in range(npieces):
                    buf = wpiece if pi % 2 == 0 else wpiece2
                    tqb = tq_ if pi % 2 == 0 else tq2_

                    def ldp2(e, dma, pi=pi, wT_dd=wT_dd, pw=pw, buf=buf):
                        dma(out=buf[:, :pw], in_=wT_dd[pi * 128:(pi + 1) * 128, :])
                    hl = S.step("sp", [hcs[pi % 2], h_wkv,
                                 prev_wi_spill if pi < 2 else None],
                                ldp2, n_dma=1)

                    def qz(v, pw=pw, thr=thr, nthr=nthr, buf=buf, tqb=tqb):
                        v.tensor_scalar(cge[:, :pw], buf[:, :pw], nthr[:, 0:1], None,
                                        ALU.is_le)
                        return v.scalar_tensor_tensor(
                            tqb[:, :pw], buf[:, :pw], thr[:, 0:1], cge[:, :pw],
                            ALU.is_ge, ALU.subtract)
                    hcmp = S.step("dve", [hl, hqs[pi % 2]], qz)
                    hcs[pi % 2] = hcmp

                    def stq(e, dma, pi=pi, wq_dram=wq_dram, pw=pw, tqb=tqb):
                        dma(out=wq_dram[:, pi * pw:(pi + 1) * pw], in_=tqb[:, :pw])
                    hq = S.step("gp", [hcmp], stq, n_dma=1)
                    hqs[pi % 2] = hq
                prev_wi_spill = hq
            h_p0_done = S.all_done()
        if upto == 'p0':
            _finish(S, block); s_scope.close(); return nc

        p1 = ExitStack()
        with p1:
            xT_st = SB(p1, "xT_st", [128, ND * GT])
            xh = SB(p1, "xh", [128, ND * GT], BF16); xl = SB(p1, "xl", [128, ND * GT], BF16)
            kphi_e = SB(p1, "kphi_e", [128, D]); kphi_r = SB(p1, "kphi_r", [128, D])
            kpT_f = SB(p1, "kpT_f", [128, D])
            kT_h = SB(p1, "kT_h", [128, G * D], BF16); kT_l = SB(p1, "kT_l", [128, G * D], BF16)
            vT_h = SB(p1, "vT_h", [128, G * D], BF16); vT_l = SB(p1, "vT_l", [128, G * D], BF16)
            bkr_h = SB(p1, "bkr_h", [1, D], BF16); bkr_l = SB(p1, "bkr_l", [1, D], BF16)
            bvr_h = SB(p1, "bvr_h", [1, D], BF16); bvr_l = SB(p1, "bvr_l", [1, D], BF16)
            pk = PS(p1, "pk", [128, 2 * D])
            pv = PS(p1, "pv", [128, D])
            pkv = PS(p1, "pkv", [128, 2 * 512])

            def mk_rows_h(a):
                a.copy(bkr_h[:], bkb_s[0:1, :])
                return a.copy(bvr_h[:], bvb_s[0:1, :])
            hra = S.step("act", [h_const], mk_rows_h)

            def mk_rows_l(v):
                v.tensor_sub(bkr_l[:], bkb_s[0:1, :], bkr_h[:])
                return v.tensor_sub(bvr_l[:], bvb_s[0:1, :], bvr_h[:])
            hrl = S.step("dve", [hra], mk_rows_l)

            prev_split = h_wkv
            prev_x_pe = None
            prev_kv_done = h_init
            prev_spk = None; prev_spv = None
            prev_kv_pe = None
            for g in range(NG):
                def ldx(e, dma, g=g):
                    dma(out=xT_st[:].rearrange("p (dt t) -> p dt t", dt=ND),
                        in_=xT_d[:, g * GT:(g + 1) * GT]
                        .rearrange("(dt p) t -> p dt t", p=128))
                hx = S.step("sp", [prev_split] + (h_p0_done if g == 0 else []),
                            ldx, n_dma=1)

                hxa = S.step("act", [hx, prev_x_pe], lambda a: a.copy(xh[:], xT_st[:]))
                hxs = S.step("dve", [hxa], lambda v: v.tensor_sub(xl[:], xT_st[:], xh[:]))

                def spz(e, dma, g=g):
                    dma(out=prefix_z[g], in_=z_acc[:])
                    dma(out=prefix_s[g], in_=Smaster[:])
                hspill = S.step("gp", [prev_kv_done], spz, n_dma=2)

                hk_stt = None
                hk_hi = None
                hk_lo = None
                hz_mm = None
                zdeps = []
                for tt in range(G):
                    def mmk(p, tt=tt):
                        slot = tt % 2
                        last = None
                        for eh in range(2):
                            pairs = []
                            for dk in range(ND):
                                lh = xh[:, dk * GT + tt * 128: dk * GT + tt * 128 + 128]
                                ll = xl[:, dk * GT + tt * 128: dk * GT + tt * 128 + 128]
                                rh = wk_h[:, dk * D + eh * 512: dk * D + eh * 512 + 512]
                                rl = wk_l[:, dk * D + eh * 512: dk * D + eh * 512 + 512]
                                pairs += hl3(lh, ll, rh, rl)
                            pairs.append((ones_bf[0:1, :], bkr_h[0:1, eh * 512:(eh + 1) * 512]))
                            pairs.append((ones_bf[0:1, :], bkr_l[0:1, eh * 512:(eh + 1) * 512]))
                            last = mm_acc(p, pk[:, slot * D + eh * 512: slot * D + eh * 512 + 512],
                                          pairs)
                        return last
                    dep = [hxs, h_wkv, hrl]
                    if tt >= 2:
                        dep.append(hk_evac2[tt - 2])
                    hmk = S.step("pe", dep, mmk)

                    def ek_e(a, tt=tt):
                        return a.activation(kphi_e[:], pk[:, (tt % 2) * D:(tt % 2 + 1) * D],
                                            ACT.Exp)
                    dep = [hmk] + ([hk_stt] if hk_stt else [])
                    hea = S.step("act", dep, ek_e)

                    def ek_r(a, tt=tt):
                        return a.activation(kphi_r[:], pk[:, (tt % 2) * D:(tt % 2 + 1) * D],
                                            ACT.Relu)
                    her = S.step("act", [hea], ek_r)

                    def ek_stt(v, tt=tt):
                        return v.scalar_tensor_tensor(kpT_f[:], kphi_e[:], 1.0, kphi_r[:],
                                                      ALU.min, ALU.add)
                    hk_stt = S.step("dve", [her] + ([hk_hi] if hk_hi else []), ek_stt)

                    def ek_hi(a, tt=tt):
                        return a.copy(kT_h[:, tt * D:(tt + 1) * D], kpT_f[:])
                    hk_hi = S.step("act", [hk_stt, prev_spk, prev_kv_pe], ek_hi)

                    def ek_lo(v, tt=tt):
                        return v.tensor_sub(kT_l[:, tt * D:(tt + 1) * D], kpT_f[:],
                                            kT_h[:, tt * D:(tt + 1) * D])
                    hk_lo = S.step("dve", [hk_hi], ek_lo)
                    if tt == 0:
                        hk_evac2 = {}
                    hk_evac2[tt] = hk_lo

                    zdeps.append((hk_lo, hk_hi))

                def spk(e, dma, g=g):
                    dma(out=spill_k[g, 0], in_=kT_h[:])
                    dma(out=spill_k[g, 1], in_=kT_l[:])
                hspk = S.step("gp", [hk_lo, hk_hi], spk, n_dma=2)

                hv_hi = None
                hv_lo = None
                for tt in range(G):
                    def mmv(p, tt=tt):
                        last = None
                        for eh in range(2):
                            pairs = []
                            for dk in range(ND):
                                lh = xh[:, dk * GT + tt * 128: dk * GT + tt * 128 + 128]
                                ll = xl[:, dk * GT + tt * 128: dk * GT + tt * 128 + 128]
                                rh = wv_h[:, dk * D + eh * 512: dk * D + eh * 512 + 512]
                                rl = wv_l[:, dk * D + eh * 512: dk * D + eh * 512 + 512]
                                pairs += hl3(lh, ll, rh, rl)
                            pairs.append((ones_bf[0:1, :], bvr_h[0:1, eh * 512:(eh + 1) * 512]))
                            pairs.append((ones_bf[0:1, :], bvr_l[0:1, eh * 512:(eh + 1) * 512]))
                            last = mm_acc(p, pv[:, eh * 512:(eh + 1) * 512], pairs)
                        return last
                    dep = [hxs, h_wkv, hrl]
                    if tt >= 1:
                        dep.append(hv_lo)
                    hmv = S.step("pe", dep, mmv)

                    def zmm(p, tt=tt):
                        last = None
                        for dt in range(ND):
                            for j, pl in enumerate((kT_h, kT_l)):
                                st = (tt == 0 and dt == 0 and j == 0)
                                sp_ = (tt == G - 1 and dt == ND - 1 and j == 1)
                                last = p.matmul(pkv[:, dt:dt + 1],
                                                pl[:, tt * D + dt * 128: tt * D + dt * 128 + 128],
                                                ones_bf[:, 0:1], start=st, stop=sp_)
                        return last
                    zdep = list(zdeps[tt])
                    if tt == 0:
                        zdep.append(prev_kv_done)
                    hz_mm = S.step("pe", zdep, zmm)

                    def ev_hi(a, tt=tt):
                        return a.copy(vT_h[:, tt * D:(tt + 1) * D], pv[:])
                    hv_hi = S.step("act", [hmv, prev_spv, prev_kv_pe], ev_hi)

                    def ev_lo(v, tt=tt):
                        return v.tensor_sub(vT_l[:, tt * D:(tt + 1) * D], pv[:],
                                            vT_h[:, tt * D:(tt + 1) * D])
                    hv_lo = S.step("dve", [hmv, hv_hi], ev_lo)

                def spv(e, dma, g=g):
                    dma(out=spill_v[g, 0], in_=vT_h[:])
                    dma(out=spill_v[g, 1], in_=vT_l[:])
                hspv = S.step("gp", [hv_lo, hv_hi], spv, n_dma=2)

                def evz(v):
                    return v.tensor_add(z_acc[:], z_acc[:], pkv[:, 0:ND])
                hz_ev = S.step("dve", [hz_mm, hspill], evz)

                hkv_ev = []
                hm_last = None
                for dt in range(ND):
                    for eh in range(2):
                        idx = dt * 2 + eh

                        def mmkv(p, dt=dt, eh=eh, idx=idx):
                            pairs = []
                            for tc in range(G):
                                lh = kT_h[:, tc * D + dt * 128: tc * D + dt * 128 + 128]
                                ll = kT_l[:, tc * D + dt * 128: tc * D + dt * 128 + 128]
                                rh = vT_h[:, tc * D + eh * 512: tc * D + eh * 512 + 512]
                                rl = vT_l[:, tc * D + eh * 512: tc * D + eh * 512 + 512]
                                pairs += hl3(lh, ll, rh, rl)
                            slot = idx % 2
                            return mm_acc(p, pkv[:, slot * 512:(slot + 1) * 512], pairs)
                        dep = [hk_lo, hk_hi, hv_lo, hv_hi, hz_ev]
                        if idx >= 2:
                            dep.append(hkv_ev[idx - 2])
                        hm_last = S.step("pe", dep, mmkv)

                        def evkv(v, dt=dt, eh=eh, idx=idx):
                            slot = idx % 2
                            dst = Smaster[:, dt * D + eh * 512: dt * D + eh * 512 + 512]
                            return v.tensor_add(dst, dst, pkv[:, slot * 512:(slot + 1) * 512])
                        hkv_ev.append(S.step("dve", [hm_last, hspill], evkv))

                prev_kv_done = hkv_ev[-1]
                prev_kv_pe = hm_last
                prev_x_pe = hm_last
                prev_split = hxs
                prev_spk = hspk; prev_spv = hspv

            h_p1a_done = S.all_done()

        wkv_scope.close()
        if upto == 'p1a':
            _finish(S, block); s_scope.close(); return nc

        def wr_cc(e, dma):
            for dt in range(ND):
                dma(out=cc_in[dt * 128:(dt + 1) * 128, :],
                    in_=Smaster[:, dt * D:(dt + 1) * D])
            for dt in range(ND):
                dma(out=cc_in[D:D + 1, dt * 128:(dt + 1) * 128],
                    in_=z_acc[:, dt:dt + 1])
        h_wrcc = S.step("gp", h_p1a_done, wr_cc, n_dma=16)

        _wr_val = h_wrcc[1]

        def _cc_emit(e):
            e.wait_ge(S.sem["gp"], _wr_val)
            e.collective_compute(
                "AllGather", ALU.bypass, replica_groups=REPLICA_GROUPS,
                ins=[cc_in[:]], outs=[cc_out[:]]).then_inc(S.sem["cc"], 1)
        block.gpsimd(_cc_emit)
        S.cnt["cc"] = 1
        h_cc = ("cc", 1)

        p1b = ExitStack()
        with p1b:
            wq_h = SB(p1b, "wq_h", [128, ND * D], BF16); wq_l = SB(p1b, "wq_l", [128, ND * D], BF16)
            ws2_scope = ExitStack()
            wstage2 = SB(ws2_scope, "wstage2", [128, ND * D])

            def ldwq(e, dma):
                dma(out=wstage2[:].rearrange("p (dt o) -> p dt o", dt=ND),
                    in_=WT_d["WqT"].rearrange("(dt p) o -> p dt o", p=128))
            hlq = S.step("sp", h_p1a_done, ldwq, n_dma=1)

            def splwq(v):
                v.tensor_copy(wq_h[:], wstage2[:])
                return v.tensor_sub(wq_l[:], wstage2[:], wq_h[:])
            hwq = S.step("dve", [hlq], splwq)
            ws2_scope.close()

            xT_st2 = SB(p1b, "xT_st2", [128, ND * GT])
            xh2 = SB(p1b, "xh2", [128, ND * GT], BF16); xl2 = SB(p1b, "xl2", [128, ND * GT], BF16)
            qD_f = SB(p1b, "qD_f", [128, ND * GT])
            q_h = SB(p1b, "q_h", [128, ND * GT], BF16); q_l = SB(p1b, "q_l", [128, ND * GT], BF16)
            ephi2 = SB(p1b, "ephi2", [128, GT]); rphi2 = SB(p1b, "rphi2", [128, GT])
            pq = PS(p1b, "pq", [128, 2 * GT])
            px = ExitStack()
            Ttmp = SB(px, "Ttmp", [128, ND * D])
            ztmp8 = SB(px, "ztmp8", [128, ND])
            hacc_x2 = None
            h_spss = None

            prev_split = hwq
            prev_x_pe = None
            prev_spq = None
            for g in range(NG):
                def ldx2(e, dma, g=g):
                    dma(out=xT_st2[:].rearrange("p (dt t) -> p dt t", dt=ND),
                        in_=xT_d[:, g * GT:(g + 1) * GT]
                        .rearrange("(dt p) t -> p dt t", p=128))
                hx = S.step("sp", [prev_split], ldx2, n_dma=1)

                def splx2(v):
                    v.tensor_copy(xh2[:], xT_st2[:])
                    return v.tensor_sub(xl2[:], xT_st2[:], xh2[:])
                hxs = S.step("dve", [hx, hwq, prev_x_pe], splx2)

                hq_evac = []
                hmq = None
                for ot in range(ND):
                    def mmq(p, ot=ot):
                        pairs = []
                        for dk in range(ND):
                            lh = wq_h[:, dk * D + ot * 128: dk * D + ot * 128 + 128]
                            ll = wq_l[:, dk * D + ot * 128: dk * D + ot * 128 + 128]
                            rh = xh2[:, dk * GT:(dk + 1) * GT]
                            rl = xl2[:, dk * GT:(dk + 1) * GT]
                            pairs += hl3(lh, ll, rh, rl)
                        return mm_acc(p, pq[:, (ot % 2) * GT:(ot % 2 + 1) * GT], pairs)
                    dep = [hxs] + ([hq_evac[ot - 2][1]] if ot >= 2 else [])
                    hmq = S.step("pe", dep, mmq)

                    def eq_act(a, ot=ot):
                        a.activation(ephi2[:], pq[:, (ot % 2) * GT:(ot % 2 + 1) * GT],
                                     ACT.Exp, bias=bq_s[:, ot:ot + 1])
                        return a.activation(rphi2[:], pq[:, (ot % 2) * GT:(ot % 2 + 1) * GT],
                                            ACT.Relu, bias=bq_s[:, ot:ot + 1])
                    heq = S.step("act", [hmq] + ([hq_evac[ot - 1][1]] if ot >= 1 else []),
                                 eq_act)

                    def eq_stt(v, ot=ot):
                        return v.scalar_tensor_tensor(qD_f[:, ot * GT:(ot + 1) * GT],
                                                      ephi2[:], 1.0, rphi2[:],
                                                      ALU.min, ALU.add)
                    hst = S.step("dve", [heq, prev_spq, prev_x_pe], eq_stt)

                    def eq_hi(a, ot=ot):
                        return a.copy(q_h[:, ot * GT:(ot + 1) * GT],
                                      qD_f[:, ot * GT:(ot + 1) * GT])
                    hhi = S.step("act", [hst], eq_hi)

                    def eq_lo(v, ot=ot):
                        return v.tensor_sub(q_l[:, ot * GT:(ot + 1) * GT],
                                            qD_f[:, ot * GT:(ot + 1) * GT],
                                            q_h[:, ot * GT:(ot + 1) * GT])
                    hev = S.step("dve", [hhi], eq_lo)
                    hq_evac.append((hmq, hev))

                def spq(e, dma, g=g):
                    dma(out=spill_q[g, 0], in_=q_h[:])
                    dma(out=spill_q[g, 1], in_=q_l[:])
                prev_spq = S.step("gp", [hq_evac[-1][1]], spq, n_dma=2)
                prev_split = hxs
                prev_x_pe = hmq

                if 3 <= g <= 6:
                    r = g - 3

                    def ldT(e, dma, r=r):
                        base = r * (D + 1)
                        dma(out=Ttmp[:].rearrange("p (dt e) -> p dt e", dt=ND),
                            in_=cc_out[base:base + D, :]
                            .rearrange("(dt p) e -> p dt e", p=128))
                        dma(out=ztmp8[:],
                            in_=cc_out[base + D:base + D + 1, :]
                            .rearrange("r (dt p) -> p (r dt)", p=128, r=1),
                            allow_slow_non_contiguous=True)
                    hl = S.step("acd", [h_cc] + ([hacc_x2] if hacc_x2 else []),
                                ldT, n_dma=2)

                    def accT(v, r=r):
                        if r == 0:
                            v.tensor_scalar(Smaster[:], Ttmp[:], pmask_s[:, 0:1],
                                            None, ALU.mult)
                            return v.tensor_scalar(z_start[:], ztmp8[:],
                                                   pmask_s[:, 0:1], None, ALU.mult)
                        v.scalar_tensor_tensor(Smaster[:], Ttmp[:], pmask_s[:, r:r + 1],
                                               Smaster[:], ALU.mult, ALU.add)
                        return v.scalar_tensor_tensor(z_start[:], ztmp8[:],
                                                      pmask_s[:, r:r + 1],
                                                      z_start[:], ALU.mult, ALU.add)
                    hacc_x2 = S.step("dve", [hl], accT)
                    if r == 3:
                        def spss(e, dma):
                            dma(out=sstart_d[:], in_=Smaster[:])
                        h_spss = S.step("acd", [hacc_x2], spss, n_dma=1)

            h_p1b_done = S.all_done()
            px.close()

        if upto == 'p1b':
            _finish(S, block); s_scope.close(); return nc

        h_x2_done = h_p1b_done
        s_scope.close()
        if upto == 'x2':
            _finish(S, block); return nc

        p2 = ExitStack()
        with p2:
            wo_h = SB(p2, "wo_h", [128, ND * D], BF16); wo_l = SB(p2, "wo_l", [128, ND * D], BF16)
            ws3_scope = ExitStack()
            wstage3 = SB(ws3_scope, "wstage3", [128, ND * D])

            def ldwo(e, dma):
                dma(out=wstage3[:].rearrange("p (dt o) -> p dt o", dt=ND),
                    in_=WT_d["WoT"].rearrange("(dt p) o -> p dt o", p=128))
            hlo = S.step("sp", h_x2_done, ldwo, n_dma=1)

            def splwo(v):
                v.tensor_copy(wo_h[:], wstage3[:])
                return v.tensor_sub(wo_l[:], wstage3[:], wo_h[:])
            hwo = S.step("dve", [hlo], splwo)
            ws3_scope.close()
            if p2sub == 'w':
                _finish(S, block)
                return nc

            Sgrp = SB(p2, "Sgrp", [128, ND * D])
            s_h = SB(p2, "s_h", [128, ND * D], BF16); s_l = SB(p2, "s_l", [128, ND * D], BF16)
            q2_h = SB(p2, "q2_h", [128, ND * GT], BF16); q2_l = SB(p2, "q2_l", [128, ND * GT], BF16)
            k2_h = SB(p2, "k2_h", [128, G * D], BF16); k2_l = SB(p2, "k2_l", [128, G * D], BF16)
            v2_h = SB(p2, "v2_h", [128, G * D], BF16); v2_l = SB(p2, "v2_l", [128, G * D], BF16)
            kd_h = SB(p2, "kd_h", [128, ND * 128], BF16); kd_l = SB(p2, "kd_l", [128, ND * 128], BF16)
            aT_h = SB(p2, "aT_h", [128, G * GT], BF16); aT_l = SB(p2, "aT_l", [128, G * GT], BF16)
            n_h = SB(p2, "n_h", [128, ND * GT], BF16); n_l = SB(p2, "n_l", [128, ND * GT], BF16)
            atmp = SB(p2, "atmp", [128, 128])
            dcol = SB(p2, "dcol", [128, G])
            recip_den = SB(p2, "recip_den", [128, G])
            x_res = SB(p2, "x_res", [128, D])
            y_t = SB(p2, "y_t", [128, D])
            y1_t = SB(p2, "y1_t", [128, D])
            zgtmp = SB(p2, "zgtmp", [128, ND])
            bnb = SB(p2, "bnb", [128, 12]); bna = SB(p2, "bna", [128, 2])
            sd_c = SB(p2, "sd_c", [128, 1]); rstd_c = SB(p2, "rstd_c", [128, 1])
            negm_c = SB(p2, "negm_c", [128, 1])
            pA = PS(p2, "pA", [128, 512])
            pden = PS(p2, "pden", [128, G])
            pnum2 = PS(p2, "pnum2", [128, 2 * 512])
            pwo = PS(p2, "pwo", [128, D])
            ptrbA = PS(p2, "ptrbA", [128, 512], BF16)
            ptrbB = PS(p2, "ptrbB", [128, 512], BF16)

            h_prev_group_end = None
            prev_hey = None
            prev_hsd = None
            prev_sty1 = None
            prev_hxr_reader = None
            prev_trev = None
            prev_mn_pe = None
            h_ldsz = None
            h_accS = None
            for g in range(NG if not p2sub else 1):
              try:
                  if h_ldsz is None:
                      def ldsz0(e, dma, g=g):
                          dma(out=Sgrp[:], in_=prefix_s[g])
                          dma(out=zgtmp[:], in_=prefix_z[g])
                      h_ldsz = S.step("sp", h_x2_done + [hwo], ldsz0, n_dma=2)

                      def accS0(e, dma):
                          for i in range(4):
                              dma(out=Sgrp[:, i * 2048:(i + 1) * 2048],
                                  in_=sstart_d[:, i * 2048:(i + 1) * 2048],
                                  accum_op=ALU.add)
                      h_accS = S.step("gp", [h_ldsz], accS0, n_dma=4)

                  def ldkqv(e, dma, g=g):
                      dma(out=q2_h[:], in_=spill_q[g, 0]); dma(out=q2_l[:], in_=spill_q[g, 1])
                      dma(out=k2_h[:], in_=spill_k[g, 0]); dma(out=k2_l[:], in_=spill_k[g, 1])
                      dma(out=v2_h[:], in_=spill_v[g, 0]); dma(out=v2_l[:], in_=spill_v[g, 1])
                  hld = S.step("sp", [prev_mn_pe], ldkqv, n_dma=6)
                  if p2sub == 'ld0':
                      raise _StopEmit

                  hz0 = S.step("dve", [h_ldsz, prev_mn_pe], lambda v: v.tensor_add(
                      zg[:], zgtmp[:], z_start[:]))

                  def mks_hi(a):
                      a.copy(s_h[:], Sgrp[:])
                      return a.copy(zg_h[:], zg[:])
                  hSa = S.step("act", [h_ldsz, h_accS, hz0, prev_mn_pe], mks_hi)

                  def mks_lo(v):
                      v.tensor_sub(s_l[:], Sgrp[:], s_h[:])
                      return v.tensor_sub(zg_l[:], zg[:], zg_h[:])
                  hS = S.step("dve", [hSa], mks_lo)
                  if p2sub == 'ld':
                      raise _StopEmit

                  if g + 1 < NG:
                      def ldsz(e, dma, g=g):
                          dma(out=Sgrp[:], in_=prefix_s[g + 1])
                          dma(out=zgtmp[:], in_=prefix_z[g + 1])
                      h_ldsz = S.step("sp", [hS], ldsz, n_dma=2)

                      def accSn(e, dma):
                          for i in range(4):
                              dma(out=Sgrp[:, i * 2048:(i + 1) * 2048],
                                  in_=sstart_d[:, i * 2048:(i + 1) * 2048],
                                  accum_op=ALU.add)
                      h_accS = S.step("gp", [h_ldsz], accSn, n_dma=4)

                  hA_ev = []
                  for st_ in range(G):
                      htr_ev = None
                      for (ksrc, kdst) in ((k2_h, kd_h), (k2_l, kd_l)):
                          for half in range(2):
                              ptrb = ptrbA if half == 0 else ptrbB

                              def trk2(p, st_=st_, half=half, ksrc=ksrc, ptrb=ptrb):
                                  last = None
                                  for j in range(4):
                                      dk = half * 4 + j
                                      last = p.transpose(
                                          ptrb[:, j * 128:(j + 1) * 128],
                                          ksrc[:, st_ * D + dk * 128: st_ * D + dk * 128 + 128],
                                          ident_b[:])
                                  return last
                              ht = S.step("pe", [hld, prev_trev], trk2)

                              def evt(a, half=half, kdst=kdst, ptrb=ptrb):
                                  return a.copy(kdst[:, half * 512:(half + 1) * 512],
                                                ptrb[:])
                              prev_trev = S.step("act", [ht], evt)
                          htr_ev = prev_trev

                      ncols = GT - st_ * 128

                      def mma(p, st_=st_, ncols=ncols):
                          pairs = []
                          for dk in range(ND):
                              lh = kd_h[:, dk * 128:(dk + 1) * 128]
                              ll = kd_l[:, dk * 128:(dk + 1) * 128]
                              rh = q2_h[:, dk * GT + st_ * 128: dk * GT + GT]
                              rl = q2_l[:, dk * GT + st_ * 128: dk * GT + GT]
                              pairs += hl3(lh, ll, rh, rl)
                          return mm_acc(p, pA[:, 0:ncols], pairs)
                      dep = [htr_ev, hS]
                      if hA_ev:
                          dep.append(hA_ev[-1])
                      hma = S.step("pe", dep, mma)

                      def eva(v, st_=st_, ncols=ncols):
                          base = st_ * GT + st_ * 128
                          v.tensor_mul(atmp[:], pA[:, 0:128], triu_f[:])
                          v.tensor_copy(aT_h[:, base:base + 128], atmp[:])
                          v.scalar_tensor_tensor(aT_l[:, base:base + 128], atmp[:],
                                                 1.0, aT_h[:, base:base + 128],
                                                 ALU.mult, ALU.subtract)
                          if ncols > 128:
                              v.tensor_copy(aT_h[:, base + 128: st_ * GT + GT],
                                            pA[:, 128:ncols])
                              return v.scalar_tensor_tensor(
                                  aT_l[:, base + 128: st_ * GT + GT], pA[:, 128:ncols],
                                  1.0, aT_h[:, base + 128: st_ * GT + GT],
                                  ALU.mult, ALU.subtract)
                          return v.engine_nop()
                      hA_ev.append(S.step("dve", [hma], eva))
                  if p2sub == 'A':
                      raise _StopEmit


                  hevden = None
                  def emit_den():
                      nonlocal hevden
                      def mden(p):
                          last = None
                          first = True
                          for cq in range(G):
                              out = pden[:, cq:cq + 1]
                              for st_ in range(cq + 1):
                                  base = st_ * GT + cq * 128
                                  for pl in (aT_h, aT_l):
                                      last = p.matmul(out, pl[:, base:base + 128],
                                                      ones_bf[:, 0:1],
                                                      start=first, stop=False)
                                      first = False
                              for dk in range(ND):
                                  qh_sl = q2_h[:, dk * GT + cq * 128: dk * GT + cq * 128 + 128]
                                  ql_sl = q2_l[:, dk * GT + cq * 128: dk * GT + cq * 128 + 128]
                                  for (qp, zp) in ((qh_sl, zg_h), (ql_sl, zg_h), (qh_sl, zg_l)):
                                      isl = (cq == G - 1 and dk == ND - 1 and
                                             qp is qh_sl and zp is zg_l)
                                      last = p.matmul(out, qp, zp[:, dk:dk + 1],
                                                      start=False, stop=isl)
                          return last
                      hmden = S.step("pe", hA_ev + [hS], mden)

                      def evden(v):
                          v.tensor_scalar(dcol[:], pden[:], 1e-6, None, ALU.add)
                          return v.reciprocal(recip_den[:], dcol[:])
                      hevden = S.step("dve", [hmden], evden)

                  h_ev_lo = {}
                  h_ev_hi = {}
                  hmn = None
                  for et in range(ND):
                      def mnum(p, et=et):
                          slot = et % 2
                          pairs = []
                          for dk in range(ND):
                              sh_sl = s_h[:, dk * D + et * 128: dk * D + et * 128 + 128]
                              sl_sl = s_l[:, dk * D + et * 128: dk * D + et * 128 + 128]
                              qh_sl = q2_h[:, dk * GT:(dk + 1) * GT]
                              ql_sl = q2_l[:, dk * GT:(dk + 1) * GT]
                              pairs += hl3(sh_sl, sl_sl, qh_sl, ql_sl)
                          n = len(pairs)
                          last = None
                          for i, (l, r) in enumerate(pairs):
                              last = p.matmul(pnum2[:, slot * 512:(slot + 1) * 512],
                                              l, r, start=(i == 0), stop=False)
                          for st_ in range(G):
                              vh_sl = v2_h[:, st_ * D + et * 128: st_ * D + et * 128 + 128]
                              vl_sl = v2_l[:, st_ * D + et * 128: st_ * D + et * 128 + 128]
                              ah_sl = aT_h[:, st_ * GT + st_ * 128: st_ * GT + GT]
                              al_sl = aT_l[:, st_ * GT + st_ * 128: st_ * GT + GT]
                              ipairs = [(vh_sl, ah_sl), (vh_sl, al_sl), (vl_sl, ah_sl)]
                              for j, (l, r) in enumerate(ipairs):
                                  isl = (st_ == G - 1 and j == 2)
                                  last = p.matmul(
                                      pnum2[:, slot * 512 + st_ * 128:(slot + 1) * 512],
                                      l, r, start=False, stop=isl)
                          return last
                      dep = [hS, hA_ev[-1]]
                      if et >= 2:
                          dep += [h_ev_lo[et - 2], h_ev_hi[et - 2]]
                      hmn = S.step("pe", dep, mnum)

                      def ev_hi(a, et=et):
                          return a.copy(n_h[:, et * GT:(et + 1) * GT],
                                        pnum2[:, (et % 2) * 512:(et % 2 + 1) * 512])
                      h_ev_hi[et] = S.step("act", [hmn], ev_hi)

                      def ev_lo(v, et=et):
                          return v.scalar_tensor_tensor(
                              n_l[:, et * GT:(et + 1) * GT],
                              pnum2[:, (et % 2) * 512:(et % 2 + 1) * 512],
                              1.0, n_h[:, et * GT:(et + 1) * GT],
                              ALU.mult, ALU.subtract)
                      h_ev_lo[et] = S.step("dve", [hmn, h_ev_hi[et]], ev_lo)
                      if et == 0:
                          emit_den()
                  prev_mn_pe = hmn
                  if p2sub == 'num':
                      raise _StopEmit

                  for cq in range(G):
                      def mwo(p, cq=cq):
                          last = None
                          for eh in range(2):
                              pairs = []
                              for ek in range(ND):
                                  lh = n_h[:, ek * GT + cq * 128: ek * GT + cq * 128 + 128]
                                  ll = n_l[:, ek * GT + cq * 128: ek * GT + cq * 128 + 128]
                                  rh = wo_h[:, ek * D + eh * 512: ek * D + eh * 512 + 512]
                                  rl = wo_l[:, ek * D + eh * 512: ek * D + eh * 512 + 512]
                                  pairs += hl3(lh, ll, rh, rl)
                              last = mm_acc(p, pwo[:, eh * 512:(eh + 1) * 512], pairs)
                          return last
                      hmwo = S.step("pe", [h_ev_lo[ND - 1], h_ev_hi[ND - 1], prev_hey], mwo)
                      if p2sub == 'wo':
                          raise _StopEmit

                      def ldxr(e, dma, g=g, cq=cq):
                          t0 = (g * G + cq) * 128
                          dma(out=x_res[:], in_=x_d[t0:t0 + 128, :])
                      hxr = S.step("sp", [prev_hxr_reader], ldxr, n_dma=1)

                      def evy(v, cq=cq):
                          v.scalar_tensor_tensor(y_t[:], pwo[:], recip_den[:, cq:cq + 1],
                                                 bob_s[:], ALU.mult, ALU.add)
                          v.tensor_add(y_t[:], y_t[:], x_res[:])
                          v.bn_stats(bnb[:, 0:6], y_t[:, 0:512])
                          v.bn_stats(bnb[:, 6:12], y_t[:, 512:1024])
                          v.bn_aggr(bna[:, 0:2], bnb[:, 0:12])
                          return v.tensor_scalar(negm_c[:], bna[:, 0:1], -1.0, None, ALU.mult)
                      hey = S.step("dve", [hmwo, hxr, hevden, prev_hsd], evy)
                      prev_hey = hey
                      prev_hxr_reader = hey

                      hsd = S.step("act", [hey], lambda a: a.activation(
                          sd_c[:], bna[:, 1:2], ACT.Sqrt, bias=eps_c[:, 0:1]))
                      prev_hsd = hsd

                      def evy2(v):
                          v.reciprocal(rstd_c[:], sd_c[:])
                          return v.tensor_scalar(y1_t[:], y_t[:], negm_c[:, 0:1],
                                                 rstd_c[:, 0:1], ALU.add, ALU.mult)
                      hy1 = S.step("dve", [hsd, prev_sty1], evy2)

                      def sty1(e, dma, g=g, cq=cq):
                          t0 = (g * G + cq) * 128
                          dma(out=y1_dram[t0:t0 + 128, :], in_=y1_t[:])
                      prev_sty1 = S.step("gp", [hy1], sty1, n_dma=1)

                  h_prev_group_end = hy1
              except _StopEmit:
                pass

            h_p2_done = S.all_done()
            if p2sub:
                pass

        if upto == 'p2':
            _finish(S, block); return nc

        p3 = ExitStack()
        with p3:
            w1q_s = SB(p3, "w1q_s", [128, ND * H], BF16)
            w2q_s = SB(p3, "w2q_s", [128, NH * D], BF16)
            y1_t0 = SB(p3, "y1_t0", [128, D]); y1_t1 = SB(p3, "y1_t1", [128, D])
            m1c = SB(p3, "m1c", [128, 1]); r1c = SB(p3, "r1c", [128, 1])
            f1c = SB(p3, "f1c", [128, 1])
            m2c = SB(p3, "m2c", [128, 1]); r2c = SB(p3, "r2c", [128, 1])
            f2c = SB(p3, "f2c", [128, 1])
            mtmp = SB(p3, "mtmp", [128, 1])
            xi_ = SB(p3, "xi_", [128, D], I32)
            xb_ = SB(p3, "xb_", [128, D], BF16)
            yqD = SB(p3, "yqD", [128, ND * 128], BF16)
            h_f = SB(p3, "h_f", [128, H])
            hi_pc0 = SB(p3, "hi_pc0", [128, 512], I32)
            hi_pc1 = SB(p3, "hi_pc1", [128, 512], I32)
            hb_pc0 = SB(p3, "hb_pc0", [128, 512], BF16)
            hb_pc1 = SB(p3, "hb_pc1", [128, 512], BF16)
            hqD = SB(p3, "hqD", [128, NH * 128], BF16)
            y2_t = SB(p3, "y2_t", [128, D])
            out_t = SB(p3, "out_t", [128, D])
            bnb2 = SB(p3, "bnb2", [128, 12]); bna2 = SB(p3, "bna2", [128, 2])
            sd2 = SB(p3, "sd2", [128, 1]); rstd2 = SB(p3, "rstd2", [128, 1])
            negm2 = SB(p3, "negm2", [128, 1])
            ph_ = PS(p3, "ph_", [128, 2 * 512])
            po_ = PS(p3, "po_", [128, D])
            ptrA3 = PS(p3, "ptrA3", [128, 512], BF16)
            ptrB3 = PS(p3, "ptrB3", [128, 512], BF16)
            ptrC3 = PS(p3, "ptrC3", [128, 512], BF16)
            ptrD3 = PS(p3, "ptrD3", [128, 512], BF16)
            ptrbanks = [ptrA3, ptrB3, ptrC3, ptrD3]
            tr_state = {"idx": 0, "ev": [None, None, None, None]}

            def ldw12(e, dma):
                dma(out=w1q_s[:], in_=w1q_dram[:])
                dma(out=w2q_s[:], in_=w2q_dram[:])
            hw12 = S.step("sp", h_p2_done, ldw12, n_dma=2)

            prev_hs2 = None
            prev_stout = None
            prev_he2 = {0: None, 1: None}
            h_last_out = None
            hly = {}

            def ldy0(e, dma):
                dma(out=y1_t0[:], in_=y1_dram[0:128, :])
            hly[0] = S.step("sp", [hw12], ldy0, n_dma=1)

            def tr_step(src_t, src_off, dst_t, dst_off, dep_pe, dep_ev):
                b = tr_state["idx"] % 4
                tr_state["idx"] += 1
                bank = ptrbanks[b]

                def trx(p, bank=bank, src_t=src_t, src_off=src_off):
                    last = None
                    for j in range(4):
                        last = p.transpose(bank[:, j * 128:(j + 1) * 128],
                                           src_t[:, src_off + j * 128:
                                                 src_off + (j + 1) * 128],
                                           ident_b[:])
                    return last
                htp = S.step("pe", [tr_state["ev"][b]] + dep_pe, trx)

                def evx(a, bank=bank, dst_t=dst_t, dst_off=dst_off):
                    return a.copy(dst_t[:, dst_off:dst_off + 512], bank[:])
                hev = S.step("act", [htp] + dep_ev, evx)
                tr_state["ev"][b] = hev
                return htp, hev

            def emit_q1(ti, dep_extra):
                y1b = y1_t0 if ti % 2 == 0 else y1_t1

                def q1(v, y1b=y1b):
                    v.reduce_max(m1c[:], y1b[:], axis=AX.X, apply_absolute_value=True)
                    v.tensor_scalar(m1c[:], m1c[:], 1e-5, None, ALU.max)
                    v.reciprocal(r1c[:], m1c[:])
                    v.tensor_scalar(f1c[:], m1c[:], s1bc[:, 0:1], 1.0 / 127.0,
                                    ALU.mult, ALU.mult)
                    return v.tensor_scalar(xi_[:], y1b[:], r1c[:, 0:1], 127.0,
                                           ALU.mult, ALU.mult)
                return S.step("dve", [hly[ti]] + dep_extra, q1)

            def emit_hxb(hq1, dep_extra):
                return S.step("act", [hq1] + dep_extra,
                              lambda a: a.copy(xb_[:], xi_[:]))

            def emit_trq(hxb, dep_mm1_last):
                htr_pe = None
                htr_ev = None
                for half in range(2):
                    htr_pe, htr_ev = tr_step(
                        xb_, half * 512, yqD, half * 512, [hxb],
                        [dep_mm1_last] if half == 0 else [])
                return htr_pe, htr_ev

            hq1_0 = emit_q1(0, [])
            hxb_0 = emit_hxb(hq1_0, [])
            cur_trq_pe, cur_trq_ev = emit_trq(hxb_0, None)

            prev_tile = {"hgl": None, "hqp": None, "mm1_last": None, "mm2": None,
                         "hxb": hxb_0, "evh_last": None}

            for ti in range(NT):
                y1b = y1_t0 if ti % 2 == 0 else y1_t1
                if ti + 1 < NT:
                    def ldyn(e, dma, ti=ti):
                        nb = y1_t1 if (ti + 1) % 2 == 1 else y1_t0
                        dma(out=nb[:], in_=y1_dram[(ti + 1) * 128:(ti + 2) * 128, :])
                    hly[ti + 1] = S.step("sp", [hly[ti], prev_he2[(ti + 1) % 2]],
                                         ldyn, n_dma=1)

                hml = {}
                hgl = {}
                hmx = None
                pgl = prev_tile["hgl"]
                pqp = prev_tile["hqp"]
                for hh in range(8):
                    def mm1(p, hh=hh):
                        pairs = [(yqD[:, dk * 128:(dk + 1) * 128],
                                  w1q_s[:, dk * H + hh * 512: dk * H + hh * 512 + 512])
                                 for dk in range(ND)]
                        return mm_acc(p, ph_[:, (hh % 2) * 512:(hh % 2 + 1) * 512], pairs)
                    dep = [cur_trq_ev, hw12]
                    if hh >= 2:
                        dep.append(hgl[hh - 2])
                    elif pgl is not None:
                        dep.append(pgl[6 + hh])
                    hm = S.step("pe", dep, mm1)
                    hml[hh] = hm

                    def gel(a, hh=hh):
                        return a.activation(h_f[:, hh * 512:(hh + 1) * 512],
                                            ph_[:, (hh % 2) * 512:(hh % 2 + 1) * 512],
                                            ACT.Gelu, scale=f1c[:, 0:1])
                    gdep = [hm] + ([pqp[hh]] if pqp is not None else [])
                    hg = S.step("act", gdep, gel)
                    hgl[hh] = hg

                    def mx(v, hh=hh):
                        if hh == 0:
                            return v.reduce_max(m2c[:], h_f[:, 0:512], axis=AX.X,
                                                apply_absolute_value=True)
                        v.reduce_max(mtmp[:], h_f[:, hh * 512:(hh + 1) * 512], axis=AX.X,
                                     apply_absolute_value=True)
                        return v.tensor_max(m2c[:], m2c[:], mtmp[:])
                    hmx = S.step("dve", [hg], mx)

                def q2s(v, ti=ti):
                    v.tensor_scalar(m2c[:], m2c[:], 1e-5, None, ALU.max)
                    v.reciprocal(r2c[:], m2c[:])
                    return v.tensor_scalar(f2c[:], m2c[:], s2bc[:, 0:1], 1.0 / 127.0,
                                           ALU.mult, ALU.mult)
                hq2a = S.step("dve", [hmx], q2s)

                hq1_n = None
                hpc_acts = {}
                htps = {}
                hqp_l = {}
                hev_last = None
                for pc in range(8):
                    hip = hi_pc0 if pc % 2 == 0 else hi_pc1
                    hbp = hb_pc0 if pc % 2 == 0 else hb_pc1

                    def qpc(v, pc=pc, hip=hip):
                        return v.tensor_scalar(hip[:], h_f[:, pc * 512:(pc + 1) * 512],
                                               r2c[:, 0:1], 127.0, ALU.mult, ALU.mult)
                    hqp = S.step("dve", [hq2a] + ([hpc_acts.get(pc - 2)] if pc >= 2 else []),
                                 qpc)
                    hqp_l[pc] = hqp

                    def cpc(a, pc=pc, hip=hip, hbp=hbp):
                        return a.copy(hbp[:], hip[:])
                    hpc_act = S.step("act", [hqp, cur_trq_pe if pc <= 1 else None,
                                             htps.get(pc - 2)], cpc)
                    hpc_acts[pc] = hpc_act

                    htp, hev_last = tr_step(
                        hbp, 0, hqD, pc * 512,
                        [hpc_act] + ([prev_tile["mm2"]] if pc == 0 else []), [])
                    htps[pc] = htp

                    if pc == 1 and ti + 1 < NT:
                        hq1_n = emit_q1(ti + 1, [hgl[7], prev_tile["hxb"]])
                hq2 = hev_last

                nxt_hxb = prev_tile["hxb"]
                if ti + 1 < NT:
                    nxt_hxb = emit_hxb(hq1_n, [cur_trq_pe])
                    cur_trq_pe, cur_trq_ev = emit_trq(nxt_hxb, hml[7])

                def mm2(p):
                    last = None
                    for eh in range(2):
                        pairs = [(hqD[:, hk * 128:(hk + 1) * 128],
                                  w2q_s[:, hk * D + eh * 512: hk * D + eh * 512 + 512])
                                 for hk in range(NH)]
                        last = mm_acc(p, po_[:, eh * 512:(eh + 1) * 512], pairs)
                    return last
                hm2 = S.step("pe", [hq2, hw12, prev_hs2], mm2)

                def evy2b(v, ti=ti, y1b=y1b):
                    v.scalar_tensor_tensor(y2_t[:], po_[:], f2c[:, 0:1],
                                           y1b[:], ALU.mult, ALU.add)
                    v.bn_stats(bnb2[:, 0:6], y2_t[:, 0:512])
                    v.bn_stats(bnb2[:, 6:12], y2_t[:, 512:1024])
                    v.bn_aggr(bna2[:, 0:2], bnb2[:, 0:12])
                    return v.tensor_scalar(negm2[:], bna2[:, 0:1], -1.0, None, ALU.mult)
                he2 = S.step("dve", [hm2, prev_hs2], evy2b)
                prev_he2[ti % 2] = he2

                hs2 = S.step("act", [he2], lambda a: a.activation(
                    sd2[:], bna2[:, 1:2], ACT.Sqrt, bias=eps_c[:, 0:1]))
                prev_hs2 = hs2

                def fin(v):
                    v.reciprocal(rstd2[:], sd2[:])
                    return v.tensor_scalar(out_t[:], y2_t[:], negm2[:, 0:1],
                                           rstd2[:, 0:1], ALU.add, ALU.mult)
                hf = S.step("dve", [hs2, prev_stout], fin)

                def stout(e, dma, ti=ti):
                    dma(out=out_d[ti * 128:(ti + 1) * 128, :], in_=out_t[:])
                h_last_out = S.step("gp", [hf], stout, n_dma=1)
                prev_stout = h_last_out

                prev_tile = {"hgl": hgl, "hqp": hqp_l, "mm1_last": hml[7],
                             "mm2": hm2, "hxb": nxt_hxb, "evh_last": hq2}

            tgt = S.cnt["gp"]

            def gp_final(e):
                e.wait_ge(sem["gp"], tgt)
            block.gpsimd(gp_final)

    return nc


_CACHE = {}


def _get_nc(dbg=False):
    key = bool(dbg)
    if key not in _CACHE:
        _CACHE[key] = build_program(dbg=dbg)
    return _CACHE[key]


def _prep_inputs(inputs):
    x = np.asarray(inputs["L0_attn_x"], np.float32)
    Wq = np.asarray(inputs["L0_attn_Wq"], np.float32)
    bq = np.asarray(inputs["L0_attn_bq"], np.float32)
    Wk = np.asarray(inputs["L0_attn_Wk"], np.float32)
    bk = np.asarray(inputs["L0_attn_bk"], np.float32)
    Wv = np.asarray(inputs["L0_attn_Wv"], np.float32)
    bv = np.asarray(inputs["L0_attn_bv"], np.float32)
    Wo = np.asarray(inputs["L0_attn_Wo"], np.float32)
    bo = np.asarray(inputs["L0_attn_bo"], np.float32)
    w1 = np.asarray(inputs["L0_ffn_w1"], np.float32)
    w2 = np.asarray(inputs["L0_ffn_w2"], np.float32)

    shared = {
        "WqT": np.ascontiguousarray(Wq.T), "WkT": np.ascontiguousarray(Wk.T),
        "WvT": np.ascontiguousarray(Wv.T), "WoT": np.ascontiguousarray(Wo.T),
        "bq_t": np.ascontiguousarray(bq.reshape(ND, 128).T),
        "bk_t": np.ascontiguousarray(bk.reshape(ND, 128).T),
        "bv_b": np.ascontiguousarray(np.broadcast_to(bv, (128, D))),
        "bk_b": np.ascontiguousarray(np.broadcast_to(bk, (128, D))),
        "bo_b": np.ascontiguousarray(np.broadcast_to(bo, (128, D))),
        "w1T": np.ascontiguousarray(w1.T), "w2T": np.ascontiguousarray(w2.T),
        "triu": np.triu(np.ones((128, 128), np.float32)),
        "ident": np.eye(128, dtype=np.float32),
    }
    in_maps = []
    for core in range(8):
        b, q = divmod(core, 4)
        seg = x[b, q * TSEG:(q + 1) * TSEG, :]
        m = dict(shared)
        m["x_seg"] = np.ascontiguousarray(seg)
        m["xT_seg"] = np.ascontiguousarray(seg.T)
        pm = (np.arange(4) < q).astype(np.float32)
        m["pmask"] = np.ascontiguousarray(np.broadcast_to(pm, (128, 4)))
        in_maps.append(m)
    return in_maps


def kernel(**inputs):
    in_maps = _prep_inputs(inputs)
    nc = _get_nc(dbg=False)
    res = run_bass_kernel_spmd(nc, in_maps, core_ids=list(range(8)))
    out = np.empty((B, T, D), np.float32)
    for core in range(8):
        b, q = divmod(core, 4)
        out[b, q * TSEG:(q + 1) * TSEG, :] = np.asarray(res.results[core]["out"])
    return out


# version 10
# speedup vs baseline: 1.2545x; 1.1279x over previous
import numpy as np
import ml_dtypes
from contextlib import ExitStack
import concourse.bass as bass
import concourse.mybir as mybir
from concourse.bass_utils import run_bass_kernel_spmd

F32 = mybir.dt.float32
BF16 = mybir.dt.bfloat16
I32 = mybir.dt.int32
AX = mybir.AxisListType
ALU = mybir.AluOpType
ACT = mybir.ActivationFunctionType

B, T, D = 2, 16384, 1024
H = 4 * D
TSEG = T // 4
C = 128
G = 4
GT = G * C
NG = TSEG // GT
ND = D // 128
NH = H // 128
NT = TSEG // C
REPLICA_GROUPS = [[0, 1, 2, 3], [4, 5, 6, 7]]

ENG_ATTR = {"pe": "tensor", "dve": "vector", "act": "scalar", "sp": "sync",
            "gp": "gpsimd", "acd": "scalar"}


class _SerialEng:

    _PASSTHRU = {"wait_ge"}

    def __init__(self, eng, sem, state):
        self._eng = eng
        self._sem = sem
        self._state = state

    def __getattr__(self, name):
        fn = getattr(self._eng, name)
        if name in self._PASSTHRU or not callable(fn):
            return fn

        def wrapped(*a, **kw):
            st = self._state
            if st["issued"] > st["waited"]:
                self._eng.wait_ge(self._sem, st["issued"])
                st["waited"] = st["issued"]
            inst = fn(*a, **kw)
            inst.then_inc(self._sem, 1)
            st["issued"] += 1
            return inst
        return wrapped


class Sched:
    ENGS = ("pe", "dve", "act", "sp", "gp", "acd")
    SERIAL = {"dve", "act"}

    def __init__(self, nc, block, es):
        self.nc = nc
        self.block = block
        self.sem = {e: es.enter_context(nc.semaphore("sem_" + e)) for e in self.ENGS}
        self.sem["cc"] = es.enter_context(nc.semaphore("sem_cc"))
        self.cnt = {e: 0 for e in self.ENGS}
        self.cnt["cc"] = 0
        self.waited = {e: {e2: 0 for e2 in (*self.ENGS, "cc")} for e in self.ENGS}
        self.serial_state = {e: {"issued": 0, "waited": 0} for e in self.SERIAL}

    def step(self, eng, deps, body, n_dma=0):
        sem = self.sem
        waited = self.waited[eng]
        serial = eng in self.SERIAL
        use = []
        for d in deps:
            if d is None:
                continue
            de, dv = d
            if de == eng:
                if serial:
                    continue
            if dv > waited[de]:
                if de == eng and not n_dma and not serial:
                    continue
                use.append((de, dv))
                waited[de] = dv
        n_body_inst = [0]
        own_wait = 0
        if n_dma:
            own_wait = self.cnt[eng]
            if own_wait <= waited[eng]:
                own_wait = 0
            else:
                waited[eng] = own_wait
            self.cnt[eng] += 16 * n_dma
        issued = [0]

        def emit(e):
            if own_wait:
                e.wait_ge(sem[eng], own_wait)
            for de, dv in use:
                e.wait_ge(sem[de], dv)
            if n_dma:
                def dma(out, in_, **kw):
                    issued[0] += 1
                    return e.dma_start(out=out, in_=in_, **kw).then_inc(sem[eng], 16)
                body(e, dma)
                assert issued[0] == n_dma, f"declared {n_dma} dmas, issued {issued[0]}"
            elif serial:
                body(_SerialEng(e, sem[eng], self.serial_state[eng]))
            else:
                last = body(e)
                assert last is not None, "compute step must return its last instruction"
                last.then_inc(sem[eng], 1)

        if serial and not n_dma:
            getattr(self.block, ENG_ATTR[eng])(emit)
            self.cnt[eng] = self.serial_state[eng]["issued"]
            return (eng, self.cnt[eng])
        if not n_dma:
            self.cnt[eng] += 1
        getattr(self.block, ENG_ATTR[eng])(emit)
        return (eng, self.cnt[eng])

    def all_done(self):
        return [(e, self.cnt[e]) for e in self.ENGS]


def _finish(S, block):
    sem = S.sem
    cnts = {e: S.cnt[e] for e in S.ENGS}

    def gp_final(e):
        for e2 in S.ENGS:
            if cnts[e2]:
                e.wait_ge(sem[e2], cnts[e2])
    block.gpsimd(gp_final)


def mm_acc(pe, out, pairs):
    n = len(pairs)
    last = None
    for i, (l, r) in enumerate(pairs):
        last = pe.matmul(out, l, r, start=(i == 0), stop=(i == n - 1))
    return last


def hl3(lh, ll, rh, rl):
    return [(lh, rh), (lh, rl), (ll, rh)]


class _StopEmit(Exception):
    pass


def build_program(dbg=False, upto='p3b'):
    p2sub = upto.split('@', 1)[1] if upto.startswith('p2@') else None
    if p2sub:
        upto = 'p2'
    nc = bass.Bass("TRN2", target_bir_lowering=False)

    def dram_in(name, shape, dt=F32):
        return nc.dram_tensor(name, shape, dt, kind="ExternalInput").ap()

    def dram_scratch(name, shape, dt=F32):
        if dbg:
            return nc.dram_tensor(name, shape, dt, kind="ExternalOutput").ap()
        return nc.dram_tensor(name, shape, dt).ap()

    x_d = dram_in("x_seg", [TSEG, D])
    xT_d = dram_in("xT_seg", [D, TSEG])
    WT_d = {w: dram_in(w, [D, D]) for w in ("WqT", "WkT", "WvT", "WoT")}
    bq_d = dram_in("bq_t", [128, ND]); bk_d = dram_in("bk_t", [128, ND])
    bvb_d = dram_in("bv_b", [128, D]); bob_d = dram_in("bo_b", [128, D])
    bkb_d = dram_in("bk_b", [128, D])
    w1T_d = dram_in("w1T", [D, H]); w2T_d = dram_in("w2T", [H, D])
    triu_d = dram_in("triu", [128, 128])
    ident_d = dram_in("ident", [128, 128])
    pmask_d = dram_in("pmask", [128, 4])
    out_d = nc.dram_tensor("out", [TSEG, D], F32, kind="ExternalOutput").ap()

    spill_k = dram_scratch("spill_k", [NG, 2, 128, G * D], BF16)
    spill_v = dram_scratch("spill_v", [NG, 2, 128, G * D], BF16)
    spill_q = dram_scratch("spill_q", [NG, 2, 128, ND * GT], BF16)
    prefix_s = dram_scratch("prefix_s", [NG, 128, ND * D])
    prefix_z = dram_scratch("prefix_z", [NG, 128, ND])
    y1_dram = dram_scratch("y1_spill", [TSEG, D])
    hq_dram = dram_scratch("hq_sp", [NT, 128, H], BF16)
    w1q_dram = dram_scratch("w1q", [128, ND * H], BF16)
    w2q_dram = dram_scratch("w2q", [128, NH * D], BF16)
    sstart_d = dram_scratch("sstart", [128, ND * D])
    cc_in = nc.dram_tensor("cc_in", [D + 1, D], F32).ap()
    cc_out = nc.dram_tensor("cc_out", [4 * (D + 1), D], F32).ap()

    es = ExitStack()
    with es:
        block = es.enter_context(nc.Block())
        S = Sched(nc, block, es)
        sem = S.sem

        def SB(scope, name, shape, dt=F32):
            return scope.enter_context(nc.sbuf_tensor(name, shape, dt))

        def PS(scope, name, shape, dt=F32):
            return scope.enter_context(nc.psum_tensor(name, shape, dt))

        ones_bf = SB(es, "ones_bf", [128, 128], BF16)
        ident_f = SB(es, "ident_f", [128, 128], F32)
        ident_b = SB(es, "ident_b", [128, 128], BF16)
        triu_f = SB(es, "triu_f", [128, 128], F32)
        s1bc = SB(es, "s1bc", [128, 1]); s2bc = SB(es, "s2bc", [128, 1])
        thr1 = SB(es, "thr1", [128, 1]); nthr1 = SB(es, "nthr1", [128, 1])
        thr2 = SB(es, "thr2", [128, 1]); nthr2 = SB(es, "nthr2", [128, 1])
        bq_s = SB(es, "bq_s", [128, ND]); bk_s = SB(es, "bk_s", [128, ND])
        bvb_s = SB(es, "bvb_s", [128, D]); bob_s = SB(es, "bob_s", [128, D])
        bkb_s = SB(es, "bkb_s", [128, D])
        pmask_s = SB(es, "pmask_s", [128, 4])
        z_acc = SB(es, "z_acc", [128, ND]); z_start = SB(es, "z_start", [128, ND])
        zg = SB(es, "zg", [128, ND])
        zg_h = SB(es, "zg_h", [128, ND], BF16); zg_l = SB(es, "zg_l", [128, ND], BF16)
        f2all = SB(es, "f2all", [128, NT])
        eps_c = SB(es, "eps_c", [128, 1])
        s_scope = ExitStack()
        Smaster = SB(s_scope, "Smaster", [128, ND * D])

        def ld_const(e, dma):
            dma(out=ident_f[:], in_=ident_d[:])
            dma(out=triu_f[:], in_=triu_d[:])
            dma(out=bq_s[:], in_=bq_d[:])
            dma(out=bk_s[:], in_=bk_d[:])
            dma(out=bvb_s[:], in_=bvb_d[:])
            dma(out=bob_s[:], in_=bob_d[:])
            dma(out=bkb_s[:], in_=bkb_d[:])
            dma(out=pmask_s[:], in_=pmask_d[:])
        h_const = S.step("sp", [], ld_const, n_dma=8)

        def init_body(v):
            v.memset(ones_bf[:], 1.0)
            v.memset(eps_c[:], 1e-5)
            v.memset(z_acc[:], 0.0)
            v.memset(Smaster[:], 0.0)
            return v.tensor_copy(ident_b[:], ident_f[:])
        h_init = S.step("dve", [h_const], init_body)

        wkv_scope = ExitStack()
        wk_h = SB(wkv_scope, "wk_h", [128, ND * D], BF16)
        wk_l = SB(wkv_scope, "wk_l", [128, ND * D], BF16)
        wv_h = SB(wkv_scope, "wv_h", [128, ND * D], BF16)
        wv_l = SB(wkv_scope, "wv_l", [128, ND * D], BF16)
        wsh_scope = ExitStack()
        wstage_half = SB(wsh_scope, "wstage_half", [128, ND * D // 2])
        hws = [h_init]
        for (wd, wh, wl) in [(WT_d["WkT"], wk_h, wk_l), (WT_d["WvT"], wv_h, wv_l)]:
            for hf in range(2):
                def ldw(e, dma, wd=wd, hf=hf):
                    dma(out=wstage_half[:].rearrange("p (dt o) -> p dt o", dt=ND // 2),
                        in_=wd[hf * 512:(hf + 1) * 512, :]
                        .rearrange("(dt p) o -> p dt o", p=128))
                hlw = S.step("acd", hws, ldw, n_dma=1)

                def splw_h(a, wh=wh, hf=hf):
                    return a.copy(wh[:, hf * (ND // 2) * D:(hf + 1) * (ND // 2) * D],
                                  wstage_half[:])
                hwa = S.step("act", [hlw], splw_h)

                def splw_l(v, wh=wh, wl=wl, hf=hf):
                    return v.tensor_sub(wl[:, hf * (ND // 2) * D:(hf + 1) * (ND // 2) * D],
                                        wstage_half[:],
                                        wh[:, hf * (ND // 2) * D:(hf + 1) * (ND // 2) * D])
                hwl = S.step("dve", [hwa], splw_l)
                hws = [hwa, hwl]
        h_wkv = hws[1]
        wsh_scope.close()

        p0 = ExitStack()
        with p0:
            wpiece = SB(p0, "wpiece", [128, H])
            wpiece2 = SB(p0, "wpiece2", [128, H])
            tq2_ = SB(p0, "tq2_", [128, H], BF16)
            wsum = SB(p0, "wsum", [128, 1]); wtmp1 = SB(p0, "wtmp1", [128, 1])
            sh_ = SB(p0, "sh_", [128, 1], BF16); sl_ = SB(p0, "sl_", [128, 1], BF16)
            sc_row = SB(p0, "sc_row", [1, 2])
            sc_h = SB(p0, "sc_h", [1, 2], BF16); sc_l = SB(p0, "sc_l", [1, 2], BF16)
            cge = SB(p0, "cge", [128, H], BF16)
            tq_ = SB(p0, "tq_", [128, H], BF16)
            ps_s = PS(p0, "ps_s", [1, 2])
            ps_b = PS(p0, "ps_b", [128, 2])

            prev_wi_spill = None
            for wi, (wT_dd, npieces, pw, wq_dram, sbc, thr, nthr) in enumerate([
                    (w1T_d, ND, H, w1q_dram, s1bc, thr1, nthr1),
                    (w2T_d, NH, D, w2q_dram, s2bc, thr2, nthr2)]):
                hz = S.step("dve", [h_init], lambda v: v.memset(wsum[:], 0.0))
                hrs = [hz, hz]
                for pi in range(npieces):
                    buf = wpiece if pi % 2 == 0 else wpiece2

                    def ldp(e, dma, pi=pi, wT_dd=wT_dd, pw=pw, buf=buf):
                        dma(out=buf[:, :pw], in_=wT_dd[pi * 128:(pi + 1) * 128, :])
                    hl = S.step("sp", [hrs[pi % 2], h_wkv], ldp, n_dma=1)

                    def red(v, pw=pw, buf=buf):
                        v.reduce_sum(wtmp1[:], buf[:, :pw], axis=AX.X,
                                     apply_absolute_value=True)
                        return v.tensor_add(wsum[:], wsum[:], wtmp1[:])
                    hrs[pi % 2] = S.step("dve", [hl], red)
                hr = hrs[(npieces - 1) % 2]

                def spl(v):
                    v.tensor_copy(sh_[:], wsum[:])
                    return v.tensor_sub(sl_[:], wsum[:], sh_[:])
                hsp = S.step("dve", [hr], spl)
                hmm = S.step("pe", [hsp], lambda p, wi=wi: mm_acc(
                    p, ps_s[0:1, wi:wi + 1],
                    [(sh_[:], ones_bf[:, 0:1]), (sl_[:], ones_bf[:, 0:1])]))
                nelem = float(npieces * 128 * pw)

                def mean(v, wi=wi, nelem=nelem):
                    v.tensor_scalar(sc_row[0:1, wi:wi + 1], ps_s[0:1, wi:wi + 1],
                                    1.0 / nelem, 1e-5, ALU.mult, ALU.max)
                    v.tensor_copy(sc_h[0:1, wi:wi + 1], sc_row[0:1, wi:wi + 1])
                    return v.tensor_sub(sc_l[0:1, wi:wi + 1], sc_row[0:1, wi:wi + 1],
                                        sc_h[0:1, wi:wi + 1])
                hmean = S.step("dve", [hmm], mean)
                hbc = S.step("pe", [hmean], lambda p, wi=wi: mm_acc(
                    p, ps_b[:, wi:wi + 1],
                    [(ones_bf[0:1, :], sc_h[0:1, wi:wi + 1]),
                     (ones_bf[0:1, :], sc_l[0:1, wi:wi + 1])]))

                def mkthr(v, sbc=sbc, thr=thr, nthr=nthr, wi=wi):
                    v.tensor_copy(sbc[:], ps_b[:, wi:wi + 1])
                    v.tensor_scalar(thr[:], sbc[:], 0.5, None, ALU.mult)
                    return v.tensor_scalar(nthr[:], sbc[:], -0.5, None, ALU.mult)
                hthr = S.step("dve", [hbc], mkthr)

                hcs = [hthr, hthr]
                hqs = [prev_wi_spill, prev_wi_spill]
                hq = hthr
                for pi in range(npieces):
                    buf = wpiece if pi % 2 == 0 else wpiece2
                    tqb = tq_ if pi % 2 == 0 else tq2_

                    def ldp2(e, dma, pi=pi, wT_dd=wT_dd, pw=pw, buf=buf):
                        dma(out=buf[:, :pw], in_=wT_dd[pi * 128:(pi + 1) * 128, :])
                    hl = S.step("sp", [hcs[pi % 2], h_wkv,
                                 prev_wi_spill if pi < 2 else None],
                                ldp2, n_dma=1)

                    def qz(v, pw=pw, thr=thr, nthr=nthr, buf=buf, tqb=tqb):
                        v.tensor_scalar(cge[:, :pw], buf[:, :pw], nthr[:, 0:1], None,
                                        ALU.is_le)
                        return v.scalar_tensor_tensor(
                            tqb[:, :pw], buf[:, :pw], thr[:, 0:1], cge[:, :pw],
                            ALU.is_ge, ALU.subtract)
                    hcmp = S.step("dve", [hl, hqs[pi % 2]], qz)
                    hcs[pi % 2] = hcmp

                    def stq(e, dma, pi=pi, wq_dram=wq_dram, pw=pw, tqb=tqb):
                        dma(out=wq_dram[:, pi * pw:(pi + 1) * pw], in_=tqb[:, :pw])
                    hq = S.step("gp", [hcmp], stq, n_dma=1)
                    hqs[pi % 2] = hq
                prev_wi_spill = hq
            h_p0_done = S.all_done()
        if upto == 'p0':
            _finish(S, block); s_scope.close(); return nc

        p1 = ExitStack()
        with p1:
            xT_st = SB(p1, "xT_st", [128, ND * GT])
            xh = SB(p1, "xh", [128, ND * GT], BF16); xl = SB(p1, "xl", [128, ND * GT], BF16)
            kphi_e = SB(p1, "kphi_e", [128, D]); kphi_r = SB(p1, "kphi_r", [128, D])
            kpT_f = SB(p1, "kpT_f", [128, D])
            kT_h = SB(p1, "kT_h", [128, G * D], BF16); kT_l = SB(p1, "kT_l", [128, G * D], BF16)
            vT_h = SB(p1, "vT_h", [128, G * D], BF16); vT_l = SB(p1, "vT_l", [128, G * D], BF16)
            bkr_h = SB(p1, "bkr_h", [1, D], BF16); bkr_l = SB(p1, "bkr_l", [1, D], BF16)
            bvr_h = SB(p1, "bvr_h", [1, D], BF16); bvr_l = SB(p1, "bvr_l", [1, D], BF16)
            pk = PS(p1, "pk", [128, 2 * D])
            pv = PS(p1, "pv", [128, D])
            pkv = PS(p1, "pkv", [128, 2 * 512])

            def mk_rows_h(a):
                a.copy(bkr_h[:], bkb_s[0:1, :])
                return a.copy(bvr_h[:], bvb_s[0:1, :])
            hra = S.step("act", [h_const], mk_rows_h)

            def mk_rows_l(v):
                v.tensor_sub(bkr_l[:], bkb_s[0:1, :], bkr_h[:])
                return v.tensor_sub(bvr_l[:], bvb_s[0:1, :], bvr_h[:])
            hrl = S.step("dve", [hra], mk_rows_l)

            prev_split = h_wkv
            prev_x_pe = None
            prev_kv_done = h_init
            prev_spk = None; prev_spv = None
            prev_kv_pe = None
            for g in range(NG):
                def ldx(e, dma, g=g):
                    dma(out=xT_st[:].rearrange("p (dt t) -> p dt t", dt=ND),
                        in_=xT_d[:, g * GT:(g + 1) * GT]
                        .rearrange("(dt p) t -> p dt t", p=128))
                hx = S.step("sp", [prev_split] + (h_p0_done if g == 0 else []),
                            ldx, n_dma=1)

                hxa = S.step("act", [hx, prev_x_pe], lambda a: a.copy(xh[:], xT_st[:]))
                hxs = S.step("dve", [hxa], lambda v: v.tensor_sub(xl[:], xT_st[:], xh[:]))

                def spz(e, dma, g=g):
                    dma(out=prefix_z[g], in_=z_acc[:])
                    dma(out=prefix_s[g], in_=Smaster[:])
                hspill = S.step("gp", [prev_kv_done], spz, n_dma=2)

                hk_stt = None
                hk_hi = None
                hk_lo = None
                hz_mm = None
                zdeps = []
                for tt in range(G):
                    def mmk(p, tt=tt):
                        slot = tt % 2
                        last = None
                        for eh in range(2):
                            pairs = []
                            for dk in range(ND):
                                lh = xh[:, dk * GT + tt * 128: dk * GT + tt * 128 + 128]
                                ll = xl[:, dk * GT + tt * 128: dk * GT + tt * 128 + 128]
                                rh = wk_h[:, dk * D + eh * 512: dk * D + eh * 512 + 512]
                                rl = wk_l[:, dk * D + eh * 512: dk * D + eh * 512 + 512]
                                pairs += hl3(lh, ll, rh, rl)
                            pairs.append((ones_bf[0:1, :], bkr_h[0:1, eh * 512:(eh + 1) * 512]))
                            pairs.append((ones_bf[0:1, :], bkr_l[0:1, eh * 512:(eh + 1) * 512]))
                            last = mm_acc(p, pk[:, slot * D + eh * 512: slot * D + eh * 512 + 512],
                                          pairs)
                        return last
                    dep = [hxs, h_wkv, hrl]
                    if tt >= 2:
                        dep.append(hk_evac2[tt - 2])
                    hmk = S.step("pe", dep, mmk)

                    def ek_e(a, tt=tt):
                        return a.activation(kphi_e[:], pk[:, (tt % 2) * D:(tt % 2 + 1) * D],
                                            ACT.Exp)
                    dep = [hmk] + ([hk_stt] if hk_stt else [])
                    hea = S.step("act", dep, ek_e)

                    def ek_r(v, tt=tt):
                        return v.tensor_scalar(kphi_r[:], pk[:, (tt % 2) * D:(tt % 2 + 1) * D],
                                               0.0, None, ALU.max)
                    her = S.step("dve", [hmk] + ([hk_hi] if hk_hi else []), ek_r)

                    def ek_stt(v, tt=tt):
                        return v.scalar_tensor_tensor(kpT_f[:], kphi_e[:], 1.0, kphi_r[:],
                                                      ALU.min, ALU.add)
                    hk_stt = S.step("dve", [hea], ek_stt)

                    def ek_hi(v, tt=tt):
                        return v.tensor_copy(kT_h[:, tt * D:(tt + 1) * D], kpT_f[:])
                    hk_hi = S.step("dve", [prev_spk, prev_kv_pe], ek_hi)

                    def ek_lo(v, tt=tt):
                        return v.tensor_sub(kT_l[:, tt * D:(tt + 1) * D], kpT_f[:],
                                            kT_h[:, tt * D:(tt + 1) * D])
                    hk_lo = S.step("dve", [hk_hi], ek_lo)
                    if tt == 0:
                        hk_evac2 = {}
                    hk_evac2[tt] = hk_lo

                    zdeps.append((hk_lo, hk_hi))

                def spk(e, dma, g=g):
                    dma(out=spill_k[g, 0], in_=kT_h[:])
                    dma(out=spill_k[g, 1], in_=kT_l[:])
                hspk = S.step("gp", [hk_lo, hk_hi], spk, n_dma=2)

                hv_hi = None
                hv_lo = None
                for tt in range(G):
                    def mmv(p, tt=tt):
                        last = None
                        for eh in range(2):
                            pairs = []
                            for dk in range(ND):
                                lh = xh[:, dk * GT + tt * 128: dk * GT + tt * 128 + 128]
                                ll = xl[:, dk * GT + tt * 128: dk * GT + tt * 128 + 128]
                                rh = wv_h[:, dk * D + eh * 512: dk * D + eh * 512 + 512]
                                rl = wv_l[:, dk * D + eh * 512: dk * D + eh * 512 + 512]
                                pairs += hl3(lh, ll, rh, rl)
                            pairs.append((ones_bf[0:1, :], bvr_h[0:1, eh * 512:(eh + 1) * 512]))
                            pairs.append((ones_bf[0:1, :], bvr_l[0:1, eh * 512:(eh + 1) * 512]))
                            last = mm_acc(p, pv[:, eh * 512:(eh + 1) * 512], pairs)
                        return last
                    dep = [hxs, h_wkv, hrl]
                    if tt >= 1:
                        dep.append(hv_lo)
                    hmv = S.step("pe", dep, mmv)

                    def zmm(p, tt=tt):
                        last = None
                        for dt in range(ND):
                            for j, pl in enumerate((kT_h, kT_l)):
                                st = (tt == 0 and dt == 0 and j == 0)
                                sp_ = (tt == G - 1 and dt == ND - 1 and j == 1)
                                last = p.matmul(pkv[:, dt:dt + 1],
                                                pl[:, tt * D + dt * 128: tt * D + dt * 128 + 128],
                                                ones_bf[:, 0:1], start=st, stop=sp_)
                        return last
                    zdep = list(zdeps[tt])
                    if tt == 0:
                        zdep.append(prev_kv_done)
                    hz_mm = S.step("pe", zdep, zmm)

                    def ev_hi(a, tt=tt):
                        return a.copy(vT_h[:, tt * D:(tt + 1) * D], pv[:])
                    hv_hi = S.step("act", [hmv, prev_spv, prev_kv_pe], ev_hi)

                    def ev_lo(v, tt=tt):
                        return v.tensor_sub(vT_l[:, tt * D:(tt + 1) * D], pv[:],
                                            vT_h[:, tt * D:(tt + 1) * D])
                    hv_lo = S.step("dve", [hmv, hv_hi], ev_lo)

                def spv(e, dma, g=g):
                    dma(out=spill_v[g, 0], in_=vT_h[:])
                    dma(out=spill_v[g, 1], in_=vT_l[:])
                hspv = S.step("gp", [hv_lo, hv_hi], spv, n_dma=2)

                def evz(v):
                    return v.tensor_add(z_acc[:], z_acc[:], pkv[:, 0:ND])
                hz_ev = S.step("dve", [hz_mm, hspill], evz)

                hkv_ev = []
                hm_last = None
                for dt in range(ND):
                    for eh in range(2):
                        idx = dt * 2 + eh

                        def mmkv(p, dt=dt, eh=eh, idx=idx):
                            pairs = []
                            for tc in range(G):
                                lh = kT_h[:, tc * D + dt * 128: tc * D + dt * 128 + 128]
                                ll = kT_l[:, tc * D + dt * 128: tc * D + dt * 128 + 128]
                                rh = vT_h[:, tc * D + eh * 512: tc * D + eh * 512 + 512]
                                rl = vT_l[:, tc * D + eh * 512: tc * D + eh * 512 + 512]
                                pairs += hl3(lh, ll, rh, rl)
                            slot = idx % 2
                            return mm_acc(p, pkv[:, slot * 512:(slot + 1) * 512], pairs)
                        dep = [hk_lo, hk_hi, hv_lo, hv_hi, hz_ev]
                        if idx >= 2:
                            dep.append(hkv_ev[idx - 2])
                        hm_last = S.step("pe", dep, mmkv)

                        def evkv(v, dt=dt, eh=eh, idx=idx):
                            slot = idx % 2
                            dst = Smaster[:, dt * D + eh * 512: dt * D + eh * 512 + 512]
                            return v.tensor_add(dst, dst, pkv[:, slot * 512:(slot + 1) * 512])
                        hkv_ev.append(S.step("dve", [hm_last, hspill], evkv))

                prev_kv_done = hkv_ev[-1]
                prev_kv_pe = hm_last
                prev_x_pe = hm_last
                prev_split = hxs
                prev_spk = hspk; prev_spv = hspv

            h_p1a_done = S.all_done()

        wkv_scope.close()
        if upto == 'p1a':
            _finish(S, block); s_scope.close(); return nc

        def wr_cc(e, dma):
            for dt in range(ND):
                dma(out=cc_in[dt * 128:(dt + 1) * 128, :],
                    in_=Smaster[:, dt * D:(dt + 1) * D])
            for dt in range(ND):
                dma(out=cc_in[D:D + 1, dt * 128:(dt + 1) * 128],
                    in_=z_acc[:, dt:dt + 1])
        h_wrcc = S.step("gp", h_p1a_done, wr_cc, n_dma=16)

        _wr_val = h_wrcc[1]

        def _cc_emit(e):
            e.wait_ge(S.sem["gp"], _wr_val)
            e.collective_compute(
                "AllGather", ALU.bypass, replica_groups=REPLICA_GROUPS,
                ins=[cc_in[:]], outs=[cc_out[:]]).then_inc(S.sem["cc"], 1)
        block.gpsimd(_cc_emit)
        S.cnt["cc"] = 1
        h_cc = ("cc", 1)

        p1b = ExitStack()
        with p1b:
            wq_h = SB(p1b, "wq_h", [128, ND * D], BF16); wq_l = SB(p1b, "wq_l", [128, ND * D], BF16)
            ws2_scope = ExitStack()
            wstage2 = SB(ws2_scope, "wstage2", [128, ND * D])

            def ldwq(e, dma):
                dma(out=wstage2[:].rearrange("p (dt o) -> p dt o", dt=ND),
                    in_=WT_d["WqT"].rearrange("(dt p) o -> p dt o", p=128))
            hlq = S.step("sp", h_p1a_done, ldwq, n_dma=1)

            def splwq(v):
                v.tensor_copy(wq_h[:], wstage2[:])
                return v.tensor_sub(wq_l[:], wstage2[:], wq_h[:])
            hwq = S.step("dve", [hlq], splwq)
            ws2_scope.close()

            xT_st2 = SB(p1b, "xT_st2", [128, ND * GT])
            xh2 = SB(p1b, "xh2", [128, ND * GT], BF16); xl2 = SB(p1b, "xl2", [128, ND * GT], BF16)
            qD_f = SB(p1b, "qD_f", [128, ND * GT])
            q_hA = SB(p1b, "q_hA", [128, ND * GT], BF16); q_lA = SB(p1b, "q_lA", [128, ND * GT], BF16)
            q_hB = SB(p1b, "q_hB", [128, ND * GT], BF16); q_lB = SB(p1b, "q_lB", [128, ND * GT], BF16)
            ephi2 = SB(p1b, "ephi2", [128, GT]); rphi2 = SB(p1b, "rphi2", [128, GT])
            pq = PS(p1b, "pq", [128, 2 * GT])
            px = ExitStack()
            Ttmp = SB(px, "Ttmp", [128, ND * D])
            ztmp8 = SB(px, "ztmp8", [128, ND])
            hacc_x2 = None
            h_spss = None

            prev_split = hwq
            prev_x_pe = None
            spq_h = {0: None, 1: None}
            for g in range(NG):
                q_h = q_hA if g % 2 == 0 else q_hB
                q_l = q_lA if g % 2 == 0 else q_lB
                def ldx2(e, dma, g=g):
                    dma(out=xT_st2[:].rearrange("p (dt t) -> p dt t", dt=ND),
                        in_=xT_d[:, g * GT:(g + 1) * GT]
                        .rearrange("(dt p) t -> p dt t", p=128))
                hx = S.step("sp", [prev_split], ldx2, n_dma=1)

                def splx2(v):
                    v.tensor_copy(xh2[:], xT_st2[:])
                    return v.tensor_sub(xl2[:], xT_st2[:], xh2[:])
                hxs = S.step("dve", [hx, hwq, prev_x_pe], splx2)

                hq_evac = []
                hmq = None
                for ot in range(ND):
                    def mmq(p, ot=ot):
                        pairs = []
                        for dk in range(ND):
                            lh = wq_h[:, dk * D + ot * 128: dk * D + ot * 128 + 128]
                            ll = wq_l[:, dk * D + ot * 128: dk * D + ot * 128 + 128]
                            rh = xh2[:, dk * GT:(dk + 1) * GT]
                            rl = xl2[:, dk * GT:(dk + 1) * GT]
                            pairs += hl3(lh, ll, rh, rl)
                        return mm_acc(p, pq[:, (ot % 2) * GT:(ot % 2 + 1) * GT], pairs)
                    dep = [hxs] + ([hq_evac[ot - 2][1]] if ot >= 2 else [])
                    hmq = S.step("pe", dep, mmq)

                    def eq_act(a, ot=ot):
                        return a.activation(ephi2[:], pq[:, (ot % 2) * GT:(ot % 2 + 1) * GT],
                                            ACT.Exp, bias=bq_s[:, ot:ot + 1])
                    heq = S.step("act", [hmq] + ([hq_evac[ot - 1][1]] if ot >= 1 else []),
                                 eq_act)

                    def eq_relu(v, ot=ot):
                        return v.tensor_scalar(rphi2[:], pq[:, (ot % 2) * GT:(ot % 2 + 1) * GT],
                                               bq_s[:, ot:ot + 1], 0.0, ALU.add, ALU.max)
                    hrel = S.step("dve", [hmq, spq_h[g % 2] if ot == 0 else None,
                                          prev_x_pe], eq_relu)

                    def eq_stt(v, ot=ot):
                        return v.scalar_tensor_tensor(qD_f[:, ot * GT:(ot + 1) * GT],
                                                      ephi2[:], 1.0, rphi2[:],
                                                      ALU.min, ALU.add)
                    hst = S.step("dve", [heq], eq_stt)

                    def eq_hi(v, ot=ot):
                        return v.tensor_copy(q_h[:, ot * GT:(ot + 1) * GT],
                                             qD_f[:, ot * GT:(ot + 1) * GT])
                    hhi = S.step("dve", [], eq_hi)

                    def eq_lo(v, ot=ot):
                        return v.tensor_sub(q_l[:, ot * GT:(ot + 1) * GT],
                                            qD_f[:, ot * GT:(ot + 1) * GT],
                                            q_h[:, ot * GT:(ot + 1) * GT])
                    hev = S.step("dve", [], eq_lo)
                    hq_evac.append((hmq, hev))

                def spq(e, dma, g=g, q_h=q_h, q_l=q_l):
                    dma(out=spill_q[g, 0], in_=q_h[:])
                    dma(out=spill_q[g, 1], in_=q_l[:])
                spq_h[g % 2] = S.step("gp", [hq_evac[-1][1]], spq, n_dma=2)
                prev_split = hxs
                prev_x_pe = hmq

                if 3 <= g <= 6:
                    r = g - 3

                    def ldT(e, dma, r=r):
                        base = r * (D + 1)
                        dma(out=Ttmp[:].rearrange("p (dt e) -> p dt e", dt=ND),
                            in_=cc_out[base:base + D, :]
                            .rearrange("(dt p) e -> p dt e", p=128))
                        dma(out=ztmp8[:],
                            in_=cc_out[base + D:base + D + 1, :]
                            .rearrange("r (dt p) -> p (r dt)", p=128, r=1),
                            allow_slow_non_contiguous=True)
                    hl = S.step("acd", [h_cc] + ([hacc_x2] if hacc_x2 else []),
                                ldT, n_dma=2)

                    def accT(v, r=r):
                        if r == 0:
                            v.tensor_scalar(Smaster[:], Ttmp[:], pmask_s[:, 0:1],
                                            None, ALU.mult)
                            return v.tensor_scalar(z_start[:], ztmp8[:],
                                                   pmask_s[:, 0:1], None, ALU.mult)
                        v.scalar_tensor_tensor(Smaster[:], Ttmp[:], pmask_s[:, r:r + 1],
                                               Smaster[:], ALU.mult, ALU.add)
                        return v.scalar_tensor_tensor(z_start[:], ztmp8[:],
                                                      pmask_s[:, r:r + 1],
                                                      z_start[:], ALU.mult, ALU.add)
                    hacc_x2 = S.step("dve", [hl], accT)
                    if r == 3:
                        def spss(e, dma):
                            dma(out=sstart_d[:], in_=Smaster[:])
                        h_spss = S.step("acd", [hacc_x2], spss, n_dma=1)

            h_p1b_done = S.all_done()
            px.close()

        if upto == 'p1b':
            _finish(S, block); s_scope.close(); return nc

        h_x2_done = h_p1b_done
        s_scope.close()
        if upto == 'x2':
            _finish(S, block); return nc

        p2 = ExitStack()
        with p2:
            wo_h = SB(p2, "wo_h", [128, ND * D], BF16); wo_l = SB(p2, "wo_l", [128, ND * D], BF16)
            ws3_scope = ExitStack()
            wstage3 = SB(ws3_scope, "wstage3", [128, ND * D])

            def ldwo(e, dma):
                dma(out=wstage3[:].rearrange("p (dt o) -> p dt o", dt=ND),
                    in_=WT_d["WoT"].rearrange("(dt p) o -> p dt o", p=128))
            hlo = S.step("sp", h_x2_done, ldwo, n_dma=1)

            def splwo(v):
                v.tensor_copy(wo_h[:], wstage3[:])
                return v.tensor_sub(wo_l[:], wstage3[:], wo_h[:])
            hwo = S.step("dve", [hlo], splwo)
            ws3_scope.close()
            if p2sub == 'w':
                _finish(S, block)
                return nc

            Sgrp = SB(p2, "Sgrp", [128, ND * D])
            s_h = SB(p2, "s_h", [128, ND * D], BF16); s_l = SB(p2, "s_l", [128, ND * D], BF16)
            q2_h = SB(p2, "q2_h", [128, ND * GT], BF16); q2_l = SB(p2, "q2_l", [128, ND * GT], BF16)
            k2_h = SB(p2, "k2_h", [128, G * D], BF16); k2_l = SB(p2, "k2_l", [128, G * D], BF16)
            v2_h = SB(p2, "v2_h", [128, G * D], BF16); v2_l = SB(p2, "v2_l", [128, G * D], BF16)
            kd_h = SB(p2, "kd_h", [128, ND * 128], BF16); kd_l = SB(p2, "kd_l", [128, ND * 128], BF16)
            aT_h = SB(p2, "aT_h", [128, G * GT], BF16); aT_l = SB(p2, "aT_l", [128, G * GT], BF16)
            n_h = SB(p2, "n_h", [128, ND * GT], BF16); n_l = SB(p2, "n_l", [128, ND * GT], BF16)
            atmp = SB(p2, "atmp", [128, 128])
            dcol = SB(p2, "dcol", [128, G])
            recip_den = SB(p2, "recip_den", [128, G])
            x_res = SB(p2, "x_res", [128, D])
            y_t = SB(p2, "y_t", [128, D])
            y1_t = SB(p2, "y1_t", [128, D])
            zgtmp = SB(p2, "zgtmp", [128, ND])
            bnb = SB(p2, "bnb", [128, 12]); bna = SB(p2, "bna", [128, 2])
            sd_c = SB(p2, "sd_c", [128, 1]); rstd_c = SB(p2, "rstd_c", [128, 1])
            negm_c = SB(p2, "negm_c", [128, 1])
            pA = PS(p2, "pA", [128, 512])
            pden = PS(p2, "pden", [128, G])
            pnum2 = PS(p2, "pnum2", [128, 2 * 512])
            pwo = PS(p2, "pwo", [128, D])
            ptrbA = PS(p2, "ptrbA", [128, 512], BF16)
            ptrbB = PS(p2, "ptrbB", [128, 512], BF16)

            h_prev_group_end = None
            prev_hey = None
            prev_hsd = None
            prev_sty1 = None
            prev_hxr_reader = None
            prev_trev = None
            prev_mn_pe = None
            h_ldsz = None
            h_accS = None
            for g in range(NG if not p2sub else 1):
              try:
                  if h_ldsz is None:
                      def ldsz0(e, dma, g=g):
                          dma(out=Sgrp[:], in_=prefix_s[g])
                          dma(out=zgtmp[:], in_=prefix_z[g])
                      h_ldsz = S.step("sp", h_x2_done + [hwo], ldsz0, n_dma=2)

                      def accS0(e, dma):
                          for i in range(4):
                              dma(out=Sgrp[:, i * 2048:(i + 1) * 2048],
                                  in_=sstart_d[:, i * 2048:(i + 1) * 2048],
                                  accum_op=ALU.add)
                      h_accS = S.step("gp", [h_ldsz], accS0, n_dma=4)

                  def ldkqv(e, dma, g=g):
                      dma(out=q2_h[:], in_=spill_q[g, 0]); dma(out=q2_l[:], in_=spill_q[g, 1])
                      dma(out=k2_h[:], in_=spill_k[g, 0]); dma(out=k2_l[:], in_=spill_k[g, 1])
                      dma(out=v2_h[:], in_=spill_v[g, 0]); dma(out=v2_l[:], in_=spill_v[g, 1])
                  hld = S.step("sp", [prev_mn_pe], ldkqv, n_dma=6)
                  if p2sub == 'ld0':
                      raise _StopEmit

                  hz0 = S.step("dve", [h_ldsz, prev_mn_pe], lambda v: v.tensor_add(
                      zg[:], zgtmp[:], z_start[:]))

                  def mks_hi(a):
                      a.copy(s_h[:], Sgrp[:])
                      return a.copy(zg_h[:], zg[:])
                  hSa = S.step("act", [h_ldsz, h_accS, hz0, prev_mn_pe], mks_hi)

                  def mks_lo(v):
                      v.tensor_sub(s_l[:], Sgrp[:], s_h[:])
                      return v.tensor_sub(zg_l[:], zg[:], zg_h[:])
                  hS = S.step("dve", [hSa], mks_lo)
                  if p2sub == 'ld':
                      raise _StopEmit

                  if g + 1 < NG:
                      def ldsz(e, dma, g=g):
                          dma(out=Sgrp[:], in_=prefix_s[g + 1])
                          dma(out=zgtmp[:], in_=prefix_z[g + 1])
                      h_ldsz = S.step("sp", [hS], ldsz, n_dma=2)

                      def accSn(e, dma):
                          for i in range(4):
                              dma(out=Sgrp[:, i * 2048:(i + 1) * 2048],
                                  in_=sstart_d[:, i * 2048:(i + 1) * 2048],
                                  accum_op=ALU.add)
                      h_accS = S.step("gp", [h_ldsz], accSn, n_dma=4)

                  hA_ev = []
                  for st_ in range(G):
                      htr_ev = None
                      for (ksrc, kdst) in ((k2_h, kd_h), (k2_l, kd_l)):
                          for half in range(2):
                              ptrb = ptrbA if half == 0 else ptrbB

                              def trk2(p, st_=st_, half=half, ksrc=ksrc, ptrb=ptrb):
                                  last = None
                                  for j in range(4):
                                      dk = half * 4 + j
                                      last = p.transpose(
                                          ptrb[:, j * 128:(j + 1) * 128],
                                          ksrc[:, st_ * D + dk * 128: st_ * D + dk * 128 + 128],
                                          ident_b[:])
                                  return last
                              ht = S.step("pe", [hld, prev_trev], trk2)

                              def evt(a, half=half, kdst=kdst, ptrb=ptrb):
                                  return a.copy(kdst[:, half * 512:(half + 1) * 512],
                                                ptrb[:])
                              prev_trev = S.step("act", [ht], evt)
                          htr_ev = prev_trev

                      ncols = GT - st_ * 128

                      def mma(p, st_=st_, ncols=ncols):
                          pairs = []
                          for dk in range(ND):
                              lh = kd_h[:, dk * 128:(dk + 1) * 128]
                              ll = kd_l[:, dk * 128:(dk + 1) * 128]
                              rh = q2_h[:, dk * GT + st_ * 128: dk * GT + GT]
                              rl = q2_l[:, dk * GT + st_ * 128: dk * GT + GT]
                              pairs += hl3(lh, ll, rh, rl)
                          return mm_acc(p, pA[:, 0:ncols], pairs)
                      dep = [htr_ev, hS]
                      if hA_ev:
                          dep.append(hA_ev[-1])
                      hma = S.step("pe", dep, mma)

                      def eva(v, st_=st_, ncols=ncols):
                          base = st_ * GT + st_ * 128
                          v.tensor_mul(atmp[:], pA[:, 0:128], triu_f[:])
                          v.tensor_copy(aT_h[:, base:base + 128], atmp[:])
                          v.scalar_tensor_tensor(aT_l[:, base:base + 128], atmp[:],
                                                 1.0, aT_h[:, base:base + 128],
                                                 ALU.mult, ALU.subtract)
                          if ncols > 128:
                              v.tensor_copy(aT_h[:, base + 128: st_ * GT + GT],
                                            pA[:, 128:ncols])
                              return v.scalar_tensor_tensor(
                                  aT_l[:, base + 128: st_ * GT + GT], pA[:, 128:ncols],
                                  1.0, aT_h[:, base + 128: st_ * GT + GT],
                                  ALU.mult, ALU.subtract)
                          return v.engine_nop()
                      hA_ev.append(S.step("dve", [hma], eva))
                  if p2sub == 'A':
                      raise _StopEmit


                  hevden = None
                  def emit_den():
                      nonlocal hevden
                      def mden(p):
                          last = None
                          first = True
                          for cq in range(G):
                              out = pden[:, cq:cq + 1]
                              for st_ in range(cq + 1):
                                  base = st_ * GT + cq * 128
                                  for pl in (aT_h, aT_l):
                                      last = p.matmul(out, pl[:, base:base + 128],
                                                      ones_bf[:, 0:1],
                                                      start=first, stop=False)
                                      first = False
                              for dk in range(ND):
                                  qh_sl = q2_h[:, dk * GT + cq * 128: dk * GT + cq * 128 + 128]
                                  ql_sl = q2_l[:, dk * GT + cq * 128: dk * GT + cq * 128 + 128]
                                  for (qp, zp) in ((qh_sl, zg_h), (ql_sl, zg_h), (qh_sl, zg_l)):
                                      isl = (cq == G - 1 and dk == ND - 1 and
                                             qp is qh_sl and zp is zg_l)
                                      last = p.matmul(out, qp, zp[:, dk:dk + 1],
                                                      start=False, stop=isl)
                          return last
                      hmden = S.step("pe", hA_ev + [hS], mden)

                      def evden(v):
                          v.tensor_scalar(dcol[:], pden[:], 1e-6, None, ALU.add)
                          return v.reciprocal(recip_den[:], dcol[:])
                      hevden = S.step("dve", [hmden], evden)

                  h_ev_lo = {}
                  h_ev_hi = {}
                  hmn = None
                  for et in range(ND):
                      def mnum(p, et=et):
                          slot = et % 2
                          pairs = []
                          for dk in range(ND):
                              sh_sl = s_h[:, dk * D + et * 128: dk * D + et * 128 + 128]
                              sl_sl = s_l[:, dk * D + et * 128: dk * D + et * 128 + 128]
                              qh_sl = q2_h[:, dk * GT:(dk + 1) * GT]
                              ql_sl = q2_l[:, dk * GT:(dk + 1) * GT]
                              pairs += hl3(sh_sl, sl_sl, qh_sl, ql_sl)
                          n = len(pairs)
                          last = None
                          for i, (l, r) in enumerate(pairs):
                              last = p.matmul(pnum2[:, slot * 512:(slot + 1) * 512],
                                              l, r, start=(i == 0), stop=False)
                          for st_ in range(G):
                              vh_sl = v2_h[:, st_ * D + et * 128: st_ * D + et * 128 + 128]
                              vl_sl = v2_l[:, st_ * D + et * 128: st_ * D + et * 128 + 128]
                              ah_sl = aT_h[:, st_ * GT + st_ * 128: st_ * GT + GT]
                              al_sl = aT_l[:, st_ * GT + st_ * 128: st_ * GT + GT]
                              ipairs = [(vh_sl, ah_sl), (vh_sl, al_sl), (vl_sl, ah_sl)]
                              for j, (l, r) in enumerate(ipairs):
                                  isl = (st_ == G - 1 and j == 2)
                                  last = p.matmul(
                                      pnum2[:, slot * 512 + st_ * 128:(slot + 1) * 512],
                                      l, r, start=False, stop=isl)
                          return last
                      dep = [hS, hA_ev[-1]]
                      if et >= 2:
                          dep += [h_ev_lo[et - 2], h_ev_hi[et - 2]]
                      hmn = S.step("pe", dep, mnum)

                      def ev_hi(a, et=et):
                          return a.copy(n_h[:, et * GT:(et + 1) * GT],
                                        pnum2[:, (et % 2) * 512:(et % 2 + 1) * 512])
                      h_ev_hi[et] = S.step("act", [hmn], ev_hi)

                      def ev_lo(v, et=et):
                          return v.scalar_tensor_tensor(
                              n_l[:, et * GT:(et + 1) * GT],
                              pnum2[:, (et % 2) * 512:(et % 2 + 1) * 512],
                              1.0, n_h[:, et * GT:(et + 1) * GT],
                              ALU.mult, ALU.subtract)
                      h_ev_lo[et] = S.step("dve", [hmn, h_ev_hi[et]], ev_lo)
                      if et == 0:
                          emit_den()
                  prev_mn_pe = hmn
                  if p2sub == 'num':
                      raise _StopEmit

                  for cq in range(G):
                      def mwo(p, cq=cq):
                          last = None
                          for eh in range(2):
                              pairs = []
                              for ek in range(ND):
                                  lh = n_h[:, ek * GT + cq * 128: ek * GT + cq * 128 + 128]
                                  ll = n_l[:, ek * GT + cq * 128: ek * GT + cq * 128 + 128]
                                  rh = wo_h[:, ek * D + eh * 512: ek * D + eh * 512 + 512]
                                  rl = wo_l[:, ek * D + eh * 512: ek * D + eh * 512 + 512]
                                  pairs += hl3(lh, ll, rh, rl)
                              last = mm_acc(p, pwo[:, eh * 512:(eh + 1) * 512], pairs)
                          return last
                      hmwo = S.step("pe", [h_ev_lo[ND - 1], h_ev_hi[ND - 1], prev_hey], mwo)
                      if p2sub == 'wo':
                          raise _StopEmit

                      def ldxr(e, dma, g=g, cq=cq):
                          t0 = (g * G + cq) * 128
                          dma(out=x_res[:], in_=x_d[t0:t0 + 128, :])
                      hxr = S.step("sp", [prev_hxr_reader], ldxr, n_dma=1)

                      def evy(v, cq=cq):
                          v.scalar_tensor_tensor(y_t[:], pwo[:], recip_den[:, cq:cq + 1],
                                                 bob_s[:], ALU.mult, ALU.add)
                          v.tensor_add(y_t[:], y_t[:], x_res[:])
                          v.bn_stats(bnb[:, 0:6], y_t[:, 0:512])
                          v.bn_stats(bnb[:, 6:12], y_t[:, 512:1024])
                          v.bn_aggr(bna[:, 0:2], bnb[:, 0:12])
                          return v.tensor_scalar(negm_c[:], bna[:, 0:1], -1.0, None, ALU.mult)
                      hey = S.step("dve", [hmwo, hxr, hevden, prev_hsd], evy)
                      prev_hey = hey
                      prev_hxr_reader = hey

                      hsd = S.step("act", [hey], lambda a: a.activation(
                          sd_c[:], bna[:, 1:2], ACT.Sqrt, bias=eps_c[:, 0:1]))
                      prev_hsd = hsd

                      def evy2(v):
                          v.reciprocal(rstd_c[:], sd_c[:])
                          return v.tensor_scalar(y1_t[:], y_t[:], negm_c[:, 0:1],
                                                 rstd_c[:, 0:1], ALU.add, ALU.mult)
                      hy1 = S.step("dve", [hsd, prev_sty1], evy2)

                      def sty1(e, dma, g=g, cq=cq):
                          t0 = (g * G + cq) * 128
                          dma(out=y1_dram[t0:t0 + 128, :], in_=y1_t[:])
                      prev_sty1 = S.step("gp", [hy1], sty1, n_dma=1)

                  h_prev_group_end = hy1
              except _StopEmit:
                pass

            h_p2_done = S.all_done()
            if p2sub:
                pass

        if upto == 'p2':
            _finish(S, block); return nc

        p3 = ExitStack()
        with p3:
            w1q_s = SB(p3, "w1q_s", [128, ND * H], BF16)
            w2q_s = SB(p3, "w2q_s", [128, NH * D], BF16)
            y1_t0 = SB(p3, "y1_t0", [128, D]); y1_t1 = SB(p3, "y1_t1", [128, D])
            m1c = SB(p3, "m1c", [128, 1]); r1c = SB(p3, "r1c", [128, 1])
            f1c = SB(p3, "f1c", [128, 1])
            m2c = SB(p3, "m2c", [128, 1]); r2c = SB(p3, "r2c", [128, 1])
            f2c = SB(p3, "f2c", [128, 1])
            mtmp = SB(p3, "mtmp", [128, 1])
            xi_ = SB(p3, "xi_", [128, D], I32)
            xb_ = SB(p3, "xb_", [128, D], BF16)
            yqD = SB(p3, "yqD", [128, ND * 128], BF16)
            h_f = SB(p3, "h_f", [128, H])
            hi_pc0 = SB(p3, "hi_pc0", [128, 512], I32)
            hi_pc1 = SB(p3, "hi_pc1", [128, 512], I32)
            hb_pc0 = SB(p3, "hb_pc0", [128, 512], BF16)
            hb_pc1 = SB(p3, "hb_pc1", [128, 512], BF16)
            hqD = SB(p3, "hqD", [128, NH * 128], BF16)
            y2_t = SB(p3, "y2_t", [128, D])
            out_t = SB(p3, "out_t", [128, D])
            bnb2 = SB(p3, "bnb2", [128, 12]); bna2 = SB(p3, "bna2", [128, 2])
            sd2 = SB(p3, "sd2", [128, 1]); rstd2 = SB(p3, "rstd2", [128, 1])
            negm2 = SB(p3, "negm2", [128, 1])
            ph_ = PS(p3, "ph_", [128, 2 * 512])
            po_ = PS(p3, "po_", [128, D])
            ptrA3 = PS(p3, "ptrA3", [128, 512], BF16)
            ptrB3 = PS(p3, "ptrB3", [128, 512], BF16)
            ptrC3 = PS(p3, "ptrC3", [128, 512], BF16)
            ptrD3 = PS(p3, "ptrD3", [128, 512], BF16)
            ptrbanks = [ptrA3, ptrB3, ptrC3, ptrD3]
            tr_state = {"idx": 0, "ev": [None, None, None, None]}

            def ldw12(e, dma):
                dma(out=w1q_s[:], in_=w1q_dram[:])
                dma(out=w2q_s[:], in_=w2q_dram[:])
            hw12 = S.step("sp", h_p2_done, ldw12, n_dma=2)

            prev_hs2 = None
            prev_stout = None
            prev_he2 = {0: None, 1: None}
            h_last_out = None
            hly = {}

            def ldy0(e, dma):
                dma(out=y1_t0[:], in_=y1_dram[0:128, :])
            hly[0] = S.step("sp", [hw12], ldy0, n_dma=1)

            def tr_step(src_t, src_off, dst_t, dst_off, dep_pe, dep_ev):
                b = tr_state["idx"] % 4
                tr_state["idx"] += 1
                bank = ptrbanks[b]

                def trx(p, bank=bank, src_t=src_t, src_off=src_off):
                    last = None
                    for j in range(4):
                        last = p.transpose(bank[:, j * 128:(j + 1) * 128],
                                           src_t[:, src_off + j * 128:
                                                 src_off + (j + 1) * 128],
                                           ident_b[:])
                    return last
                htp = S.step("pe", [tr_state["ev"][b]] + dep_pe, trx)

                def evx(a, bank=bank, dst_t=dst_t, dst_off=dst_off):
                    return a.copy(dst_t[:, dst_off:dst_off + 512], bank[:])
                hev = S.step("act", [htp] + dep_ev, evx)
                tr_state["ev"][b] = hev
                return htp, hev

            def emit_q1(ti, dep_extra):
                y1b = y1_t0 if ti % 2 == 0 else y1_t1

                def q1(v, y1b=y1b):
                    v.reduce_max(m1c[:], y1b[:], axis=AX.X, apply_absolute_value=True)
                    v.tensor_scalar(m1c[:], m1c[:], 1e-5, None, ALU.max)
                    v.reciprocal(r1c[:], m1c[:])
                    v.tensor_scalar(f1c[:], m1c[:], s1bc[:, 0:1], 1.0 / 127.0,
                                    ALU.mult, ALU.mult)
                    return v.tensor_scalar(xi_[:], y1b[:], r1c[:, 0:1], 127.0,
                                           ALU.mult, ALU.mult)
                return S.step("dve", [hly[ti]] + dep_extra, q1)

            def emit_hxb(hq1, dep_extra):
                return S.step("dve", [hq1] + dep_extra,
                              lambda v: v.tensor_copy(xb_[:], xi_[:]))

            def emit_trq(hxb, dep_mm1_last):
                htr_pe = None
                htr_ev = None
                for half in range(2):
                    htr_pe, htr_ev = tr_step(
                        xb_, half * 512, yqD, half * 512, [hxb],
                        [dep_mm1_last] if half == 0 else [])
                return htr_pe, htr_ev

            hq1_0 = emit_q1(0, [])
            hxb_0 = emit_hxb(hq1_0, [])
            cur_trq_pe, cur_trq_ev = emit_trq(hxb_0, None)

            prev_tile = {"hgl": None, "hqp": None, "mm1_last": None, "mm2": None,
                         "hxb": hxb_0, "evh_last": None}

            for ti in range(NT):
                y1b = y1_t0 if ti % 2 == 0 else y1_t1
                if ti + 1 < NT:
                    def ldyn(e, dma, ti=ti):
                        nb = y1_t1 if (ti + 1) % 2 == 1 else y1_t0
                        dma(out=nb[:], in_=y1_dram[(ti + 1) * 128:(ti + 2) * 128, :])
                    hly[ti + 1] = S.step("sp", [hly[ti], prev_he2[(ti + 1) % 2]],
                                         ldyn, n_dma=1)

                hml = {}
                hgl = {}
                hmx = None
                pgl = prev_tile["hgl"]
                pqp = prev_tile["hqp"]
                for hh in range(8):
                    def mm1(p, hh=hh):
                        pairs = [(yqD[:, dk * 128:(dk + 1) * 128],
                                  w1q_s[:, dk * H + hh * 512: dk * H + hh * 512 + 512])
                                 for dk in range(ND)]
                        return mm_acc(p, ph_[:, (hh % 2) * 512:(hh % 2 + 1) * 512], pairs)
                    dep = [cur_trq_ev, hw12]
                    if hh >= 2:
                        dep.append(hgl[hh - 2])
                    elif pgl is not None:
                        dep.append(pgl[6 + hh])
                    hm = S.step("pe", dep, mm1)
                    hml[hh] = hm

                    def gel(a, hh=hh):
                        return a.activation(h_f[:, hh * 512:(hh + 1) * 512],
                                            ph_[:, (hh % 2) * 512:(hh % 2 + 1) * 512],
                                            ACT.Gelu, scale=f1c[:, 0:1])
                    gdep = [hm] + ([pqp[hh]] if pqp is not None else [])
                    hg = S.step("act", gdep, gel)
                    hgl[hh] = hg

                    def mx(v, hh=hh):
                        if hh == 0:
                            return v.reduce_max(m2c[:], h_f[:, 0:512], axis=AX.X,
                                                apply_absolute_value=True)
                        v.reduce_max(mtmp[:], h_f[:, hh * 512:(hh + 1) * 512], axis=AX.X,
                                     apply_absolute_value=True)
                        return v.tensor_max(m2c[:], m2c[:], mtmp[:])
                    hmx = S.step("dve", [hg], mx)

                def q2s(v, ti=ti):
                    v.tensor_scalar(m2c[:], m2c[:], 1e-5, None, ALU.max)
                    v.reciprocal(r2c[:], m2c[:])
                    return v.tensor_scalar(f2c[:], m2c[:], s2bc[:, 0:1], 1.0 / 127.0,
                                           ALU.mult, ALU.mult)
                hq2a = S.step("dve", [hmx], q2s)

                hq1_n = None
                hpc_acts = {}
                htps = {}
                hqp_l = {}
                hev_last = None
                for pc in range(8):
                    hip = hi_pc0 if pc % 2 == 0 else hi_pc1
                    hbp = hb_pc0 if pc % 2 == 0 else hb_pc1

                    def qpc(v, pc=pc, hip=hip):
                        return v.tensor_scalar(hip[:], h_f[:, pc * 512:(pc + 1) * 512],
                                               r2c[:, 0:1], 127.0, ALU.mult, ALU.mult)
                    hqp = S.step("dve", [hq2a] + ([hpc_acts.get(pc - 2)] if pc >= 2 else []),
                                 qpc)
                    hqp_l[pc] = hqp

                    def cpc(v, pc=pc, hip=hip, hbp=hbp):
                        return v.tensor_copy(hbp[:], hip[:])
                    hpc_act = S.step("dve", [hqp, cur_trq_pe if pc <= 1 else None,
                                             htps.get(pc - 2)], cpc)
                    hpc_acts[pc] = hpc_act

                    htp, hev_last = tr_step(
                        hbp, 0, hqD, pc * 512,
                        [hpc_act] + ([prev_tile["mm2"]] if pc == 0 else []), [])
                    htps[pc] = htp

                    if pc == 1 and ti + 1 < NT:
                        hq1_n = emit_q1(ti + 1, [hgl[7], prev_tile["hxb"]])
                hq2 = hev_last

                nxt_hxb = prev_tile["hxb"]
                if ti + 1 < NT:
                    nxt_hxb = emit_hxb(hq1_n, [cur_trq_pe])
                    cur_trq_pe, cur_trq_ev = emit_trq(nxt_hxb, hml[7])

                def mm2(p):
                    last = None
                    for eh in range(2):
                        pairs = [(hqD[:, hk * 128:(hk + 1) * 128],
                                  w2q_s[:, hk * D + eh * 512: hk * D + eh * 512 + 512])
                                 for hk in range(NH)]
                        last = mm_acc(p, po_[:, eh * 512:(eh + 1) * 512], pairs)
                    return last
                hm2 = S.step("pe", [hq2, hw12, prev_hs2], mm2)

                def evy2b(v, ti=ti, y1b=y1b):
                    v.scalar_tensor_tensor(y2_t[:], po_[:], f2c[:, 0:1],
                                           y1b[:], ALU.mult, ALU.add)
                    v.bn_stats(bnb2[:, 0:6], y2_t[:, 0:512])
                    v.bn_stats(bnb2[:, 6:12], y2_t[:, 512:1024])
                    v.bn_aggr(bna2[:, 0:2], bnb2[:, 0:12])
                    return v.tensor_scalar(negm2[:], bna2[:, 0:1], -1.0, None, ALU.mult)
                he2 = S.step("dve", [hm2, prev_hs2], evy2b)
                prev_he2[ti % 2] = he2

                hs2 = S.step("act", [he2], lambda a: a.activation(
                    sd2[:], bna2[:, 1:2], ACT.Sqrt, bias=eps_c[:, 0:1]))
                prev_hs2 = hs2

                def fin(v):
                    v.reciprocal(rstd2[:], sd2[:])
                    return v.tensor_scalar(out_t[:], y2_t[:], negm2[:, 0:1],
                                           rstd2[:, 0:1], ALU.add, ALU.mult)
                hf = S.step("dve", [hs2, prev_stout], fin)

                def stout(e, dma, ti=ti):
                    dma(out=out_d[ti * 128:(ti + 1) * 128, :], in_=out_t[:])
                h_last_out = S.step("gp", [hf], stout, n_dma=1)
                prev_stout = h_last_out

                prev_tile = {"hgl": hgl, "hqp": hqp_l, "mm1_last": hml[7],
                             "mm2": hm2, "hxb": nxt_hxb, "evh_last": hq2}

            tgt = S.cnt["gp"]

            def gp_final(e):
                e.wait_ge(sem["gp"], tgt)
            block.gpsimd(gp_final)

    return nc


_CACHE = {}


def _get_nc(dbg=False):
    key = bool(dbg)
    if key not in _CACHE:
        _CACHE[key] = build_program(dbg=dbg)
    return _CACHE[key]


def _prep_inputs(inputs):
    x = np.asarray(inputs["L0_attn_x"], np.float32)
    Wq = np.asarray(inputs["L0_attn_Wq"], np.float32)
    bq = np.asarray(inputs["L0_attn_bq"], np.float32)
    Wk = np.asarray(inputs["L0_attn_Wk"], np.float32)
    bk = np.asarray(inputs["L0_attn_bk"], np.float32)
    Wv = np.asarray(inputs["L0_attn_Wv"], np.float32)
    bv = np.asarray(inputs["L0_attn_bv"], np.float32)
    Wo = np.asarray(inputs["L0_attn_Wo"], np.float32)
    bo = np.asarray(inputs["L0_attn_bo"], np.float32)
    w1 = np.asarray(inputs["L0_ffn_w1"], np.float32)
    w2 = np.asarray(inputs["L0_ffn_w2"], np.float32)

    shared = {
        "WqT": np.ascontiguousarray(Wq.T), "WkT": np.ascontiguousarray(Wk.T),
        "WvT": np.ascontiguousarray(Wv.T), "WoT": np.ascontiguousarray(Wo.T),
        "bq_t": np.ascontiguousarray(bq.reshape(ND, 128).T),
        "bk_t": np.ascontiguousarray(bk.reshape(ND, 128).T),
        "bv_b": np.ascontiguousarray(np.broadcast_to(bv, (128, D))),
        "bk_b": np.ascontiguousarray(np.broadcast_to(bk, (128, D))),
        "bo_b": np.ascontiguousarray(np.broadcast_to(bo, (128, D))),
        "w1T": np.ascontiguousarray(w1.T), "w2T": np.ascontiguousarray(w2.T),
        "triu": np.triu(np.ones((128, 128), np.float32)),
        "ident": np.eye(128, dtype=np.float32),
    }
    in_maps = []
    for core in range(8):
        b, q = divmod(core, 4)
        seg = x[b, q * TSEG:(q + 1) * TSEG, :]
        m = dict(shared)
        m["x_seg"] = np.ascontiguousarray(seg)
        m["xT_seg"] = np.ascontiguousarray(seg.T)
        pm = (np.arange(4) < q).astype(np.float32)
        m["pmask"] = np.ascontiguousarray(np.broadcast_to(pm, (128, 4)))
        in_maps.append(m)
    return in_maps


def kernel(**inputs):
    in_maps = _prep_inputs(inputs)
    nc = _get_nc(dbg=False)
    res = run_bass_kernel_spmd(nc, in_maps, core_ids=list(range(8)))
    out = np.empty((B, T, D), np.float32)
    for core in range(8):
        b, q = divmod(core, 4)
        out[b, q * TSEG:(q + 1) * TSEG, :] = np.asarray(res.results[core]["out"])
    return out
